# version 11
# baseline (speedup 1.0000x reference)
import math
from contextlib import ExitStack

import numpy as np
import ml_dtypes
import concourse.bass as bass
import concourse.mybir as mybir
from concourse.bass_utils import run_bass_kernel_spmd
from concourse.bass_core_rust import IndirectOffsetOnAxis

F32 = mybir.dt.float32
BF16 = mybir.dt.bfloat16
U32 = mybir.dt.uint32
AF = mybir.ActivationFunctionType
ALU = mybir.AluOpType
AX = mybir.AxisListType

NCORES = 8
B, S, D = 16, 4096, 1024
TPC = B * S // NCORES
NT = TPC // 128
WIN, NW = 256, TPC // 256
NH, HD = 16, 64
NE, TOPK, CAP, EI = 64, 8, 256, 768
EPC = NE // NCORES
I1 = 2816
LN_EPS = 1e-5
BIGIDX = float(1 << 24)


def _host_consts():
    c = {}
    c["ident_f32"] = np.eye(128, dtype=np.float32)
    c["ident_bf"] = np.eye(128, dtype=np.float32)
    c["tstrictT"] = np.triu(np.ones((128, 128), np.float32), 1)
    c["ones_row"] = np.ones((1, 128), np.float32)
    c["ones_col"] = np.ones((128, 1), np.float32)
    c["iotaM"] = np.tile((np.arange(NE, dtype=np.float32) - BIGIDX)[None, :],
                         (128, TOPK)).reshape(128, TOPK * NE)
    c["iotaE"] = np.tile(np.arange(NE, dtype=np.float32), (128, TOPK))
    kk = (np.arange(TOPK)[None, :] < np.arange(TOPK)[:, None]).astype(np.float32)
    c["kkmask"] = np.tile(kk.reshape(1, TOPK * TOPK), (128, 1))
    return c

_CONST_DTYPES = {"ident_f32": F32, "ident_bf": BF16, "tstrictT": F32,
                 "ones_row": F32, "ones_col": F32, "iotaM": F32,
                 "iotaE": F32, "kkmask": F32}


class Buf:
    __slots__ = ("name", "w", "r")

    def __init__(self, name):
        self.name = name
        self.w = None
        self.r = []


class P:

    def __init__(self, name):
        self.nc = bass.Bass("TRN2", target_bir_lowering=False, num_devices=NCORES)
        self.ctx = ExitStack()
        self.name = name
        self.n = {}
        self.sem = {}
        self.bufs = {}
        self.seen = {}

    def semaphore(self, name):
        self.sem[name] = self.ctx.enter_context(self.nc.semaphore(name))
        self.n[name] = 0
        return self.sem[name]

    def sbuf(self, name, shape, dt=F32):
        t = self.ctx.enter_context(self.nc.sbuf_tensor(name, shape, dt))
        self.bufs[name] = Buf(name)
        return t

    def psum(self, name, shape, dt=F32):
        t = self.ctx.enter_context(self.nc.psum_tensor(name, shape, dt))
        self.bufs[name] = Buf(name)
        return t

    def buf(self, name):
        if name not in self.bufs:
            self.bufs[name] = Buf(name)
        return self.bufs[name]

    def _wait(self, eng, ekey, sem_name, val):
        if val <= 0:
            return
        seen = self.seen.setdefault(ekey, {})
        if seen.get(sem_name, -1) >= val:
            return
        eng.wait_ge(self.sem[sem_name], val)
        seen[sem_name] = val

    DMA_SEMS = frozenset(["sd", "gd", "st", "x0", "x1", "ld0", "ld1", "ld2",
                          "id", "zs"])

    def _dep_waits(self, eng, ekey, reads, writes, is_pe=False):
        todo = {}

        def add(s, v):
            if s in self.DMA_SEMS:
                v = self.n[s]
            todo[s] = max(todo.get(s, 0), v)

        for bn in reads:
            b = self.buf(bn)
            if b.w is not None:
                s, v = b.w
                if is_pe and s == "pt":
                    continue
                add(s, v)
        for bn in writes:
            b = self.buf(bn)
            if b.w is not None:
                s, v = b.w
                if not (is_pe and s == "pt"):
                    add(s, v)
            for s, v in b.r:
                if is_pe and s == "pt":
                    continue
                add(s, v)
        for s, v in sorted(todo.items()):
            self._wait(eng, ekey, s, v)

    def _record(self, reads, writes, stamp):
        for bn in writes:
            b = self.buf(bn)
            b.w = stamp
            b.r = []
        for bn in reads:
            self.buf(bn).r.append(stamp)

    def op(self, eng, tok, emit, reads=(), writes=()):
        self._dep_waits(eng, tok, reads, writes)
        ins = emit()
        ins.then_inc(self.sem[tok], 1)
        self.n[tok] += 1
        self._record(reads, writes, (tok, self.n[tok]))
        return self.n[tok]

    def mm(self, eng, emit_list, reads=(), writes=()):
        self._dep_waits(eng, "pe", reads, writes, is_pe=True)
        ins = None
        for emit in emit_list:
            ins = emit()
        ins.then_inc(self.sem["pt"], 1)
        self.n["pt"] += 1
        self._record(reads, writes, ("pt", self.n["pt"]))
        return self.n["pt"]

    def dma(self, eng, ekey, sem_name, out, in_, reads=(), writes=(), **kw):
        self._dep_waits(eng, ekey, reads, writes)
        ins = eng.dma_start(out=out, in_=in_, **kw)
        ins.then_inc(self.sem[sem_name], 16)
        self.n[sem_name] += 16
        self._record(reads, writes, (sem_name, self.n[sem_name]))
        return self.n[sem_name]

    def idma(self, eng, ekey, sem_name, reads=(), writes=(), **kw):
        self._dep_waits(eng, ekey, reads, writes)
        ins = eng.indirect_dma_start(**kw)
        ins.then_inc(self.sem[sem_name], 16)
        self.n[sem_name] += 16
        self._record(reads, writes, (sem_name, self.n[sem_name]))
        return self.n[sem_name]


def _declare_consts(p, consts_needed):
    hbs, tiles = {}, {}
    for nm in consts_needed:
        shape = list(_host_consts()[nm].shape)
        hbs[nm] = p.nc.dram_tensor("c_" + nm, shape, F32, kind="ExternalInput").ap()
        tiles[nm] = p.sbuf("s_" + nm, shape, _CONST_DTYPES[nm])
    return hbs, tiles


def _load_consts(p, blk, hbs, tiles):
    @blk.sync
    def _(sync):
        for nm, hb in hbs.items():
            if _CONST_DTYPES[nm] == F32:
                p.dma(sync, "sync", "sd", tiles[nm][:], hb, writes=("s_" + nm,))

    @blk.gpsimd
    def _(g):
        for nm, hb in hbs.items():
            if _CONST_DTYPES[nm] != F32:
                p.dma(g, "gp", "gd", tiles[nm][:], hb, writes=("s_" + nm,))


def const_inputs(consts_needed):
    hc = _host_consts()
    return {"c_" + nm: hc[nm] for nm in consts_needed}



L1_CONSTS = ["ident_f32", "ident_bf", "tstrictT", "ones_row", "ones_col",
             "iotaM", "iotaE", "kkmask"]


def build_launch1(nwin=NW, lvl=99, hlvl=99):
    p = P("l1")
    nc = p.nc
    x_h = nc.dram_tensor("x", [TPC, D], F32, kind="ExternalInput").ap()
    qw_h = nc.dram_tensor("q_w", [D, D], F32, kind="ExternalInput").ap()
    kvw_h = nc.dram_tensor("kv_w", [2 * HD, D], F32, kind="ExternalInput").ap()
    pw_h = nc.dram_tensor("proj_w", [D, D], F32, kind="ExternalInput").ap()
    pb_h = nc.dram_tensor("proj_b", [1, D], F32, kind="ExternalInput").ap()
    gw_h = nc.dram_tensor("gate_w", [NE, D], F32, kind="ExternalInput").ap()

    x1_h = nc.dram_tensor("x1", [TPC, D], F32, kind="ExternalOutput").ap()
    xln1_h = nc.dram_tensor("xln1b", [TPC, D], BF16, kind="ExternalOutput").ap()
    sdat_h = nc.dram_tensor("slotdat", [TPC, 3 * TOPK], F32, kind="ExternalOutput").ap()
    cnt_h = nc.dram_tensor("cnt", [1, NE], F32, kind="ExternalOutput").ap()

    hbs, C = _declare_consts(p, L1_CONSTS)

    with p.ctx:
        for s in ["sd", "gd", "x0", "x1", "pt", "vt", "at", "st"]:
            p.semaphore(s)
        qw_bf = p.sbuf("qw_bf", [128, 8 * D], BF16)
        qwT = p.sbuf("qwT", [128, 8 * D], BF16)
        pwT = p.sbuf("pwT", [128, 8 * D], BF16)
        kv_bf = p.sbuf("kv_bf", [128, D], BF16)
        kvwT = p.sbuf("kvwT", [128, 8 * 128], BF16)
        gw_f = p.sbuf("gw_f", [NE, D], F32)
        gwT = p.sbuf("gwT", [128, 8 * NE], F32)
        pb_bf = p.sbuf("pb_bf", [1, D], BF16)
        ones_bf = p.sbuf("ones_bf", [1, 128], BF16)
        xw = [p.sbuf(f"xw{i}", [128, 2 * D]) for i in range(2)]
        xc = p.sbuf("xc", [128, 2 * D])
        sqs = p.sbuf("sqs", [128, D])
        nmean = p.sbuf("nmean", [128, 2])
        rstd = p.sbuf("rstd", [128, 2])
        xlnb = p.sbuf("xlnb", [128, 2 * D], BF16)
        xlnT = p.sbuf("xlnT", [128, 8 * WIN], BF16)
        qT = p.sbuf("qTs", [64, NH * WIN], BF16)
        kvsb = p.sbuf("kvsb", [64, 2 * WIN], BF16)
        vtok = p.sbuf("vtok", [128, 2 * HD], BF16)
        Psb = p.sbuf("Psb", [128, 4 * WIN], BF16)
        PTsb = p.sbuf("PTsb", [128, 8 * 128], BF16)
        nmax = p.sbuf("nmax", [128, 4])
        rcp = p.sbuf("rcp", [128, NH])
        attn = p.sbuf("attn", [128, D], BF16)
        attnT = p.sbuf("attnT", [128, 8 * 128], BF16)
        x1sb = p.sbuf("x1sb", [128, 2 * D])
        xc1 = p.sbuf("xc1", [128, D])
        rstd1 = p.sbuf("rstd1", [128, 1])
        xln1sb = p.sbuf("xln1sb", [128, D], BF16)
        xc1T = p.sbuf("xc1T", [128, 8 * 128])
        lgt = p.sbuf("lgt", [128, NE])
        top8 = p.sbuf("top8", [128, TOPK])
        nmax0 = p.sbuf("nmax0", [128, 1])
        sd_t = p.sbuf("sd_t", [128, 3 * TOPK])
        eq3 = p.sbuf("eq3", [128, TOPK * NE])
        oh3 = p.sbuf("oh3", [128, TOPK * NE])
        cntT = p.sbuf("cntT", [128, NE])
        scr8 = p.sbuf("scr8", [128, TOPK])
        scr64 = p.sbuf("scr64", [128, TOPK * TOPK])
        base_sb = p.sbuf("base_sb", [128, NE])
        carry = p.sbuf("carry", [1, NE])
        c_eps = p.sbuf("c_eps", [128, 1])
        c_invD = p.sbuf("c_invD", [128, 1])
        c_mhalf = p.sbuf("c_mhalf", [128, 1])
        c_qs = p.sbuf("c_qs", [128, 1])

        ptA = p.psum("ptA", [128, 1024], BF16)
        ptB = p.psum("ptB", [128, 1024], BF16)
        psA = p.psum("psA", [128, 512])
        psB = p.psum("psB", [128, 512])
        psC = p.psum("psC", [128, 512])
        psD = p.psum("psD", [128, 512])
        psS = p.psum("psS", [128, 1024])

        blk = p.ctx.enter_context(nc.Block())
        _load_consts(p, blk, hbs, C)

        @blk.gpsimd
        def _(g):
            p.dma(g, "gp", "gd", qw_bf[:].rearrange("p (c d) -> p c d", c=8),
                  qw_h.rearrange("(c p) d -> p c d", p=128), writes=("qw_bf",))
            p.dma(g, "gp", "gd", kv_bf[:], kvw_h, writes=("kv_bf",))
            p.dma(g, "gp", "gd", pb_bf[:], pb_h, writes=("pb_bf",))
            p.dma(g, "gp", "gd", ones_bf[:], hbs["ones_row"], writes=("ones_bf",))

        @blk.sync
        def _(sync):
            p.dma(sync, "sync", "sd", gw_f[:], gw_h, writes=("gw_f",))

        for cdi in range(8):
            @blk.tensor
            def _(t):
                p.mm(t, [
                    (lambda cdo=cdo, cdi=cdi: t.transpose(
                        ptA[:, cdo * 128:(cdo + 1) * 128],
                        qw_bf[:, cdo * D + cdi * 128: cdo * D + (cdi + 1) * 128],
                        C["ident_bf"][:])) for cdo in range(8)],
                    reads=("qw_bf", "s_ident_bf"), writes=("ptA",))

            @blk.vector
            def _(v):
                p.op(v, "vt", lambda cdi=cdi: v.tensor_copy(
                    out=qwT[:, cdi * D:(cdi + 1) * D], in_=ptA[:]),
                    reads=("ptA",), writes=("qwT",))

        @blk.tensor
        def _(t):
            p.mm(t, [
                (lambda c=c: t.transpose(
                    ptB[:, c * 128:(c + 1) * 128],
                    kv_bf[:, c * 128:(c + 1) * 128],
                    C["ident_bf"][:])) for c in range(8)],
                reads=("kv_bf", "s_ident_bf"), writes=("ptB",))

        @blk.vector
        def _(v):
            p.op(v, "vt", lambda: v.tensor_copy(out=kvwT[:], in_=ptB[:]),
                 reads=("ptB",), writes=("kvwT",))
            p.op(v, "vt", lambda: v.memset(carry[:], 0.0), writes=("carry",))
            p.op(v, "vt", lambda: v.memset(c_eps[:], LN_EPS), writes=("c_eps",))
            p.op(v, "vt", lambda: v.memset(c_invD[:], 1.0 / D), writes=("c_invD",))
            p.op(v, "vt", lambda: v.memset(c_mhalf[:], -0.5), writes=("c_mhalf",))
            p.op(v, "vt", lambda: v.memset(c_qs[:], float(HD) ** -0.5), writes=("c_qs",))

        @blk.gpsimd
        def _(g):
            p.dma(g, "gp", "gd", qw_bf[:].rearrange("p (c d) -> p c d", c=8),
                  pw_h.rearrange("(c p) d -> p c d", p=128), writes=("qw_bf",))

        for cdi in range(8):
            @blk.tensor
            def _(t):
                p.mm(t, [
                    (lambda cdo=cdo, cdi=cdi: t.transpose(
                        ptA[:, cdo * 128:(cdo + 1) * 128],
                        qw_bf[:, cdo * D + cdi * 128: cdo * D + (cdi + 1) * 128],
                        C["ident_bf"][:])) for cdo in range(8)],
                    reads=("qw_bf", "s_ident_bf"), writes=("ptA",))

            @blk.vector
            def _(v):
                p.op(v, "vt", lambda cdi=cdi: v.tensor_copy(
                    out=pwT[:, cdi * D:(cdi + 1) * D], in_=ptA[:]),
                    reads=("ptA",), writes=("pwT",))

        @blk.tensor
        def _(t):
            p.mm(t, [
                (lambda c=c: t.transpose(
                    psD[:, c * 64:(c + 1) * 64],
                    gw_f[0:64, c * 128:(c + 1) * 128],
                    C["ident_f32"][0:64, 0:64])) for c in range(8)],
                reads=("gw_f", "s_ident_f32"), writes=("psD",))

        @blk.vector
        def _(v):
            p.op(v, "vt", lambda: v.tensor_copy(out=gwT[:], in_=psD[:]),
                 reads=("psD",), writes=("gwT",))

        for w in range(nwin):
            par = w % 2
            xwb = xw[par]
            xwn = f"xw{par}"
            xsem = "x0" if par == 0 else "x1"

            @blk.sync
            def _(sync):
                p.dma(sync, "sync", xsem,
                      xwb[:].rearrange("p (t d) -> p t d", t=2),
                      x_h.rearrange("(w t p) d -> w p t d", t=2, p=128)[w],
                      writes=(xwn,))

            if lvl < 1:
                continue
            @blk.vector
            def _(v):
                for i in range(2):
                    sl = slice(i * D, (i + 1) * D)
                    p.op(v, "vt", lambda sl=sl, i=i: v.reduce_sum(
                        out=nmean[:, i:i + 1], in_=xwb[:, sl], axis=AX.X,
                        negate=True), reads=(xwn,), writes=("nmean",))
                    p.op(v, "vt", lambda i=i: v.tensor_scalar_mul(
                        out=nmean[:, i:i + 1], in0=nmean[:, i:i + 1],
                        scalar1=1.0 / D), reads=("nmean",), writes=("nmean",))
                    p.op(v, "vt", lambda sl=sl, i=i: v.tensor_scalar_add(
                        out=xc[:, sl], in0=xwb[:, sl], scalar1=nmean[:, i:i + 1]),
                        reads=(xwn, "nmean"), writes=("xc",))

            @blk.scalar
            def _(s):
                for i in range(2):
                    sl = slice(i * D, (i + 1) * D)
                    p.op(s, "at", lambda sl=sl, i=i: s.activation(
                        out=sqs[:], in_=xc[:, sl], func=AF.Square,
                        accum_out=rstd[:, i:i + 1]),
                        reads=("xc",), writes=("sqs", "rstd"))
                    p.op(s, "at", lambda i=i: s.activation(
                        out=rstd[:, i:i + 1], in_=rstd[:, i:i + 1], func=AF.Ln,
                        scale=c_invD[:], bias=c_eps[:]),
                        reads=("rstd", "c_invD", "c_eps"), writes=("rstd",))
                    p.op(s, "at", lambda i=i: s.activation(
                        out=rstd[:, i:i + 1], in_=rstd[:, i:i + 1], func=AF.Exp,
                        scale=c_mhalf[:]), reads=("rstd", "c_mhalf"), writes=("rstd",))

            @blk.vector
            def _(v):
                for i in range(2):
                    sl = slice(i * D, (i + 1) * D)
                    p.op(v, "vt", lambda sl=sl, i=i: v.tensor_scalar_mul(
                        out=xlnb[:, sl], in0=xc[:, sl], scalar1=rstd[:, i:i + 1]),
                        reads=("xc", "rstd"), writes=("xlnb",))

            if lvl < 2:
                continue
            for i in range(2):
                ptn = "ptA" if i == 0 else "ptB"
                pt = ptA if i == 0 else ptB

                @blk.tensor
                def _(t):
                    p.mm(t, [
                        (lambda pt=pt, i=i, c=c: t.transpose(
                            pt[:, c * 128:(c + 1) * 128],
                            xlnb[:, i * D + c * 128: i * D + (c + 1) * 128],
                            C["ident_bf"][:])) for c in range(8)],
                        reads=("xlnb", "s_ident_bf"), writes=(ptn,))

                @blk.vector
                def _(v):
                    p.op(v, "vt", lambda pt=pt, i=i: v.tensor_copy(
                        out=xlnT[:].rearrange("p (c x) -> p c x", c=8)[
                            :, :, i * 128:(i + 1) * 128],
                        in_=pt[:].rearrange("p (c x) -> p c x", c=8)),
                        reads=(ptn,), writes=("xlnT",))

            if lvl < 3:
                continue
            for half in range(8):
                bank = psA if half % 2 == 0 else psB
                bn = "psA" if half % 2 == 0 else "psB"

                @blk.tensor
                def _(t):
                    for j in range(2):
                        h = half * 2 + j
                        p.mm(t, [
                            (lambda bank=bank, j=j, h=h, cdi=cdi: t.matmul(
                                bank[0:64, j * 256:(j + 1) * 256],
                                qwT[:, cdi * D + h * 64: cdi * D + h * 64 + 64],
                                xlnT[:, cdi * WIN:(cdi + 1) * WIN],
                                start=(cdi == 0), stop=(cdi == 7)))
                            for cdi in range(8)],
                            reads=("qwT", "xlnT"), writes=(bn,))

                @blk.scalar
                def _(s):
                    p.op(s, "at", lambda bank=bank, half=half: s.activation(
                        out=qT[0:64, half * 512:(half + 1) * 512],
                        in_=bank[0:64, :],
                        func=AF.Copy, scale=c_qs[0:64, :]),
                        reads=(bn, "c_qs"), writes=("qTs",))

            if lvl < 4:
                continue
            @blk.tensor
            def _(t):
                for kvh in range(2):
                    p.mm(t, [
                        (lambda kvh=kvh, cdi=cdi: t.matmul(
                            psC[0:64, kvh * 256:(kvh + 1) * 256],
                            kvwT[:, cdi * 128 + kvh * 64:
                                 cdi * 128 + kvh * 64 + 64],
                            xlnT[:, cdi * WIN:(cdi + 1) * WIN],
                            start=(cdi == 0), stop=(cdi == 7)))
                        for cdi in range(8)],
                        reads=("kvwT", "xlnT"), writes=("psC",))

            @blk.vector
            def _(v):
                p.op(v, "vt", lambda: v.tensor_copy(
                    out=kvsb[:], in_=psC[0:64, :]), reads=("psC",), writes=("kvsb",))

            if lvl < 5:
                continue
            @blk.tensor
            def _(t):
                p.mm(t, [
                    (lambda ktc=ktc: t.transpose(
                        ptA[:, ktc * 64:(ktc + 1) * 64],
                        kvsb[0:64, 256 + ktc * 128: 256 + (ktc + 1) * 128],
                        C["ident_bf"][0:64, 0:64])) for ktc in range(2)],
                    reads=("kvsb", "s_ident_bf"), writes=("ptA",))

            @blk.vector
            def _(v):
                p.op(v, "vt", lambda: v.tensor_copy(
                    out=vtok[:], in_=ptA[:, 0:128]), reads=("ptA",), writes=("vtok",))

            if lvl < 6:
                continue
            for qc in range(2):
                for hg in range(4 if lvl >= 7 else 0):
                    @blk.tensor
                    def _(t):
                        for j in range(4):
                            h = hg * 4 + j
                            p.mm(t, [lambda h=h, j=j, qc=qc: t.matmul(
                                psS[:, j * 256:(j + 1) * 256],
                                qT[0:64, h * WIN + qc * 128: h * WIN + qc * 128 + 128],
                                kvsb[0:64, 0:256],
                                start=True, stop=True)],
                                reads=("qTs", "kvsb"), writes=("psS",))

                    if hlvl < 2:
                        continue
                    @blk.vector
                    def _(v):
                        p.op(v, "vt", lambda: v.reduce_max(
                            out=nmax[:], in_=psS[:].rearrange(
                                "p (g k) -> p g k", g=4),
                            axis=AX.X, negate=True),
                            reads=("psS",), writes=("nmax",))
                        if hlvl < 3:
                            return
                        p.op(v, "vt", lambda: v.tensor_tensor(
                            out=psS[:], in0=psS[:],
                            in1=nmax[:].unsqueeze(2).to_broadcast(
                                [128, 4, 256]), op=ALU.add),
                            reads=("psS", "nmax"), writes=("psS",))

                    if hlvl < 4:
                        continue
                    @blk.scalar
                    def _(s):
                        p.op(s, "at", lambda: s.activation(
                            out=Psb[:], in_=psS[:], func=AF.Exp),
                            reads=("psS",), writes=("Psb",))

                    if hlvl < 5:
                        continue
                    @blk.vector
                    def _(v):
                        p.op(v, "vt", lambda hg=hg: v.reduce_sum(
                            out=rcp[:, hg * 4:(hg + 1) * 4],
                            in_=Psb[:].rearrange("p (g k) -> p g k", g=4),
                            axis=AX.X), reads=("Psb",), writes=("rcp",))
                        p.op(v, "vt", lambda hg=hg: v.reciprocal(
                            out=rcp[:, hg * 4:(hg + 1) * 4],
                            in_=rcp[:, hg * 4:(hg + 1) * 4]),
                            reads=("rcp",), writes=("rcp",))

                    if hlvl < 6:
                        continue
                    @blk.tensor
                    def _(t):
                        p.mm(t, [
                            (lambda j=j, ktc=ktc: t.transpose(
                                ptB[:, (j * 2 + ktc) * 128:(j * 2 + ktc + 1) * 128],
                                Psb[:, j * 256 + ktc * 128:
                                    j * 256 + ktc * 128 + 128],
                                C["ident_bf"][:]))
                            for j in range(4) for ktc in range(2)],
                            reads=("Psb", "s_ident_bf"), writes=("ptB",))

                    if hlvl < 7:
                        continue
                    @blk.vector
                    def _(v):
                        p.op(v, "vt", lambda: v.tensor_copy(
                            out=PTsb[:], in_=ptB[:]), reads=("ptB",), writes=("PTsb",))

                    if hlvl < 8:
                        continue
                    @blk.tensor
                    def _(t):
                        for j in range(4):
                            h = hg * 4 + j
                            bank = psA if h < 8 else psB
                            bn = "psA" if h < 8 else "psB"
                            col = (h % 8) * 64
                            p.mm(t, [
                                (lambda bank=bank, col=col, j=j, ktc=ktc: t.matmul(
                                    bank[:, col:col + 64],
                                    PTsb[:, (j * 2 + ktc) * 128:
                                         (j * 2 + ktc + 1) * 128],
                                    vtok[:, ktc * 64:(ktc + 1) * 64],
                                    start=(ktc == 0), stop=(ktc == 1)))
                                for ktc in range(2)],
                                reads=("PTsb", "vtok"), writes=(bn,))

                if lvl < 8:
                    continue
                @blk.vector
                def _(v):
                    for bi, (bank, bn) in enumerate([(psA, "psA"), (psB, "psB")]):
                        p.op(v, "vt", lambda bank=bank, bi=bi: v.tensor_tensor(
                            out=attn[:, bi * 512:(bi + 1) * 512], in0=bank[:],
                            in1=rcp[:, bi * 8:(bi + 1) * 8].unsqueeze(2).to_broadcast([128, 8, 64]),
                            op=ALU.mult), reads=(bn, "rcp"), writes=("attn",))

                if lvl < 9:
                    continue
                @blk.tensor
                def _(t):
                    p.mm(t, [
                        (lambda c=c: t.transpose(
                            ptA[:, c * 128:(c + 1) * 128],
                            attn[:, c * 128:(c + 1) * 128],
                            C["ident_bf"][:])) for c in range(8)],
                        reads=("attn", "s_ident_bf"), writes=("ptA",))

                @blk.vector
                def _(v):
                    p.op(v, "vt", lambda: v.tensor_copy(
                        out=attnT[:], in_=ptA[:]), reads=("ptA",), writes=("attnT",))

                @blk.tensor
                def _(t):
                    for nb in range(2):
                        bank = psA if nb == 0 else psB
                        bn = "psA" if nb == 0 else "psB"
                        emits = [(lambda bank=bank, nb=nb: t.matmul(
                            bank[:], ones_bf[:],
                            pb_bf[:, nb * 512:(nb + 1) * 512],
                            start=True, stop=False))]
                        emits += [
                            (lambda bank=bank, nb=nb, cdi=cdi: t.matmul(
                                bank[:],
                                attnT[:, cdi * 128:(cdi + 1) * 128],
                                pwT[:, cdi * D + nb * 512: cdi * D + (nb + 1) * 512],
                                start=False, stop=(cdi == 7)))
                            for cdi in range(8)]
                        p.mm(t, emits, reads=("ones_bf", "pb_bf", "attnT", "pwT"),
                             writes=(bn,))

                @blk.vector
                def _(v):
                    for nb, (bank, bn) in enumerate([(psA, "psA"), (psB, "psB")]):
                        p.op(v, "vt", lambda bank=bank, nb=nb: v.tensor_add(
                            out=x1sb[:, qc * D + nb * 512: qc * D + (nb + 1) * 512],
                            in0=xwb[:, qc * D + nb * 512: qc * D + (nb + 1) * 512],
                            in1=bank[:]), reads=(xwn, bn), writes=("x1sb",))

                if lvl < 10:
                    continue
                tile = 2 * w + qc
                sl = slice(qc * D, (qc + 1) * D)

                @blk.vector
                def _(v):
                    p.op(v, "vt", lambda: v.reduce_sum(
                        out=nmean[:, 0:1], in_=x1sb[:, sl], axis=AX.X,
                        negate=True), reads=("x1sb",), writes=("nmean",))
                    p.op(v, "vt", lambda: v.tensor_scalar_mul(
                        out=nmean[:, 0:1], in0=nmean[:, 0:1], scalar1=1.0 / D),
                        reads=("nmean",), writes=("nmean",))
                    p.op(v, "vt", lambda: v.tensor_scalar_add(
                        out=xc1[:], in0=x1sb[:, sl], scalar1=nmean[:, 0:1]),
                        reads=("x1sb", "nmean"), writes=("xc1",))

                @blk.scalar
                def _(s):
                    p.op(s, "at", lambda: s.activation(
                        out=sqs[:], in_=xc1[:], func=AF.Square,
                        accum_out=rstd1[:]), reads=("xc1",), writes=("sqs", "rstd1"))
                    p.op(s, "at", lambda: s.activation(
                        out=rstd1[:], in_=rstd1[:], func=AF.Ln,
                        scale=c_invD[:], bias=c_eps[:]),
                        reads=("rstd1", "c_invD", "c_eps"), writes=("rstd1",))
                    p.op(s, "at", lambda: s.activation(
                        out=rstd1[:], in_=rstd1[:], func=AF.Exp, scale=c_mhalf[:]),
                        reads=("rstd1", "c_mhalf"), writes=("rstd1",))

                @blk.vector
                def _(v):
                    p.op(v, "vt", lambda: v.tensor_scalar_mul(
                        out=xln1sb[:], in0=xc1[:], scalar1=rstd1[:]),
                        reads=("xc1", "rstd1"), writes=("xln1sb",))

                @blk.sync
                def _(sync):
                    p.dma(sync, "sync", "st",
                          x1_h[tile * 128:(tile + 1) * 128, :], x1sb[:, sl],
                          reads=("x1sb",))
                    p.dma(sync, "sync", "st",
                          xln1_h[tile * 128:(tile + 1) * 128, :], xln1sb[:],
                          reads=("xln1sb",))

                if lvl < 11:
                    continue
                @blk.tensor
                def _(t):
                    for bi, (bank, bn) in enumerate([(psC, "psC"), (psD, "psD")]):
                        p.mm(t, [
                            (lambda bank=bank, bi=bi, c=c: t.transpose(
                                bank[:, c * 128:(c + 1) * 128],
                                xc1[:, (bi * 4 + c) * 128:(bi * 4 + c + 1) * 128],
                                C["ident_f32"][:])) for c in range(4)],
                            reads=("xc1", "s_ident_f32"), writes=(bn,))

                @blk.vector
                def _(v):
                    for bi, (bank, bn) in enumerate([(psC, "psC"), (psD, "psD")]):
                        p.op(v, "vt", lambda bank=bank, bi=bi: v.tensor_copy(
                            out=xc1T[:, bi * 512:(bi + 1) * 512], in_=bank[:]),
                            reads=(bn,), writes=("xc1T",))

                @blk.tensor
                def _(t):
                    p.mm(t, [
                        (lambda cdi=cdi: t.matmul(
                            psC[:, 0:64],
                            xc1T[:, cdi * 128:(cdi + 1) * 128],
                            gwT[:, cdi * NE:(cdi + 1) * NE],
                            start=(cdi == 0), stop=(cdi == 7))) for cdi in range(8)],
                        reads=("xc1T", "gwT"), writes=("psC",))

                @blk.vector
                def _(v):
                    p.op(v, "vt", lambda: v.tensor_scalar_mul(
                        out=lgt[:], in0=psC[:, 0:64], scalar1=rstd1[:]),
                        reads=("psC", "rstd1"), writes=("lgt",))
                    p.op(v, "vt", lambda: v.max(out=top8[:], in_=lgt[:]),
                         reads=("lgt",), writes=("top8",))
                    p.op(v, "vt", lambda: v.tensor_scalar_mul(
                        out=nmax0[:], in0=top8[:, 0:1], scalar1=-1.0),
                        reads=("top8",), writes=("nmax0",))

                @blk.scalar
                def _(s):
                    p.op(s, "at", lambda: s.activation(
                        out=sd_t[:, TOPK:2 * TOPK], in_=top8[:], func=AF.Exp,
                        bias=nmax0[:]), reads=("top8", "nmax0"), writes=("sd_t",))

                @blk.vector
                def _(v):
                    p.op(v, "vt", lambda: v.tensor_tensor(
                        out=eq3[:],
                        in0=lgt[:].unsqueeze(1).to_broadcast(
                            [128, TOPK, NE]),
                        in1=top8[:].unsqueeze(2).to_broadcast(
                            [128, TOPK, NE]),
                        op=ALU.is_equal), reads=("lgt", "top8"), writes=("eq3",))
                    p.op(v, "vt", lambda: v.tensor_mul(
                        out=eq3[:], in0=eq3[:], in1=C["iotaM"][:]),
                        reads=("eq3", "s_iotaM"), writes=("eq3",))
                    p.op(v, "vt", lambda: v.tensor_reduce(
                        out=sd_t[:, 0:TOPK],
                        in_=eq3[:].rearrange("p (k e) -> p k e", k=TOPK),
                        axis=AX.X, op=ALU.min), reads=("eq3",), writes=("sd_t",))
                    p.op(v, "vt", lambda: v.tensor_scalar_add(
                        out=sd_t[:, 0:TOPK], in0=sd_t[:, 0:TOPK], scalar1=BIGIDX),
                        reads=("sd_t",), writes=("sd_t",))
                    p.op(v, "vt", lambda: v.tensor_tensor(
                        out=oh3[:],
                        in0=sd_t[:, 0:TOPK].unsqueeze(2).to_broadcast(
                            [128, TOPK, NE]),
                        in1=C["iotaE"][:].rearrange("p (k e) -> p k e", k=TOPK),
                        op=ALU.is_equal), reads=("sd_t", "s_iotaE"), writes=("oh3",))
                    p.op(v, "vt", lambda: v.tensor_reduce(
                        out=cntT[:],
                        in_=oh3[:].rearrange("p (k e) -> p e k", k=TOPK),
                        axis=AX.X, op=ALU.add), reads=("oh3",), writes=("cntT",))

                if lvl < 12:
                    continue
                @blk.tensor
                def _(t):
                    p.mm(t, [
                        lambda: t.matmul(psD[:, 0:64], C["tstrictT"][:], cntT[:],
                                         start=True, stop=False),
                        lambda: t.matmul(psD[:, 0:64], C["ones_row"][:], carry[:],
                                         start=False, stop=True)],
                        reads=("s_tstrictT", "cntT", "s_ones_row", "carry"),
                        writes=("psD",))
                    p.mm(t, [
                        lambda: t.matmul(psD[0:1, 128:192], C["ones_col"][:],
                                         cntT[:], start=True, stop=True)],
                        reads=("s_ones_col", "cntT"), writes=("psD",))

                @blk.vector
                def _(v):
                    p.op(v, "vt", lambda: v.tensor_copy(
                        out=base_sb[:], in_=psD[:, 0:64]),
                        reads=("psD",), writes=("base_sb",))
                    p.op(v, "vt", lambda: v.tensor_add(
                        out=carry[:], in0=carry[:], in1=psD[0:1, 128:192]),
                        reads=("carry", "psD"), writes=("carry",))
                    p.op(v, "vt", lambda: v.tensor_tensor(
                        out=eq3[:], in0=oh3[:],
                        in1=base_sb[:].unsqueeze(1).to_broadcast(
                            [128, TOPK, NE]), op=ALU.mult),
                        reads=("oh3", "base_sb"), writes=("eq3",))
                    p.op(v, "vt", lambda: v.tensor_reduce(
                        out=sd_t[:, 2 * TOPK:3 * TOPK],
                        in_=eq3[:].rearrange("p (k e) -> p k e", k=TOPK),
                        axis=AX.X, op=ALU.add), reads=("eq3",), writes=("sd_t",))
                    p.op(v, "vt", lambda: v.tensor_tensor(
                        out=scr64[:],
                        in0=sd_t[:, 0:TOPK].unsqueeze(2).to_broadcast(
                            [128, TOPK, TOPK]),
                        in1=sd_t[:, 0:TOPK].unsqueeze(1).to_broadcast(
                            [128, TOPK, TOPK]),
                        op=ALU.is_equal), reads=("sd_t",), writes=("scr64",))
                    p.op(v, "vt", lambda: v.tensor_mul(
                        out=scr64[:], in0=scr64[:], in1=C["kkmask"][:]),
                        reads=("scr64", "s_kkmask"), writes=("scr64",))
                    p.op(v, "vt", lambda: v.tensor_reduce(
                        out=scr8[:],
                        in_=scr64[:].rearrange("p (k j) -> p k j", k=TOPK),
                        axis=AX.X, op=ALU.add), reads=("scr64",), writes=("scr8",))
                    p.op(v, "vt", lambda: v.tensor_add(
                        out=sd_t[:, 2 * TOPK:3 * TOPK],
                        in0=sd_t[:, 2 * TOPK:3 * TOPK], in1=scr8[:]),
                        reads=("sd_t", "scr8"), writes=("sd_t",))

                @blk.sync
                def _(sync):
                    p.dma(sync, "sync", "st",
                          sdat_h[tile * 128:(tile + 1) * 128, :], sd_t[:],
                          reads=("sd_t",))

        @blk.sync
        def _(sync):
            p.dma(sync, "sync", "st", cnt_h, carry[:], reads=("carry",))

        @blk.gpsimd
        def _(g):
            g.wait_ge(p.sem["st"], p.n["st"])
            g.wait_ge(p.sem["sd"], p.n["sd"])
            g.wait_ge(p.sem["gd"], p.n["gd"])
            g.wait_ge(p.sem["x0"], p.n["x0"])
            g.wait_ge(p.sem["x1"], p.n["x1"])
    return p


def launch1_inputs(core, full):
    x = full["L0_attn_x"].reshape(B * S, D)
    d = {
        "x": np.ascontiguousarray(x[core * TPC:(core + 1) * TPC]),
        "q_w": full["L0_attn_q_w"],
        "kv_w": full["L0_attn_kv_w"],
        "proj_w": full["L0_attn_proj_w"],
        "proj_b": full["L0_attn_proj_b"].reshape(1, D),
        "gate_w": full["L0_ffn_gate_w"],
    }
    d.update(const_inputs(L1_CONSTS))
    return d


def kernel(**inputs):
    cores = list(range(NCORES))
    p1 = build_launch1()
    r1 = run_bass_kernel_spmd(p1.nc, [launch1_inputs(c, inputs) for c in cores],
                              core_ids=cores)
    L1r = [r1.results[c] for c in cores]
    cnts_all = np.concatenate([L1r[c]["cnt"] for c in cores], axis=0)

    p2 = build_launch2()
    r2 = run_bass_kernel_spmd(
        p2.nc,
        [launch2_inputs(c, cnts_all, L1r[c]["slotdat"], L1r[c]["xln1b"])
         for c in cores], core_ids=cores)
    L2r = [r2.results[c] for c in cores]
    contribs = [L2r[c]["contrib"] for c in cores]

    p3 = build_launch3()
    r3 = run_bass_kernel_spmd(p3.nc, [launch3_inputs(c, contribs, inputs)
                                      for c in cores], core_ids=cores)
    y_full = np.concatenate([r3.results[c]["y"] for c in cores], axis=0)

    p4a = build_launch4a()
    r4a = run_bass_kernel_spmd(
        p4a.nc,
        [launch4a_inputs(c, L1r[c]["x1"], L2r[c]["wg"], y_full, inputs)
         for c in cores], core_ids=cores)

    p4b = build_launch4b()
    r4b = run_bass_kernel_spmd(
        p4b.nc,
        [launch4b_inputs(c, r4a.results[c]["x3"], inputs) for c in cores],
        core_ids=cores)
    out = np.concatenate([r4b.results[c]["out"] for c in cores], axis=0)
    return out.reshape(B, S, D)



L2_CONSTS = ["ones_row", "iotaE"]


def build_launch2():
    p = P("l2")
    nc = p.nc
    cnts_h = nc.dram_tensor("cnts_all", [NCORES, NE], F32, kind="ExternalInput").ap()
    mask_h = nc.dram_tensor("coremask", [NCORES, 1], F32, kind="ExternalInput").ap()
    sdat_h = nc.dram_tensor("slotdat", [TPC, 3 * TOPK], F32, kind="ExternalInput").ap()
    xln1_h = nc.dram_tensor("xln1b", [TPC, D], BF16, kind="ExternalInput").ap()

    contrib_h = nc.dram_tensor("contrib", [NE * CAP, D], BF16, kind="ExternalOutput").ap()
    wg_h = nc.dram_tensor("wg", [TPC, 2 * TOPK], F32, kind="ExternalOutput").ap()

    hbs, C = _declare_consts(p, L2_CONSTS)
    with p.ctx:
        for s in ["sd", "gd", "zs", "id", "pt", "vt", "at", "st", "x0", "x1"]:
            p.semaphore(s)
        cnts = p.sbuf("cnts", [NCORES, NE])
        cmask = p.sbuf("cmask", [NCORES, 1])
        offs = p.sbuf("offs", [1, NE])
        offgb = p.sbuf("offgb", [128, NE])
        sdall = p.sbuf("sdall", [128, NT * 3 * TOPK])
        wgall = p.sbuf("wgall", [128, NT * 2 * TOPK])
        zt = p.sbuf("zt", [128, 8192], BF16)
        xt = [p.sbuf(f"xt{i}", [128, D], BF16) for i in range(2)]
        oh3 = p.sbuf("oh3", [128, TOPK * NE])
        selr = p.sbuf("selr", [128, TOPK])
        g_t = p.sbuf("g_t", [128, TOPK])
        gi2 = [p.sbuf(f"gi{i}", [128, TOPK], U32) for i in range(2)]
        idx256 = p.sbuf("idx256", [128, TOPK])
        valid = p.sbuf("valid", [128, TOPK])
        pv = p.sbuf("pv", [128, TOPK])
        wsum = p.sbuf("wsum", [128, 1])
        psA = p.psum("psA", [128, 512])
        blk = p.ctx.enter_context(nc.Block())
        _load_consts(p, blk, hbs, C)
        bcreg = nc.alloc_register(mybir.EngineType.Pool)

        @blk.gpsimd
        def _(g):
            g.reg_mov(bcreg, NE * CAP - 1)

        @blk.sync
        def _(sync):
            p.dma(sync, "sync", "sd", cnts[:], cnts_h, writes=("cnts",))
            p.dma(sync, "sync", "sd", cmask[:], mask_h, writes=("cmask",))
            p.dma(sync, "sync", "sd",
                  sdall[:].rearrange("p (t c) -> p t c", t=NT),
                  sdat_h.rearrange("(t p) c -> p t c", p=128), writes=("sdall",))

        @blk.tensor
        def _(t):
            p.mm(t, [lambda: t.matmul(psA[0:1, 0:64], cmask[:], cnts[:],
                                      start=True, stop=True)],
                 reads=("cmask", "cnts"), writes=("psA",))

        @blk.vector
        def _(v):
            p.op(v, "vt", lambda: v.tensor_copy(out=offs[:], in_=psA[0:1, 0:64]),
                 reads=("psA",), writes=("offs",))
            p.op(v, "vt", lambda: v.scalar_tensor_tensor(
                out=offs[:], in0=C["iotaE"][0:1, 0:NE], scalar=float(CAP),
                in1=offs[:], op0=ALU.mult, op1=ALU.add),
                reads=("offs", "s_iotaE"), writes=("offs",))

        @blk.tensor
        def _(t):
            p.mm(t, [lambda: t.matmul(psA[:, 64:128], C["ones_row"][:], offs[:],
                                      start=True, stop=True)],
                 reads=("s_ones_row", "offs"), writes=("psA",))

        @blk.vector
        def _(v):
            p.op(v, "vt", lambda: v.tensor_copy(out=offgb[:], in_=psA[:, 64:128]),
                 reads=("psA",), writes=("offgb",))
            p.op(v, "vt", lambda: v.memset(zt[:], 0.0), writes=("zt",))

        @blk.sync
        def _(sync):
            for i in range(16):
                p.dma(sync, "sync", "zs",
                      contrib_h[1024 * i:1024 * (i + 1), :].rearrange(
                          "(r p) d -> p r d", p=128),
                      zt[:].rearrange("p (r d) -> p r d", r=8),
                      reads=("zt",), writes=("contrib",))

        for tile in range(NT):
            par = tile % 2
            xtb = xt[par]
            xtn = f"xt{par}"
            xsem = "x0" if par == 0 else "x1"
            sd = sdall[:, tile * 24: tile * 24 + 24]
            wg = wgall[:, tile * 16: tile * 16 + 16]

            @blk.sync
            def _(sync):
                p.dma(sync, "sync", xsem, xtb[:],
                      xln1_h[tile * 128:(tile + 1) * 128, :], writes=(xtn,))

            @blk.vector
            def _(v):
                p.op(v, "vt", lambda sd=sd: v.tensor_tensor(
                    out=oh3[:],
                    in0=sd[:, 0:TOPK].unsqueeze(2).to_broadcast([128, TOPK, NE]),
                    in1=C["iotaE"][:].rearrange("p (k e) -> p k e", k=TOPK),
                    op=ALU.is_equal), reads=("sdall", "s_iotaE"), writes=("oh3",))
                p.op(v, "vt", lambda: v.tensor_tensor(
                    out=oh3[:], in0=oh3[:],
                    in1=offgb[:].unsqueeze(1).to_broadcast([128, TOPK, NE]),
                    op=ALU.mult), reads=("oh3", "offgb"), writes=("oh3",))
                p.op(v, "vt", lambda: v.tensor_reduce(
                    out=selr[:], in_=oh3[:].rearrange("p (k e) -> p k e", k=TOPK),
                    axis=AX.X, op=ALU.add), reads=("oh3",), writes=("selr",))
                p.op(v, "vt", lambda sd=sd: v.tensor_tensor(
                    out=g_t[:], in0=sd[:, 2 * TOPK:3 * TOPK], in1=selr[:],
                    op=ALU.add), reads=("sdall", "selr"), writes=("g_t",))
                p.op(v, "vt", lambda sd=sd: v.tensor_scalar_mul(
                    out=idx256[:], in0=sd[:, 0:TOPK], scalar1=float(CAP)),
                    reads=("sdall",), writes=("idx256",))
                p.op(v, "vt", lambda: v.tensor_sub(
                    out=valid[:], in0=g_t[:], in1=idx256[:]),
                    reads=("g_t", "idx256"), writes=("valid",))
                p.op(v, "vt", lambda: v.tensor_single_scalar(
                    out=valid[:], in_=valid[:], scalar=float(CAP), op=ALU.is_lt),
                    reads=("valid",), writes=("valid",))
                p.op(v, "vt", lambda sd=sd: v.tensor_tensor(
                    out=pv[:], in0=sd[:, TOPK:2 * TOPK], in1=valid[:],
                    op=ALU.mult), reads=("sdall", "valid"), writes=("pv",))
                p.op(v, "vt", lambda sd=sd: v.reduce_sum(
                    out=wsum[:], in_=sd[:, TOPK:2 * TOPK], axis=AX.X),
                    reads=("sdall",), writes=("wsum",))
                p.op(v, "vt", lambda: v.reciprocal(out=wsum[:], in_=wsum[:]),
                     reads=("wsum",), writes=("wsum",))
                p.op(v, "vt", lambda wg=wg: v.tensor_scalar_mul(
                    out=wg[:, 0:TOPK], in0=pv[:], scalar1=wsum[:]),
                    reads=("pv", "wsum"), writes=("wgall",))
                p.op(v, "vt", lambda: v.tensor_single_scalar(
                    out=pv[:], in_=valid[:], scalar=1.0, op=ALU.is_lt),
                    reads=("valid",), writes=("pv",))
                p.op(v, "vt", lambda wg=wg: v.scalar_tensor_tensor(
                    out=wg[:, TOPK:2 * TOPK], in0=pv[:], scalar=BIGIDX,
                    in1=g_t[:], op0=ALU.mult, op1=ALU.add),
                    reads=("pv", "g_t"), writes=("wgall",))
                p.op(v, "vt", lambda wg=wg, tile=tile: v.tensor_copy(
                    out=gi2[tile % 2][:], in_=wg[:, TOPK:2 * TOPK]),
                    reads=("wgall",), writes=(f"gi{tile % 2}",))

            @blk.gpsimd
            def _(g):
                for k in range(TOPK):
                    p.idma(g, "gp", "id",
                           out=contrib_h,
                           out_offset=IndirectOffsetOnAxis(
                               gi2[tile % 2][:, k:k + 1], 0),
                           in_=xtb[:], in_offset=None,
                           bounds_check=bcreg, oob_is_err=False,
                           reads=(f"gi{tile % 2}", xtn, "contrib"),
                           writes=("contrib",))

        @blk.sync
        def _(sync):
            p.dma(sync, "sync", "st",
                  wg_h.rearrange("(t p) c -> p t c", p=128),
                  wgall[:].rearrange("p (t c) -> p t c", t=NT),
                  reads=("wgall",))

        @blk.gpsimd
        def _(g):
            for s in ["st", "sd", "zs", "id", "x0", "x1", "gd"]:
                g.wait_ge(p.sem[s], p.n[s])
    return p


def launch2_inputs(core, cnts_all, sdat, xln1):
    mask = np.zeros((NCORES, 1), np.float32)
    mask[:core] = 1.0
    d = {"cnts_all": cnts_all, "coremask": mask, "slotdat": sdat, "xln1b": xln1}
    d.update(const_inputs(L2_CONSTS))
    return d



L3_CONSTS = ["ident_bf"]


def build_launch3():
    p = P("l3")
    nc = p.nc
    csl_h = nc.dram_tensor("contribs_sl", [NCORES, EPC * CAP, D], BF16,
                           kind="ExternalInput").ap()
    wg_h = nc.dram_tensor("ew_gate", [EPC, EI, D], F32, kind="ExternalInput").ap()
    wu_h = nc.dram_tensor("ew_up", [EPC, EI, D], F32, kind="ExternalInput").ap()
    wd_h = nc.dram_tensor("ew_down", [EPC, D, EI], F32, kind="ExternalInput").ap()
    y_h = nc.dram_tensor("y", [EPC * CAP, D], BF16, kind="ExternalOutput").ap()

    hbs, C = _declare_consts(p, L3_CONSTS)
    NIC = EI // 128
    with p.ctx:
        for s in ["sd", "gd", "ld0", "pt", "vt", "at", "st"]:
            p.semaphore(s)
        disp = p.sbuf("disp", [128, 2 * D], BF16)
        dispT = p.sbuf("dispT", [128, 8 * CAP], BF16)
        wgt = [p.sbuf(f"wgt{i}", [128, NIC * D], BF16) for i in range(2)]
        wgT = p.sbuf("wgT", [128, 8 * EI], BF16)
        wuT = p.sbuf("wuT", [128, 8 * EI], BF16)
        wdr = [p.sbuf(f"wdr{i}", [128, 8 * EI], BF16) for i in range(2)]
        wdT = p.sbuf("wdT", [128, NIC * D], BF16)
        gsb = p.sbuf("gsb", [128, 2 * EI])
        hsb = p.sbuf("hsb", [128, 2 * EI], BF16)
        hT = p.sbuf("hT", [128, NIC * CAP], BF16)
        ysb = p.sbuf("ysb", [128, 2 * D], BF16)
        ptA = p.psum("ptA", [128, 1024], BF16)
        ptB = p.psum("ptB", [128, 1024], BF16)
        psA = p.psum("psA", [128, 512])
        psB = p.psum("psB", [128, 512])
        psC = p.psum("psC", [128, 512])
        psD = p.psum("psD", [128, 512])
        psE = p.psum("psE", [128, 512])
        psF = p.psum("psF", [128, 512])
        blk = p.ctx.enter_context(nc.Block())
        _load_consts(p, blk, hbs, C)

        for e in range(EPC):
            wpar = e % 2
            wgt_b, wdr_b = wgt[wpar], wdr[wpar]
            wgt_n, wdr_n = f"wgt{wpar}", f"wdr{wpar}"

            @blk.sync
            def _(sync):
                p.dma(sync, "sync", "sd",
                      disp[:].rearrange("p (t d) -> p t d", t=2),
                      csl_h[0, e * CAP:(e + 1) * CAP, :].rearrange(
                          "(t p) d -> p t d", p=128),
                      writes=("disp",))

            @blk.gpsimd
            def _(g):
                for c2 in range(1, NCORES):
                    p.dma(g, "gp", "sd",
                          disp[:].rearrange("p (t d) -> p t d", t=2),
                          csl_h[c2, e * CAP:(e + 1) * CAP, :].rearrange(
                              "(t p) d -> p t d", p=128),
                          reads=("disp",), writes=("disp",), accum_op=ALU.add)
                p.dma(g, "gp", "ld0", wgt_b[:].rearrange("p (c d) -> p c d", c=NIC),
                      wg_h[e].rearrange("(c p) d -> p c d", p=128),
                      writes=(wgt_n,))

            @blk.tensor
            def _(t):
                for i in range(2):
                    p.mm(t, [
                        (lambda i=i, c=c: t.transpose(
                            ptA[:, c * 128:(c + 1) * 128],
                            disp[:, i * D + c * 128: i * D + (c + 1) * 128],
                            C["ident_bf"][:])) for c in range(8)],
                        reads=("disp", "s_ident_bf"), writes=("ptA",))
                    @blk.vector
                    def _(v):
                        p.op(v, "vt", lambda i=i: v.tensor_copy(
                            out=dispT[:].rearrange("p (c x) -> p c x", c=8)[
                                :, :, i * 128:(i + 1) * 128],
                            in_=ptA[:].rearrange("p (c x) -> p c x", c=8)),
                            reads=("ptA",), writes=("dispT",))

            @blk.tensor
            def _(t):
                for ch in range(8):
                    p.mm(t, [
                        (lambda ch=ch, ci=ci: t.transpose(
                            ptB[:, ci * 128:(ci + 1) * 128],
                            wgt_b[:, ci * D + ch * 128: ci * D + (ch + 1) * 128],
                            C["ident_bf"][:])) for ci in range(NIC)],
                        reads=(wgt_n, "s_ident_bf"), writes=("ptB",))

                    @blk.vector
                    def _(v):
                        p.op(v, "vt", lambda ch=ch: v.tensor_copy(
                            out=wgT[:, ch * EI:(ch + 1) * EI],
                            in_=ptB[:, 0:EI]), reads=("ptB",), writes=("wgT",))

            @blk.gpsimd
            def _(g):
                p.dma(g, "gp", "ld0", wgt_b[:].rearrange("p (c d) -> p c d", c=NIC),
                      wu_h[e].rearrange("(c p) d -> p c d", p=128),
                      writes=(wgt_n,))

            @blk.tensor
            def _(t):
                for ch in range(8):
                    p.mm(t, [
                        (lambda ch=ch, ci=ci: t.transpose(
                            ptB[:, ci * 128:(ci + 1) * 128],
                            wgt_b[:, ci * D + ch * 128: ci * D + (ch + 1) * 128],
                            C["ident_bf"][:])) for ci in range(NIC)],
                        reads=(wgt_n, "s_ident_bf"), writes=("ptB",))

                    @blk.vector
                    def _(v):
                        p.op(v, "vt", lambda ch=ch: v.tensor_copy(
                            out=wuT[:, ch * EI:(ch + 1) * EI],
                            in_=ptB[:, 0:EI]), reads=("ptB",), writes=("wuT",))

            @blk.gpsimd
            def _(g):
                p.dma(g, "gp", "ld0", wdr_b[:].rearrange("p (c d) -> p c d", c=8),
                      wd_h[e].rearrange("(c p) d -> p c d", p=128),
                      writes=(wdr_n,))

            @blk.tensor
            def _(t):
                for ci in range(NIC):
                    p.mm(t, [
                        (lambda ci=ci, ch=ch: t.transpose(
                            ptB[:, ch * 128:(ch + 1) * 128],
                            wdr_b[:, ch * EI + ci * 128: ch * EI + (ci + 1) * 128],
                            C["ident_bf"][:])) for ch in range(8)],
                        reads=(wdr_n, "s_ident_bf"), writes=("ptB",))

                    @blk.vector
                    def _(v):
                        p.op(v, "vt", lambda ci=ci: v.tensor_copy(
                            out=wdT[:, ci * D:(ci + 1) * D],
                            in_=ptB[:]), reads=("ptB",), writes=("wdT",))

            for cc in range(2):
                @blk.tensor
                def _(t):
                    for nb in range(2):
                        nw = 512 if nb == 0 else EI - 512
                        bankG = [psA, psB][nb]
                        bankU = [psC, psD][nb]
                        bG = ["psA", "psB"][nb]
                        bU = ["psC", "psD"][nb]
                        p.mm(t, [
                            (lambda cc=cc, nb=nb, nw=nw, bankG=bankG, ch=ch: t.matmul(
                                bankG[:, 0:nw],
                                dispT[:, ch * CAP + cc * 128: ch * CAP + cc * 128 + 128],
                                wgT[:, ch * EI + nb * 512: ch * EI + nb * 512 + nw],
                                start=(ch == 0), stop=(ch == 7))) for ch in range(8)],
                            reads=("dispT", "wgT"), writes=(bG,))
                        p.mm(t, [
                            (lambda cc=cc, nb=nb, nw=nw, bankU=bankU, ch=ch: t.matmul(
                                bankU[:, 0:nw],
                                dispT[:, ch * CAP + cc * 128: ch * CAP + cc * 128 + 128],
                                wuT[:, ch * EI + nb * 512: ch * EI + nb * 512 + nw],
                                start=(ch == 0), stop=(ch == 7))) for ch in range(8)],
                            reads=("dispT", "wuT"), writes=(bU,))

                @blk.scalar
                def _(s):
                    p.op(s, "at", lambda cc=cc: s.activation(
                        out=gsb[:, cc * EI: cc * EI + 512], in_=psA[:],
                        func=AF.Silu), reads=("psA",), writes=("gsb",))
                    p.op(s, "at", lambda cc=cc: s.activation(
                        out=gsb[:, cc * EI + 512: (cc + 1) * EI], in_=psB[:, 0:EI - 512],
                        func=AF.Silu), reads=("psB",), writes=("gsb",))

                @blk.vector
                def _(v):
                    p.op(v, "vt", lambda cc=cc: v.tensor_tensor(
                        out=hsb[:, cc * EI: cc * EI + 512],
                        in0=gsb[:, cc * EI: cc * EI + 512], in1=psC[:],
                        op=ALU.mult), reads=("gsb", "psC"), writes=("hsb",))
                    p.op(v, "vt", lambda cc=cc: v.tensor_tensor(
                        out=hsb[:, cc * EI + 512: (cc + 1) * EI],
                        in0=gsb[:, cc * EI + 512: (cc + 1) * EI],
                        in1=psD[:, 0:EI - 512],
                        op=ALU.mult), reads=("gsb", "psD"), writes=("hsb",))

            @blk.tensor
            def _(t):
                for ci in range(NIC):
                    p.mm(t, [
                        (lambda ci=ci, cc=cc: t.transpose(
                            ptA[:, cc * 128:(cc + 1) * 128],
                            hsb[:, cc * EI + ci * 128: cc * EI + (ci + 1) * 128],
                            C["ident_bf"][:])) for cc in range(2)],
                        reads=("hsb", "s_ident_bf"), writes=("ptA",))

                    @blk.vector
                    def _(v):
                        p.op(v, "vt", lambda ci=ci: v.tensor_copy(
                            out=hT[:, ci * CAP:(ci + 1) * CAP],
                            in_=ptA[:, 0:CAP]), reads=("ptA",), writes=("hT",))

            for cc in range(2):
                @blk.tensor
                def _(t):
                    for nb in range(2):
                        bank = [psE, psF][nb]
                        bnm = ["psE", "psF"][nb]
                        p.mm(t, [
                            (lambda cc=cc, nb=nb, bank=bank, ci=ci: t.matmul(
                                bank[:],
                                hT[:, ci * CAP + cc * 128: ci * CAP + cc * 128 + 128],
                                wdT[:, ci * D + nb * 512: ci * D + (nb + 1) * 512],
                                start=(ci == 0), stop=(ci == NIC - 1)))
                            for ci in range(NIC)],
                            reads=("hT", "wdT"), writes=(bnm,))

                @blk.vector
                def _(v):
                    for nb, bnm in enumerate(["psE", "psF"]):
                        bank = [psE, psF][nb]
                        p.op(v, "vt", lambda cc=cc, nb=nb, bank=bank: v.tensor_copy(
                            out=ysb[:, cc * D + nb * 512: cc * D + (nb + 1) * 512],
                            in_=bank[:]), reads=(bnm,), writes=("ysb",))

            @blk.sync
            def _(sync):
                p.dma(sync, "sync", "st",
                      y_h[e * CAP:(e + 1) * CAP, :].rearrange(
                          "(t p) d -> p t d", p=128),
                      ysb[:].rearrange("p (t d) -> p t d", t=2),
                      reads=("ysb",))

        @blk.gpsimd
        def _(g):
            for s in ["st", "sd", "ld0", "gd"]:
                g.wait_ge(p.sem[s], p.n[s])
    return p


def launch3_inputs(core, contribs, full):
    sl = np.stack([contribs[c2][core * EPC * CAP:(core + 1) * EPC * CAP]
                   for c2 in range(NCORES)])
    d = {"contribs_sl": sl,
         "ew_gate": full["L0_ffn_w_gate"][core * EPC:(core + 1) * EPC],
         "ew_up": full["L0_ffn_w_up"][core * EPC:(core + 1) * EPC],
         "ew_down": full["L0_ffn_w_down"][core * EPC:(core + 1) * EPC]}
    d.update(const_inputs(L3_CONSTS))
    return d




def _emit_ln(p, blk, src, src_n, dst, dst_n, rstd_t, rstd_n, xcbuf, xc_n,
             nmean, sqs, c_eps, c_invD, c_mhalf):
    @blk.vector
    def _(v):
        p.op(v, "vt", lambda: v.reduce_sum(
            out=nmean[:, 0:1], in_=src, axis=AX.X, negate=True),
            reads=(src_n,), writes=("nmean",))
        p.op(v, "vt", lambda: v.tensor_scalar_mul(
            out=nmean[:, 0:1], in0=nmean[:, 0:1], scalar1=1.0 / D),
            reads=("nmean",), writes=("nmean",))
        p.op(v, "vt", lambda: v.tensor_scalar_add(
            out=xcbuf, in0=src, scalar1=nmean[:, 0:1]),
            reads=(src_n, "nmean"), writes=(xc_n,))

    @blk.scalar
    def _(s):
        p.op(s, "at", lambda: s.activation(
            out=sqs[:], in_=xcbuf, func=AF.Square, accum_out=rstd_t),
            reads=(xc_n,), writes=("sqs", rstd_n))
        p.op(s, "at", lambda: s.activation(
            out=rstd_t, in_=rstd_t, func=AF.Ln, scale=c_invD[:], bias=c_eps[:]),
            reads=(rstd_n, "c_invD", "c_eps"), writes=(rstd_n,))
        p.op(s, "at", lambda: s.activation(
            out=rstd_t, in_=rstd_t, func=AF.Exp, scale=c_mhalf[:]),
            reads=(rstd_n, "c_mhalf"), writes=(rstd_n,))

    if dst is not None:
        @blk.vector
        def _(v):
            p.op(v, "vt", lambda: v.tensor_scalar_mul(
                out=dst, in0=xcbuf, scalar1=rstd_t),
                reads=(xc_n, rstd_n), writes=(dst_n,))



def _p_sbuf_push(p, name, shape, dt=F32):
    guard = p.nc.sbuf_tensor(name, shape, dt)
    t = guard.__enter__()
    if not hasattr(p, "_guards"):
        p._guards = []
        p._free_stamps = []
    p._guards.append((name, guard))
    b = Buf(name)
    b.r = list(p._free_stamps)
    p.bufs[name] = b
    return t


def _p_sbuf_pop(p, upto):
    while p._guards:
        name, guard = p._guards.pop()
        b = p.bufs.pop(name)
        if b.w is not None:
            p._free_stamps.append(b.w)
        p._free_stamps.extend(b.r)
        guard.__exit__(None, None, None)
        if name == upto:
            return
    raise KeyError(upto)



L4A_CONSTS = ["ident_bf", "ones_row"]
NG = S // 512
NQB = S // 128
NKB = S // 512


def build_launch4a():
    p = P("l4a")
    nc = p.nc
    x1_h = nc.dram_tensor("x1", [TPC, D], F32, kind="ExternalInput").ap()
    wgs_h = nc.dram_tensor("wg", [TPC, 2 * TOPK], F32, kind="ExternalInput").ap()
    y_h = nc.dram_tensor("y_all", [NE * CAP, D], BF16, kind="ExternalInput").ap()
    Wq_h = nc.dram_tensor("Wq", [D, D], F32, kind="ExternalInput").ap()
    Wk_h = nc.dram_tensor("Wk", [D, D], F32, kind="ExternalInput").ap()
    Wv_h = nc.dram_tensor("Wv", [D, D], F32, kind="ExternalInput").ap()
    Wo_h = nc.dram_tensor("Wo", [D, D], F32, kind="ExternalInput").ap()
    bq_h = nc.dram_tensor("bq", [1, D], F32, kind="ExternalInput").ap()
    bk_h = nc.dram_tensor("bk", [1, D], F32, kind="ExternalInput").ap()
    bv_h = nc.dram_tensor("bv", [1, D], F32, kind="ExternalInput").ap()
    bo_h = nc.dram_tensor("bo", [1, D], F32, kind="ExternalInput").ap()
    x3_h = nc.dram_tensor("x3", [TPC, D], F32, kind="ExternalOutput").ap()

    x2_h = nc.dram_tensor("x2sp", [TPC, D], F32).ap()
    xl2_h = nc.dram_tensor("xl2sp", [TPC, D], BF16).ap()
    qt_h = nc.dram_tensor("qtsp", [D, S], BF16).ap()

    hbs, C = _declare_consts(p, L4A_CONSTS)
    with p.ctx:
        for s in ["sd", "gd", "id", "x0", "x1s", "pt", "vt", "at", "st"]:
            p.semaphore(s)
        p.DMA_SEMS = frozenset(["sd", "gd", "st", "x0", "x1s", "id"])
        bcreg = nc.alloc_register(mybir.EngineType.Pool)

        ones_bf = p.sbuf("ones_bf", [1, 128], BF16)
        onesc_bf = p.sbuf("onesc_bf", [128, 1], BF16)
        brow = p.sbuf("brow", [1, 2 * D], BF16)
        bqc = p.sbuf("bqc", [128, 16])
        c_eps = p.sbuf("c_eps", [128, 1])
        c_invD = p.sbuf("c_invD", [128, 1])
        c_mhalf = p.sbuf("c_mhalf", [128, 1])
        c_two = p.sbuf("c_two", [128, 1])
        c_m1 = p.sbuf("c_m1", [128, 1])
        nmean = p.sbuf("nmean", [128, 2])
        rstd = p.sbuf("rstd", [128, 2])

        ptA = p.psum("ptA", [128, 1024], BF16)
        ptB = p.psum("ptB", [128, 1024], BF16)
        psA = p.psum("psA", [128, 512])
        psB = p.psum("psB", [128, 512])
        psC = p.psum("psC", [128, 512])
        psD = p.psum("psD", [128, 512])
        psE = p.psum("psE", [128, 512])
        psF = p.psum("psF", [128, 512])

        blk = p.ctx.enter_context(nc.Block())
        _load_consts(p, blk, hbs, C)

        onesc = p.sbuf("onesc", [1, 1])
        bqr = _p_sbuf_push(p, "bqr", [1, 2 * D])

        @blk.gpsimd
        def _(g):
            g.reg_mov(bcreg, NE * CAP - 1)
            p.dma(g, "gp", "gd", ones_bf[:], hbs["ones_row"], writes=("ones_bf",))
            p.dma(g, "gp", "gd", onesc_bf[:], hbs["ones_row"].rearrange("o p -> p o"),
                  writes=("onesc_bf",))
            p.dma(g, "gp", "gd", brow[0:1, 0:D], bv_h, writes=("brow",))
            p.dma(g, "gp", "gd", brow[0:1, D:2 * D], bo_h, writes=("brow",))

        @blk.sync
        def _(sync):
            p.dma(sync, "sync", "sd", bqr[0:1, 0:D], bq_h, writes=("bqr",))
            p.dma(sync, "sync", "sd", bqr[0:1, D:2 * D], bk_h, writes=("bqr",))

        @blk.vector
        def _(v):
            p.op(v, "vt", lambda: v.memset(c_eps[:], LN_EPS), writes=("c_eps",))
            p.op(v, "vt", lambda: v.memset(c_invD[:], 1.0 / D), writes=("c_invD",))
            p.op(v, "vt", lambda: v.memset(c_mhalf[:], -0.5), writes=("c_mhalf",))
            p.op(v, "vt", lambda: v.memset(c_two[:], 2.0), writes=("c_two",))
            p.op(v, "vt", lambda: v.memset(c_m1[:], -1.0), writes=("c_m1",))
            p.op(v, "vt", lambda: v.memset(onesc[:], 1.0), writes=("onesc",))

        @blk.tensor
        def _(t):
            for c in range(16):
                p.mm(t, [lambda c=c: t.matmul(
                    psA[:, c:c + 1], bqr[0:1, c * 128:(c + 1) * 128],
                    onesc[:], start=True, stop=True)],
                    reads=("bqr", "onesc"), writes=("psA",))

        @blk.vector
        def _(v):
            p.op(v, "vt", lambda: v.tensor_copy(out=bqc[:], in_=psA[:, 0:16]),
                 reads=("psA",), writes=("bqc",))
        _p_sbuf_pop(p, "bqr")

        yti = [_p_sbuf_push(p, f"yti{k}", [128, D], BF16)
               for k in range(2 * TOPK)]
        x1t = [_p_sbuf_push(p, f"x1t{i}", [128, D]) for i in range(2)]
        wgt_t2 = [_p_sbuf_push(p, f"wgt_t{i}", [128, 2 * TOPK]) for i in range(2)]
        gidx2 = [_p_sbuf_push(p, f"gidx{i}", [128, TOPK], U32) for i in range(2)]
        acc = _p_sbuf_push(p, "acc", [128, D])
        x2t = _p_sbuf_push(p, "x2t", [128, D])
        xcb = _p_sbuf_push(p, "xcb", [128, D])
        sqs = _p_sbuf_push(p, "sqs", [128, D])
        xlnb = _p_sbuf_push(p, "xlnb", [128, D], BF16)

        @blk.vector
        def _(v):
            for k in range(2 * TOPK):
                p.op(v, "vt", lambda k=k: v.memset(yti[k][:], 0.0),
                     writes=(f"yti{k}",))

        for tile in range(NT):
            gtok = tile * 128
            par = tile % 2
            x1b, x1n = x1t[par], f"x1t{par}"
            xsem = "x0" if par == 0 else "x1s"

            wgt_t = wgt_t2[tile % 2]
            wgn = f"wgt_t{tile % 2}"

            @blk.sync
            def _(sync):
                p.dma(sync, "sync", xsem, x1b[:], x1_h[gtok:gtok + 128, :],
                      writes=(x1n,))
                p.dma(sync, "sync", xsem, wgt_t[:], wgs_h[gtok:gtok + 128, :],
                      writes=(wgn,))

            gidx = gidx2[tile % 2]
            gxn = f"gidx{tile % 2}"

            @blk.vector
            def _(v):
                p.op(v, "vt", lambda gidx=gidx, wgt_t=wgt_t: v.tensor_copy(
                    out=gidx[:], in_=wgt_t[:, TOPK:2 * TOPK]),
                    reads=(wgn,), writes=(gxn,))

            @blk.gpsimd
            def _(g):
                for k in range(TOPK):
                    p.idma(g, "gp", "id",
                           out=yti[k][:], out_offset=None,
                           in_=y_h,
                           in_offset=IndirectOffsetOnAxis(gidx[:, k:k + 1], 0),
                           bounds_check=bcreg, oob_is_err=False,
                           reads=("gidx", f"yti{k}"), writes=(f"yti{k}",))

            @blk.vector
            def _(v):
                p.op(v, "vt", lambda: v.tensor_scalar_mul(
                    out=acc[:], in0=yti[0][:], scalar1=wgt_t[:, 0:1]),
                    reads=("yti0", "wgt_t"), writes=("acc",))
                for k in range(1, TOPK):
                    p.op(v, "vt", lambda k=k: v.scalar_tensor_tensor(
                        out=acc[:], in0=yti[k][:], scalar=wgt_t[:, k:k + 1],
                        in1=acc[:], op0=ALU.mult, op1=ALU.add),
                        reads=(f"yti{k}", "wgt_t", "acc"), writes=("acc",))
                p.op(v, "vt", lambda: v.tensor_add(
                    out=x2t[:], in0=x1b[:], in1=acc[:]),
                    reads=(x1n, "acc"), writes=("x2t",))

            _emit_ln(p, blk, x2t[:], "x2t", xlnb[:], "xlnb", rstd[:, 0:1], "rstd",
                     xcb[:], "xcb", nmean, sqs, c_eps, c_invD, c_mhalf)

            @blk.sync
            def _(sync):
                p.dma(sync, "sync", "st", x2_h[gtok:gtok + 128, :], x2t[:],
                      reads=("x2t",))
                p.dma(sync, "sync", "st", xl2_h[gtok:gtok + 128, :], xlnb[:],
                      reads=("xlnb",))

        _p_sbuf_pop(p, "yti0")

        Kt = _p_sbuf_push(p, "Kt", [128, 8 * S], BF16)
        Vt = _p_sbuf_push(p, "Vt", [128, (S // 128) * D], BF16)
        nk2h = _p_sbuf_push(p, "nk2h", [1, S], BF16)
        q2row = _p_sbuf_push(p, "q2row", [1, S], BF16)
        onesc2 = _p_sbuf_push(p, "onesc2", [1, 1], BF16)
        wa = [_p_sbuf_push(p, f"wa{i}", [128, 8 * D], BF16) for i in range(2)]
        xlg = [_p_sbuf_push(p, "xlg0", [128, D], BF16)] * 2
        xlTg = _p_sbuf_push(p, "xlTg", [128, 8 * 512], BF16)
        ksq = _p_sbuf_push(p, "ksq", [128, 512], BF16)
        q2c = _p_sbuf_push(p, "q2c", [128, 1])
        Qg = _p_sbuf_push(p, "Qg", [128, 1024], BF16)
        Psb = [_p_sbuf_push(p, f"PsbB{i}", [128, 512], BF16) for i in range(2)]
        PTs = [_p_sbuf_push(p, f"PTs{i}", [128, 512], BF16) for i in range(2)]
        dcol = _p_sbuf_push(p, "dcol", [128, NKB])
        rcpc = _p_sbuf_push(p, "rcpc", [128, 1])
        attn = _p_sbuf_push(p, "attn4", [128, D], BF16)
        attnT = _p_sbuf_push(p, "attnT4", [128, 8 * 128], BF16)
        x3t = _p_sbuf_push(p, "x3t", [128, D])

        @blk.vector
        def _(v):
            p.op(v, "vt", lambda: v.memset(onesc2[:], 1.0), writes=("onesc2",))

        for sq in range(2):
            toff = sq * (TPC // 2)

            @blk.gpsimd
            def _(g):
                p.dma(g, "gp", "gd", wa[0][:].rearrange("p (c d) -> p c d", c=8),
                      Wk_h.rearrange("(c p) d -> p c d", p=128), writes=("wa0",))
                p.dma(g, "gp", "gd", wa[1][:].rearrange("p (c d) -> p c d", c=8),
                      Wv_h.rearrange("(c p) d -> p c d", p=128), writes=("wa1",))

            for grp in range(NG):
                for ti in range(4):
                    gtok = toff + (grp * 4 + ti) * 128
                    par = ti % 2
                    xb, xn_ = xlg[0], "xlg0"
                    xsem = "x0" if par == 0 else "x1s"

                    @blk.sync
                    def _(sync):
                        p.dma(sync, "sync", xsem, xb[:],
                              xl2_h[gtok:gtok + 128, :], writes=(xn_,))

                    ptn = "ptA" if ti % 2 == 0 else "ptB"
                    pt = ptA if ti % 2 == 0 else ptB

                    @blk.tensor
                    def _(t):
                        p.mm(t, [
                            (lambda pt=pt, c=c, xb=xb: t.transpose(
                                pt[:, c * 128:(c + 1) * 128],
                                xb[:, c * 128:(c + 1) * 128],
                                C["ident_bf"][:])) for c in range(8)],
                            reads=(xn_, "s_ident_bf"), writes=(ptn,))

                    @blk.vector
                    def _(v):
                        p.op(v, "vt", lambda pt=pt, ti=ti: v.tensor_copy(
                            out=xlTg[:].rearrange("p (c x) -> p c x", c=8)[
                                :, :, ti * 128:(ti + 1) * 128],
                            in_=pt[:].rearrange("p (c x) -> p c x", c=8)),
                            reads=(ptn,), writes=("xlTg",))

                for wave in range(4):
                    @blk.tensor
                    def _(t):
                        for j in range(2):
                            oc = wave * 2 + j
                            bank = psE if j == 0 else psF
                            bnm = "psE" if j == 0 else "psF"
                            p.mm(t, [
                                (lambda bank=bank, oc=oc, cdi=cdi: t.matmul(
                                    bank[:],
                                    wa[0][:, cdi * D + oc * 128:
                                          cdi * D + (oc + 1) * 128],
                                    xlTg[:, cdi * 512:(cdi + 1) * 512],
                                    start=(cdi == 0), stop=(cdi == 7)))
                                for cdi in range(8)],
                                reads=("wa0", "xlTg"), writes=(bnm,))

                    @blk.vector
                    def _(v):
                        for j in range(2):
                            oc = wave * 2 + j
                            bank = psE if j == 0 else psF
                            bnm = "psE" if j == 0 else "psF"
                            p.op(v, "vt", lambda bank=bank, oc=oc, grp=grp:
                                 v.tensor_scalar_add(
                                     out=Kt[:, oc * S + grp * 512:
                                            oc * S + (grp + 1) * 512],
                                     in0=bank[:], scalar1=bqc[:, 8 + oc: 9 + oc]),
                                 reads=(bnm, "bqc"), writes=("Kt",))

                for ti in range(4):
                    tile = grp * 4 + ti
                    @blk.tensor
                    def _(t):
                        for nb in range(2):
                            bank = psC if nb == 0 else psD
                            bnm = "psC" if nb == 0 else "psD"
                            emits = [(lambda bank=bank, nb=nb: t.matmul(
                                bank[:], ones_bf[:],
                                brow[0:1, nb * 512:(nb + 1) * 512],
                                start=True, stop=False))]
                            emits += [
                                (lambda bank=bank, nb=nb, cdi=cdi, ti=ti: t.matmul(
                                    bank[:],
                                    xlTg[:, cdi * 512 + ti * 128:
                                         cdi * 512 + (ti + 1) * 128],
                                    wa[1][:, cdi * D + nb * 512:
                                          cdi * D + (nb + 1) * 512],
                                    start=False, stop=(cdi == 7)))
                                for cdi in range(8)]
                            p.mm(t, emits,
                                 reads=("ones_bf", "brow", "xlTg", "wa1"),
                                 writes=(bnm,))

                    @blk.vector
                    def _(v):
                        for nb, bnm in enumerate(["psC", "psD"]):
                            bank = psC if nb == 0 else psD
                            p.op(v, "vt", lambda bank=bank, nb=nb, tile=tile:
                                 v.tensor_copy(
                                     out=Vt[:, tile * D + nb * 512:
                                            tile * D + (nb + 1) * 512],
                                     in_=bank[:]), reads=(bnm,), writes=("Vt",))

            @blk.gpsimd
            def _(g):
                p.dma(g, "gp", "gd", wa[0][:].rearrange("p (c d) -> p c d", c=8),
                      Wq_h.rearrange("(c p) d -> p c d", p=128), writes=("wa0",))

            for grp in range(NG):
                for ti in range(4):
                    gtok = toff + (grp * 4 + ti) * 128
                    par = ti % 2
                    xb, xn_ = xlg[0], "xlg0"
                    xsem = "x0" if par == 0 else "x1s"

                    @blk.sync
                    def _(sync):
                        p.dma(sync, "sync", xsem, xb[:],
                              xl2_h[gtok:gtok + 128, :], writes=(xn_,))

                    ptn = "ptA" if ti % 2 == 0 else "ptB"
                    pt = ptA if ti % 2 == 0 else ptB

                    @blk.tensor
                    def _(t):
                        p.mm(t, [
                            (lambda pt=pt, c=c, xb=xb: t.transpose(
                                pt[:, c * 128:(c + 1) * 128],
                                xb[:, c * 128:(c + 1) * 128],
                                C["ident_bf"][:])) for c in range(8)],
                            reads=(xn_, "s_ident_bf"), writes=(ptn,))

                    @blk.vector
                    def _(v):
                        p.op(v, "vt", lambda pt=pt, ti=ti: v.tensor_copy(
                            out=xlTg[:].rearrange("p (c x) -> p c x", c=8)[
                                :, :, ti * 128:(ti + 1) * 128],
                            in_=pt[:].rearrange("p (c x) -> p c x", c=8)),
                            reads=(ptn,), writes=("xlTg",))

                for wave in range(4):
                    @blk.tensor
                    def _(t):
                        for j in range(2):
                            oc = wave * 2 + j
                            bank = psE if j == 0 else psF
                            bnm = "psE" if j == 0 else "psF"
                            p.mm(t, [
                                (lambda bank=bank, oc=oc, cdi=cdi: t.matmul(
                                    bank[:],
                                    wa[0][:, cdi * D + oc * 128:
                                          cdi * D + (oc + 1) * 128],
                                    xlTg[:, cdi * 512:(cdi + 1) * 512],
                                    start=(cdi == 0), stop=(cdi == 7)))
                                for cdi in range(8)],
                                reads=("wa0", "xlTg"), writes=(bnm,))

                    @blk.vector
                    def _(v):
                        for j in range(2):
                            oc = wave * 2 + j
                            bank = psE if j == 0 else psF
                            bnm = "psE" if j == 0 else "psF"
                            p.op(v, "vt", lambda bank=bank, oc=oc: v.tensor_scalar_add(
                                out=Qg[:, (oc % 2) * 512:(oc % 2 + 1) * 512],
                                in0=bank[:], scalar1=bqc[:, oc:oc + 1]),
                                reads=(bnm, "bqc"), writes=("Qg",))

                    @blk.scalar
                    def _(s):
                        for j in range(2):
                            oc = wave * 2 + j
                            p.op(s, "at", lambda oc=oc: s.activation(
                                out=ksq[:], in_=Qg[:, (oc % 2) * 512:
                                                  (oc % 2 + 1) * 512],
                                func=AF.Square), reads=("Qg",), writes=("ksq",))

                            @blk.tensor
                            def _(t):
                                p.mm(t, [lambda oc=oc: t.matmul(
                                    psB[0:1, 0:512], onesc_bf[:], ksq[:],
                                    start=(oc == 0), stop=(oc == 7))],
                                    reads=("onesc_bf", "ksq"), writes=("psB",))

                    @blk.sync
                    def _(sync):
                        for j in range(2):
                            oc = wave * 2 + j
                            p.dma(sync, "sync", "st",
                                  qt_h[oc * 128:(oc + 1) * 128,
                                       grp * 512:(grp + 1) * 512],
                                  Qg[:, (oc % 2) * 512:(oc % 2 + 1) * 512],
                                  reads=("Qg",))

                @blk.vector
                def _(v):
                    p.op(v, "vt", lambda grp=grp: v.tensor_copy(
                        out=q2row[0:1, grp * 512:(grp + 1) * 512],
                        in_=psB[0:1, 0:512]), reads=("psB",), writes=("q2row",))

            for kb in range(NKB):
                for c in range(8):
                    @blk.scalar
                    def _(s):
                        p.op(s, "at", lambda c=c, kb=kb: s.activation(
                            out=ksq[:], in_=Kt[:, c * S + kb * 512:
                                              c * S + (kb + 1) * 512],
                            func=AF.Square), reads=("Kt",), writes=("ksq",))

                    @blk.tensor
                    def _(t):
                        p.mm(t, [lambda c=c: t.matmul(
                            psB[0:1, 0:512], onesc_bf[:], ksq[:],
                            start=(c == 0), stop=(c == 7))],
                            reads=("onesc_bf", "ksq"), writes=("psB",))

                @blk.vector
                def _(v):
                    p.op(v, "vt", lambda kb=kb: v.tensor_scalar_mul(
                        out=nk2h[0:1, kb * 512:(kb + 1) * 512],
                        in_=psB[0:1, 0:512], scalar1=-0.5)
                        if False else v.tensor_scalar_mul(
                        out=nk2h[0:1, kb * 512:(kb + 1) * 512],
                        in0=psB[0:1, 0:512], scalar1=-0.5),
                        reads=("psB",), writes=("nk2h",))

            @blk.gpsimd
            def _(g):
                p.dma(g, "gp", "gd", wa[0][:].rearrange("p (c d) -> p c d", c=8),
                      Wo_h.rearrange("(c p) d -> p c d", p=128), writes=("wa0",))

            for qb in range(NQB):
                gtok = toff + qb * 128

                @blk.sync
                def _(sync):
                    p.dma(sync, "sync", "x0",
                          Qg[:].rearrange("p (c x) -> p c x", c=8),
                          qt_h.rearrange("(c p) x -> p c x", p=128)[
                              :, :, qb * 128:(qb + 1) * 128],
                          writes=("Qg",))

                @blk.tensor
                def _(t):
                    p.mm(t, [lambda qb=qb: t.matmul(
                        psB[:, 128:129], q2row[0:1, qb * 128:(qb + 1) * 128],
                        onesc2[:], start=True, stop=True)],
                        reads=("q2row", "onesc2"), writes=("psB",))

                @blk.vector
                def _(v):
                    p.op(v, "vt", lambda: v.tensor_scalar_mul(
                        out=q2c[:, 0:1], in0=psB[:, 128:129], scalar1=-1.0),
                        reads=("psB",), writes=("q2c",))
                    p.op(v, "vt", lambda: v.memset(dcol[:], 0.0), writes=("dcol",))

                def emit_qk(kb):
                    par_b = kb % 2
                    qkb = psA if par_b == 0 else psC
                    qkn = "psA" if par_b == 0 else "psC"
                    vb_t, vb_n = (psB, "psB") if par_b == 0 else (psD, "psD")
                    Pb, Pn = Psb[par_b], f"PsbB{par_b}"

                    @blk.tensor
                    def _(t):
                        emits = [
                            (lambda qkb=qkb, kb=kb, cdi=cdi: t.matmul(
                                qkb[:],
                                Qg[:, cdi * 128:(cdi + 1) * 128],
                                Kt[:, cdi * S + kb * 512:cdi * S + (kb + 1) * 512],
                                start=(cdi == 0), stop=False))
                            for cdi in range(8)]
                        emits.append(lambda qkb=qkb, kb=kb: t.matmul(
                            qkb[:], ones_bf[:],
                            nk2h[0:1, kb * 512:(kb + 1) * 512],
                            start=False, stop=True))
                        p.mm(t, emits, reads=("Qg", "Kt", "ones_bf", "nk2h"),
                             writes=(qkn,))

                    @blk.scalar
                    def _(s):
                        p.op(s, "at", lambda qkb=qkb, vb_t=vb_t: s.activation(
                            out=vb_t[:], in_=qkb[:], func=AF.Exp,
                            scale=c_two[:], bias=q2c[:, 0:1]),
                            reads=(qkn, "c_two", "q2c"), writes=(vb_n,))
                        p.op(s, "at", lambda kb=kb, vb_t=vb_t, Pb=Pb: s.activation(
                            out=Pb[:], in_=vb_t[:], func=AF.Exp,
                            bias=c_m1[:],
                            accum_out=dcol[:, kb:kb + 1]),
                            reads=(vb_n, "c_m1"), writes=(Pn, "dcol"))

                def emit_ptpv(kb):
                    par_b = kb % 2
                    Pb, Pn = Psb[par_b], f"PsbB{par_b}"
                    PTb, PTn = PTs[par_b], f"PTs{par_b}"
                    ptb_t, ptb_n = (ptA, "ptA") if par_b == 0 else (ptB, "ptB")

                    @blk.tensor
                    def _(t):
                        p.mm(t, [
                            (lambda j=j, Pb=Pb, ptb_t=ptb_t: t.transpose(
                                ptb_t[:, j * 128:(j + 1) * 128],
                                Pb[:, j * 128:(j + 1) * 128],
                                C["ident_bf"][:])) for j in range(4)],
                            reads=(Pn, "s_ident_bf"), writes=(ptb_n,))

                    @blk.vector
                    def _(v):
                        p.op(v, "vt", lambda PTb=PTb, ptb_t=ptb_t: v.tensor_copy(
                            out=PTb[:], in_=ptb_t[:, 0:512]),
                            reads=(ptb_n,), writes=(PTn,))

                    @blk.tensor
                    def _(t):
                        for nb in range(2):
                            bank = psE if nb == 0 else psF
                            bnm = "psE" if nb == 0 else "psF"
                            p.mm(t, [
                                (lambda bank=bank, nb=nb, kb=kb, j=j, PTb=PTb:
                                 t.matmul(
                                    bank[:],
                                    PTb[:, j * 128:(j + 1) * 128],
                                    Vt[:, (kb * 4 + j) * D + nb * 512:
                                       (kb * 4 + j) * D + (nb + 1) * 512],
                                    start=(kb == 0 and j == 0),
                                    stop=(kb == NKB - 1 and j == 3)))
                                for j in range(4)],
                                reads=(PTn, "Vt"), writes=(bnm,))

                for kb in range(NKB + 1):
                    if kb < NKB:
                        emit_qk(kb)
                    if kb > 0:
                        emit_ptpv(kb - 1)

                @blk.vector
                def _(v):
                    p.op(v, "vt", lambda: v.reduce_sum(
                        out=rcpc[:], in_=dcol[:], axis=AX.X),
                        reads=("dcol",), writes=("rcpc",))
                    p.op(v, "vt", lambda: v.reciprocal(out=rcpc[:], in_=rcpc[:]),
                         reads=("rcpc",), writes=("rcpc",))
                    for nb, bnm in enumerate(["psE", "psF"]):
                        bank = psE if nb == 0 else psF
                        p.op(v, "vt", lambda bank=bank, nb=nb: v.tensor_scalar_mul(
                            out=attn[:, nb * 512:(nb + 1) * 512], in0=bank[:],
                            scalar1=rcpc[:]), reads=(bnm, "rcpc"), writes=("attn4",))

                @blk.tensor
                def _(t):
                    p.mm(t, [
                        (lambda c=c: t.transpose(
                            ptB[:, c * 128:(c + 1) * 128],
                            attn[:, c * 128:(c + 1) * 128],
                            C["ident_bf"][:])) for c in range(8)],
                        reads=("attn4", "s_ident_bf"), writes=("ptB",))

                @blk.vector
                def _(v):
                    p.op(v, "vt", lambda: v.tensor_copy(
                        out=attnT[:], in_=ptB[:]), reads=("ptB",), writes=("attnT4",))

                @blk.sync
                def _(sync):
                    p.dma(sync, "sync", "x1s", x3t[:], x2_h[gtok:gtok + 128, :],
                          writes=("x3t",))

                @blk.tensor
                def _(t):
                    for nb in range(2):
                        bank = psA if nb == 0 else psC
                        bnm = "psA" if nb == 0 else "psC"
                        emits = [(lambda bank=bank, nb=nb: t.matmul(
                            bank[:], ones_bf[:],
                            brow[0:1, D + nb * 512: D + (nb + 1) * 512],
                            start=True, stop=False))]
                        emits += [
                            (lambda bank=bank, nb=nb, cdi=cdi: t.matmul(
                                bank[:],
                                attnT[:, cdi * 128:(cdi + 1) * 128],
                                wa[0][:, cdi * D + nb * 512:
                                      cdi * D + (nb + 1) * 512],
                                start=False, stop=(cdi == 7)))
                            for cdi in range(8)]
                        p.mm(t, emits, reads=("ones_bf", "brow", "attnT4", "wa0"),
                             writes=(bnm,))

                @blk.vector
                def _(v):
                    for nb, bnm in enumerate(["psA", "psC"]):
                        bank = psA if nb == 0 else psC
                        p.op(v, "vt", lambda bank=bank, nb=nb: v.tensor_add(
                            out=x3t[:, nb * 512:(nb + 1) * 512],
                            in0=x3t[:, nb * 512:(nb + 1) * 512], in1=bank[:]),
                            reads=("x3t", bnm), writes=("x3t",))

                @blk.sync
                def _(sync):
                    p.dma(sync, "sync", "st", x3_h[gtok:gtok + 128, :], x3t[:],
                          reads=("x3t",))

        @blk.gpsimd
        def _(g):
            for s in ["st", "sd", "gd", "id", "x0", "x1s"]:
                g.wait_ge(p.sem[s], p.n[s])
        _p_sbuf_pop(p, "Kt")
    return p


def launch4a_inputs(core, x1, wg, y_full, full):
    d = {"x1": x1, "wg": wg, "y_all": y_full,
         "Wq": full["L1_attn_Wq"], "Wk": full["L1_attn_Wk"],
         "Wv": full["L1_attn_Wv"], "Wo": full["L1_attn_Wo"],
         "bq": full["L1_attn_bq"].reshape(1, D),
         "bk": full["L1_attn_bk"].reshape(1, D),
         "bv": full["L1_attn_bv"].reshape(1, D),
         "bo": full["L1_attn_bo"].reshape(1, D)}
    d.update(const_inputs(L4A_CONSTS))
    return d



L4B_CONSTS = ["ident_bf"]
NIC1 = I1 // 128
FFN_NB = [(0, 512), (512, 512), (1024, 512), (1536, 512), (2048, 512), (2560, 256)]


QCW = 1024


def _emit_quant(p, blk, q, src, src_n, dst, dst_n, W):
    NB = W // 16
    n = q["_n"]
    sg, xn, g5, g7 = q["sg"], q["xn"], q["g5"], q["g7"]
    bm, hs, rcp = q["bm"], q["hs"], q["rcp"]
    i32 = q["i32"]
    i32_n = q["_i32n"]

    @blk.vector
    def _(v):
        p.op(v, "vt", lambda: v.tensor_reduce(
            out=bm[:, 0:NB], in_=src.rearrange("p (b x) -> p b x", x=16),
            axis=AX.X, op=ALU.max, apply_absolute_value=True),
            reads=(src_n,), writes=(n + "bm",))
        p.op(v, "vt", lambda: v.tensor_scalar(
            out=hs[:, 0:NB], in0=bm[:, 0:NB], scalar1=1.0 / 12.0,
            scalar2=5e-13, op0=ALU.mult, op1=ALU.max),
            reads=(n + "bm",), writes=(n + "hs",))
        p.op(v, "vt", lambda: v.reciprocal(
            out=rcp[:, 0:NB], in_=hs[:, 0:NB]),
            reads=(n + "hs",), writes=(n + "rcp",))

    @blk.scalar
    def _(s):
        p.op(s, "at", lambda: s.sign(out=sg[:, 0:W], in_=src),
             reads=(src_n,), writes=(n + "sg",))
        p.op(s, "at", lambda: s.activation(out=src, in_=src, func=AF.Abs),
             reads=(src_n,), writes=(src_n,))

    @blk.gpsimd
    def _(g):
        p.op(g, "gt", lambda: g.tensor_tensor(
            out=xn[:, 0:W], in0=src,
            in1=rcp[:, 0:NB].unsqueeze(2).to_broadcast([128, NB, 16]),
            op=ALU.mult), reads=(src_n, n + "rcp"), writes=(n + "xn",))
        p.op(g, "gt", lambda: g.tensor_single_scalar(
            out=g5[:, 0:W], in_=xn[:, 0:W], scalar=5.0, op=ALU.is_gt),
            reads=(n + "xn",), writes=(n + "g5",))
        p.op(g, "gt", lambda: g.tensor_single_scalar(
            out=g7[:, 0:W], in_=xn[:, 0:W], scalar=7.0, op=ALU.is_gt),
            reads=(n + "xn",), writes=(n + "g7",))

    @blk.vector
    def _(v):
        p.op(v, "vt", lambda: v.tensor_add(
            out=g5[:, 0:W], in0=g5[:, 0:W], in1=g7[:, 0:W]),
            reads=(n + "g5", n + "g7"), writes=(n + "g5",))

    @blk.gpsimd
    def _(g):
        p.op(g, "gt", lambda: g.tensor_single_scalar(
            out=g7[:, 0:W], in_=xn[:, 0:W], scalar=10.0, op=ALU.is_gt),
            reads=(n + "xn",), writes=(n + "g7",))

    @blk.vector
    def _(v):
        p.op(v, "vt", lambda: v.scalar_tensor_tensor(
            out=g5[:, 0:W], in0=g7[:, 0:W], scalar=2.0, in1=g5[:, 0:W],
            op0=ALU.mult, op1=ALU.add),
            reads=(n + "g5", n + "g7"), writes=(n + "g5",))

    @blk.gpsimd
    def _(g):
        p.op(g, "gt", lambda: g.tensor_scalar(
            out=xn[:, 0:W], in0=xn[:, 0:W], scalar1=1.0, scalar2=4.49,
            op0=ALU.mult, op1=ALU.min), reads=(n + "xn",), writes=(n + "xn",))
        p.op(g, "gt", lambda: g.tensor_tensor(
            out=sg[:, 0:W], in0=sg[:, 0:W],
            in1=hs[:, 0:NB].unsqueeze(2).to_broadcast([128, NB, 16]),
            op=ALU.mult), reads=(n + "sg", n + "hs"), writes=(n + "sg",))

    @blk.scalar
    def _(s):
        p.op(s, "at", lambda: s.activation(
            out=i32[:, 0:W], in_=xn[:, 0:W], func=AF.Copy),
            reads=(n + "xn",), writes=(i32_n,))
        p.op(s, "at", lambda: s.activation(
            out=g7[:, 0:W], in_=i32[:, 0:W], func=AF.Copy),
            reads=(i32_n,), writes=(n + "g7",))

    @blk.vector
    def _(v):
        p.op(v, "vt", lambda: v.scalar_tensor_tensor(
            out=g7[:, 0:W], in0=g5[:, 0:W], scalar=2.0, in1=g7[:, 0:W],
            op0=ALU.mult, op1=ALU.add),
            reads=(n + "g5", n + "g7"), writes=(n + "g7",))
        p.op(v, "vt", lambda: v.tensor_tensor(
            out=dst, in0=g7[:, 0:W], in1=sg[:, 0:W], op=ALU.mult),
            reads=(n + "g7", n + "sg"), writes=(dst_n,))


def _emit_quant_chunks(p, blk, qs, src_t, src_n, dst_t, dst_n, W):
    off = 0
    while off < W:
        cw = min(QCW, W - off)
        q = qs[qs[-1] % 2]
        qs[-1] += 1
        _emit_quant(p, blk, q, src_t[:, off:off + cw], src_n,
                    dst_t[:, off:off + cw], dst_n, cw)
        off += cw


def _push_quant_scratch(p):
    i32 = _p_sbuf_push(p, "qi32", [128, QCW], mybir.dt.int32)
    sets = []
    for pref in ("q0_", "q1_"):
        q = {"_n": pref, "i32": i32, "_i32n": "qi32"}
        for nm in ["sg", "xn", "g5", "g7"]:
            q[nm] = _p_sbuf_push(p, pref + nm, [128, QCW])
        for nm in ["bm", "hs", "rcp"]:
            q[nm] = _p_sbuf_push(p, pref + nm, [128, QCW // 16])
        sets.append(q)
    return sets + [0]


def build_launch4b():
    p = P("l4b")
    nc = p.nc
    x3_h = nc.dram_tensor("x3", [TPC, D], F32, kind="ExternalInput").ap()
    fg_h = nc.dram_tensor("f_gate", [I1, D], F32, kind="ExternalInput").ap()
    fu_h = nc.dram_tensor("f_up", [I1, D], F32, kind="ExternalInput").ap()
    fd_h = nc.dram_tensor("f_down", [D, I1], F32, kind="ExternalInput").ap()
    out_h = nc.dram_tensor("out", [TPC, D], F32, kind="ExternalOutput").ap()
    hq_h = nc.dram_tensor("hqsp", [TPC, I1], BF16).ap()

    hbs, C = _declare_consts(p, L4B_CONSTS)
    with p.ctx:
        for s in ["sd", "gd", "x0", "x1s", "pt", "vt", "at", "st", "gt"]:
            p.semaphore(s)
        p.DMA_SEMS = frozenset(["sd", "gd", "st", "x0", "x1s"])
        c_eps = p.sbuf("c_eps", [128, 1])
        c_invD = p.sbuf("c_invD", [128, 1])
        c_mhalf = p.sbuf("c_mhalf", [128, 1])
        nmean = p.sbuf("nmean", [128, 2])
        rstd = p.sbuf("rstd", [128, 2])

        ptA = p.psum("ptA", [128, 1024], BF16)
        ptB = p.psum("ptB", [128, 1024], BF16)
        psA = p.psum("psA", [128, 512])
        psB = p.psum("psB", [128, 512])
        psC = p.psum("psC", [128, 512])
        psD = p.psum("psD", [128, 512])

        blk = p.ctx.enter_context(nc.Block())
        _load_consts(p, blk, hbs, C)

        @blk.vector
        def _(v):
            p.op(v, "vt", lambda: v.memset(c_eps[:], LN_EPS), writes=("c_eps",))
            p.op(v, "vt", lambda: v.memset(c_invD[:], 1.0 / D), writes=("c_invD",))
            p.op(v, "vt", lambda: v.memset(c_mhalf[:], -0.5), writes=("c_mhalf",))

        qs = _push_quant_scratch(p)
        wgT = _p_sbuf_push(p, "wgT", [128, 8 * I1], BF16)
        wuT = _p_sbuf_push(p, "wuT", [128, 8 * I1], BF16)
        wrow = [_p_sbuf_push(p, f"wrow{i}", [128, I1]) for i in range(2)]
        wqb2p = [_p_sbuf_push(p, f"wqb{i}", [128, I1], BF16) for i in range(2)]

        def prep_gate_up(w_h, dstT, dst_n):
            for r in range(NIC1):
                par = r % 2
                wb, wn = wrow[par], f"wrow{par}"
                xsem = "x0" if par == 0 else "x1s"

                @blk.sync
                def _(sync):
                    p.dma(sync, "sync", xsem, wb[:, 0:D],
                          w_h[r * 128:(r + 1) * 128, :], writes=(wn,))

                wqb, wqn = wqb2p[par], f"wqb{par}"
                _emit_quant_chunks(p, blk, qs, wb[:, 0:D], wn, wqb[:, 0:D], wqn, D)

                @blk.tensor
                def _(t):
                    p.mm(t, [
                        (lambda c=c, wqb=wqb: t.transpose(
                            ptA[:, c * 128:(c + 1) * 128],
                            wqb[:, c * 128:(c + 1) * 128],
                            C["ident_bf"][:])) for c in range(8)],
                        reads=(wqn, "s_ident_bf"), writes=("ptA",))

                @blk.vector
                def _(v):
                    p.op(v, "vt", lambda r=r: v.tensor_copy(
                        out=dstT[:].rearrange("p (c x) -> p c x", c=8)[
                            :, :, r * 128:(r + 1) * 128],
                        in_=ptA[:].rearrange("p (c x) -> p c x", c=8)),
                        reads=("ptA",), writes=(dst_n,))

        prep_gate_up(fg_h, wgT, "wgT")
        prep_gate_up(fu_h, wuT, "wuT")

        _p_sbuf_pop(p, "wrow0")

        x3t = [_p_sbuf_push(p, f"x3t{i}", [128, D]) for i in range(2)]
        xln3p = [_p_sbuf_push(p, f"xln3{i}", [128, D]) for i in range(2)]
        stats = _p_sbuf_push(p, "stats", [128, 2 * NT])
        xcb = _p_sbuf_push(p, "xcb", [128, D])
        sqs = _p_sbuf_push(p, "sqs", [128, D])
        for tile in range(NT):
            gtok = tile * 128
            par = tile % 2
            xb, xn_ = x3t[par], f"x3t{par}"
            xsem = "x0" if par == 0 else "x1s"

            @blk.sync
            def _(sync):
                p.dma(sync, "sync", xsem, xb[:], x3_h[gtok:gtok + 128, :],
                      writes=(xn_,))

            @blk.vector
            def _(v):
                p.op(v, "vt", lambda xb=xb, tile=tile: v.reduce_sum(
                    out=stats[:, 2 * tile:2 * tile + 1], in_=xb[:], axis=AX.X,
                    negate=True), reads=(xn_,), writes=("stats",))
                p.op(v, "vt", lambda tile=tile: v.tensor_scalar_mul(
                    out=stats[:, 2 * tile:2 * tile + 1],
                    in0=stats[:, 2 * tile:2 * tile + 1], scalar1=1.0 / D),
                    reads=("stats",), writes=("stats",))
                p.op(v, "vt", lambda xb=xb, tile=tile: v.tensor_scalar_add(
                    out=xcb[:], in0=xb[:],
                    scalar1=stats[:, 2 * tile:2 * tile + 1]),
                    reads=(xn_, "stats"), writes=("xcb",))

            @blk.scalar
            def _(s):
                p.op(s, "at", lambda tile=tile: s.activation(
                    out=sqs[:], in_=xcb[:], func=AF.Square,
                    accum_out=stats[:, 2 * tile + 1:2 * tile + 2]),
                    reads=("xcb",), writes=("sqs", "stats"))
                p.op(s, "at", lambda tile=tile: s.activation(
                    out=stats[:, 2 * tile + 1:2 * tile + 2],
                    in_=stats[:, 2 * tile + 1:2 * tile + 2], func=AF.Ln,
                    scale=c_invD[:], bias=c_eps[:]),
                    reads=("stats", "c_invD", "c_eps"), writes=("stats",))
                p.op(s, "at", lambda tile=tile: s.activation(
                    out=stats[:, 2 * tile + 1:2 * tile + 2],
                    in_=stats[:, 2 * tile + 1:2 * tile + 2], func=AF.Exp,
                    scale=c_mhalf[:]),
                    reads=("stats", "c_mhalf"), writes=("stats",))

        _p_sbuf_pop(p, "xcb")

        xqp = [_p_sbuf_push(p, f"xq{i}", [128, D], BF16) for i in range(2)]
        xqTp = [_p_sbuf_push(p, f"xqT{i}", [128, 8 * 128], BF16) for i in range(2)]
        gsp = [_p_sbuf_push(p, f"gs{i}", [128, 512]) for i in range(2)]
        hbufp = [_p_sbuf_push(p, f"hbuf{i}", [128, I1]) for i in range(2)]
        hqp = [_p_sbuf_push(p, f"hq{i}", [128, I1], BF16) for i in range(2)]

        for tile in range(NT):
            gtok = tile * 128
            par = tile % 2
            xb, xn_ = x3t[par], f"x3t{par}"
            xsem = "x0" if par == 0 else "x1s"
            xln3 = xln3p[par]
            xq = xqp[par]
            xqT = xqTp[par]
            gs = gsp[par]
            hbuf = hbufp[par]
            hq = hqp[par]
            nm_xln3, nm_xq, nm_xqT = f"xln3{par}", f"xq{par}", f"xqT{par}"
            nm_gs, nm_hbuf, nm_hq = f"gs{par}", f"hbuf{par}", f"hq{par}"

            @blk.sync
            def _(sync):
                p.dma(sync, "sync", xsem, xb[:], x3_h[gtok:gtok + 128, :],
                      writes=(xn_,))

            @blk.vector
            def _(v):
                p.op(v, "vt", lambda xb=xb, tile=tile, xln3=xln3: v.tensor_scalar(
                    out=xln3[:], in0=xb[:],
                    scalar1=stats[:, 2 * tile:2 * tile + 1],
                    scalar2=stats[:, 2 * tile + 1:2 * tile + 2],
                    op0=ALU.add, op1=ALU.mult),
                    reads=(xn_, "stats"), writes=(nm_xln3,))

            _emit_quant_chunks(p, blk, qs, xln3[:], nm_xln3, xq[:], nm_xq, D)

            ptx = ptA if par == 0 else ptB
            ptxn = "ptA" if par == 0 else "ptB"

            @blk.tensor
            def _(t):
                p.mm(t, [
                    (lambda c=c, xq=xq, ptx=ptx: t.transpose(
                        ptx[:, c * 128:(c + 1) * 128],
                        xq[:, c * 128:(c + 1) * 128],
                        C["ident_bf"][:])) for c in range(8)],
                    reads=(nm_xq, "s_ident_bf"), writes=(ptxn,))

            @blk.vector
            def _(v):
                p.op(v, "vt", lambda xqT=xqT, ptx=ptx: v.tensor_copy(
                    out=xqT[:], in_=ptx[:]), reads=(ptxn,), writes=(nm_xqT,))

            for inb, (noff, nw) in enumerate(FFN_NB):
                bankG = psA if inb % 2 == 0 else psC
                bG = "psA" if inb % 2 == 0 else "psC"
                bankU = psB if inb % 2 == 0 else psD
                bU = "psB" if inb % 2 == 0 else "psD"

                @blk.tensor
                def _(t):
                    p.mm(t, [
                        (lambda bankG=bankG, noff=noff, nw=nw, c=c, xqT=xqT:
                         t.matmul(
                            bankG[:, 0:nw],
                            xqT[:, c * 128:(c + 1) * 128],
                            wgT[:, c * I1 + noff: c * I1 + noff + nw],
                            start=(c == 0), stop=(c == 7))) for c in range(8)],
                        reads=(nm_xqT, "wgT"), writes=(bG,))
                    p.mm(t, [
                        (lambda bankU=bankU, noff=noff, nw=nw, c=c, xqT=xqT:
                         t.matmul(
                            bankU[:, 0:nw],
                            xqT[:, c * 128:(c + 1) * 128],
                            wuT[:, c * I1 + noff: c * I1 + noff + nw],
                            start=(c == 0), stop=(c == 7))) for c in range(8)],
                        reads=(nm_xqT, "wuT"), writes=(bU,))

                @blk.scalar
                def _(s):
                    p.op(s, "at", lambda bankG=bankG, nw=nw, gs=gs: s.activation(
                        out=gs[:, 0:nw], in_=bankG[:, 0:nw], func=AF.Silu),
                        reads=(bG,), writes=(nm_gs,))

                @blk.vector
                def _(v):
                    p.op(v, "vt", lambda bankU=bankU, noff=noff, nw=nw, gs=gs,
                         hbuf=hbuf: v.tensor_tensor(
                             out=hbuf[:, noff:noff + nw], in0=gs[:, 0:nw],
                             in1=bankU[:, 0:nw], op=ALU.mult),
                         reads=(nm_gs, bU), writes=(nm_hbuf,))

            _emit_quant_chunks(p, blk, qs, hbuf[:], nm_hbuf, hq[:], nm_hq, I1)

            @blk.sync
            def _(sync):
                p.dma(sync, "sync", "st", hq_h[gtok:gtok + 128, :], hq[:],
                      reads=(nm_hq,))

        _p_sbuf_pop(p, "x3t0")
        _p_sbuf_pop(p, "wgT")

        wdT2 = _p_sbuf_push(p, "wdT2", [128, NIC1 * D], BF16)
        hq2 = [_p_sbuf_push(p, f"hq2{i}", [128, I1], BF16) for i in range(2)]
        hqT = _p_sbuf_push(p, "hqT", [128, NIC1 * 128], BF16)
        x3r = _p_sbuf_push(p, "x3r", [128, D])
        outt = _p_sbuf_push(p, "outt", [128, D])

        wrow2 = _p_sbuf_push(p, "wrow2", [128, I1])
        wqb2 = _p_sbuf_push(p, "wqb2", [128, I1], BF16)
        for r in range(8):
            @blk.sync
            def _(sync):
                p.dma(sync, "sync", "x0", wrow2[:],
                      fd_h[r * 128:(r + 1) * 128, :], writes=("wrow2",))

            _emit_quant_chunks(p, blk, qs, wrow2[:], "wrow2", wqb2[:], "wqb2", I1)

            for bt in range(3):
                nt_ = min(8, NIC1 - bt * 8)

                @blk.tensor
                def _(t):
                    p.mm(t, [
                        (lambda bt=bt, j=j: t.transpose(
                            ptA[:, j * 128:(j + 1) * 128],
                            wqb2[:, (bt * 8 + j) * 128:(bt * 8 + j + 1) * 128],
                            C["ident_bf"][:])) for j in range(nt_)],
                        reads=("wqb2", "s_ident_bf"), writes=("ptA",))

                @blk.vector
                def _(v):
                    p.op(v, "vt", lambda r=r, bt=bt, nt_=nt_: v.tensor_copy(
                        out=wdT2[:].rearrange("p (c x) -> p c x", c=NIC1)[
                            :, bt * 8: bt * 8 + nt_, r * 128:(r + 1) * 128],
                        in_=ptA[:, 0:nt_ * 128].rearrange(
                            "p (c x) -> p c x", x=128)),
                        reads=("ptA",), writes=("wdT2",))

        for tile in range(NT):
            gtok = tile * 128
            par = tile % 2
            hb, hn = hq2[par], f"hq2{par}"
            xsem = "x0" if par == 0 else "x1s"

            @blk.sync
            def _(sync):
                p.dma(sync, "sync", xsem, hb[:], hq_h[gtok:gtok + 128, :],
                      writes=(hn,))
                p.dma(sync, "sync", xsem, x3r[:], x3_h[gtok:gtok + 128, :],
                      writes=("x3r",))

            for bt in range(3):
                nt_ = min(8, NIC1 - bt * 8)
                pt = ptA if bt % 2 == 0 else ptB
                ptn = "ptA" if bt % 2 == 0 else "ptB"

                @blk.tensor
                def _(t):
                    p.mm(t, [
                        (lambda pt=pt, bt=bt, j=j, hb=hb: t.transpose(
                            pt[:, j * 128:(j + 1) * 128],
                            hb[:, (bt * 8 + j) * 128:(bt * 8 + j + 1) * 128],
                            C["ident_bf"][:])) for j in range(nt_)],
                        reads=(hn, "s_ident_bf"), writes=(ptn,))

                @blk.vector
                def _(v):
                    p.op(v, "vt", lambda pt=pt, bt=bt, nt_=nt_: v.tensor_copy(
                        out=hqT[:, bt * 8 * 128: (bt * 8 + nt_) * 128],
                        in_=pt[:, 0:nt_ * 128]), reads=(ptn,), writes=("hqT",))

            @blk.tensor
            def _(t):
                for nb in range(2):
                    bank = psA if nb == 0 else psB
                    bnm = "psA" if nb == 0 else "psB"
                    p.mm(t, [
                        (lambda bank=bank, nb=nb, ci=ci: t.matmul(
                            bank[:],
                            hqT[:, ci * 128:(ci + 1) * 128],
                            wdT2[:, ci * D + nb * 512: ci * D + (nb + 1) * 512],
                            start=(ci == 0), stop=(ci == NIC1 - 1)))
                        for ci in range(NIC1)],
                        reads=("hqT", "wdT2"), writes=(bnm,))

            @blk.vector
            def _(v):
                for nb, bnm in enumerate(["psA", "psB"]):
                    bank = psA if nb == 0 else psB
                    p.op(v, "vt", lambda bank=bank, nb=nb: v.tensor_add(
                        out=outt[:, nb * 512:(nb + 1) * 512],
                        in0=x3r[:, nb * 512:(nb + 1) * 512], in1=bank[:]),
                        reads=("x3r", bnm), writes=("outt",))

            @blk.sync
            def _(sync):
                p.dma(sync, "sync", "st", out_h[gtok:gtok + 128, :], outt[:],
                      reads=("outt",))

        @blk.gpsimd
        def _(g):
            for s in ["st", "sd", "gd", "x0", "x1s"]:
                g.wait_ge(p.sem[s], p.n[s])
        _p_sbuf_pop(p, "wdT2")
        _p_sbuf_pop(p, "qi32")
    return p


def launch4b_inputs(core, x3, full):
    d = {"x3": x3, "f_gate": full["L1_ffn_w_gate"], "f_up": full["L1_ffn_w_up"],
         "f_down": full["L1_ffn_w_down"]}
    d.update(const_inputs(L4B_CONSTS))
    return d


# version 12
# speedup vs baseline: 1.5644x; 1.5644x over previous
import math
from contextlib import ExitStack

import numpy as np
import ml_dtypes
import concourse.bass as bass
import concourse.mybir as mybir
from concourse.bass_utils import run_bass_kernel_spmd
from concourse.bass_core_rust import IndirectOffsetOnAxis

F32 = mybir.dt.float32
BF16 = mybir.dt.bfloat16
U32 = mybir.dt.uint32
AF = mybir.ActivationFunctionType
ALU = mybir.AluOpType
AX = mybir.AxisListType

NCORES = 8
B, S, D = 16, 4096, 1024
TPC = B * S // NCORES
NT = TPC // 128
WIN, NW = 256, TPC // 256
NH, HD = 16, 64
NE, TOPK, CAP, EI = 64, 8, 256, 768
EPC = NE // NCORES
I1 = 2816
LN_EPS = 1e-5
BIGIDX = float(1 << 24)


def _host_consts():
    c = {}
    c["ident_f32"] = np.eye(128, dtype=np.float32)
    c["ident_bf"] = np.eye(128, dtype=np.float32)
    c["tstrictT"] = np.triu(np.ones((128, 128), np.float32), 1)
    c["ones_row"] = np.ones((1, 128), np.float32)
    c["ones_col"] = np.ones((128, 1), np.float32)
    c["iotaM"] = np.tile((np.arange(NE, dtype=np.float32) - BIGIDX)[None, :],
                         (128, TOPK)).reshape(128, TOPK * NE)
    c["iotaE"] = np.tile(np.arange(NE, dtype=np.float32), (128, TOPK))
    kk = (np.arange(TOPK)[None, :] < np.arange(TOPK)[:, None]).astype(np.float32)
    c["kkmask"] = np.tile(kk.reshape(1, TOPK * TOPK), (128, 1))
    return c

_CONST_DTYPES = {"ident_f32": F32, "ident_bf": BF16, "tstrictT": F32,
                 "ones_row": F32, "ones_col": F32, "iotaM": F32,
                 "iotaE": F32, "kkmask": F32}


class Buf:
    __slots__ = ("name", "w", "r")

    def __init__(self, name):
        self.name = name
        self.w = None
        self.r = []


class P:

    def __init__(self, name):
        self.nc = bass.Bass("TRN2", target_bir_lowering=False, num_devices=NCORES)
        self.ctx = ExitStack()
        self.name = name
        self.n = {}
        self.sem = {}
        self.bufs = {}
        self.seen = {}

    def semaphore(self, name):
        self.sem[name] = self.ctx.enter_context(self.nc.semaphore(name))
        self.n[name] = 0
        return self.sem[name]

    def sbuf(self, name, shape, dt=F32):
        t = self.ctx.enter_context(self.nc.sbuf_tensor(name, shape, dt))
        self.bufs[name] = Buf(name)
        return t

    def psum(self, name, shape, dt=F32):
        t = self.ctx.enter_context(self.nc.psum_tensor(name, shape, dt))
        self.bufs[name] = Buf(name)
        return t

    def buf(self, name):
        if name not in self.bufs:
            self.bufs[name] = Buf(name)
        return self.bufs[name]

    def _wait(self, eng, ekey, sem_name, val):
        if val <= 0:
            return
        seen = self.seen.setdefault(ekey, {})
        if seen.get(sem_name, -1) >= val:
            return
        eng.wait_ge(self.sem[sem_name], val)
        seen[sem_name] = val

    DMA_SEMS = frozenset(["sd", "gd", "st", "x0", "x1", "ld0", "ld1", "ld2",
                          "id", "zs"])

    def _dep_waits(self, eng, ekey, reads, writes, is_pe=False):
        todo = {}

        def add(s, v):
            if s in self.DMA_SEMS:
                v = self.n[s]
            todo[s] = max(todo.get(s, 0), v)

        for bn in reads:
            b = self.buf(bn)
            if b.w is not None:
                s, v = b.w
                if is_pe and s == "pt":
                    continue
                add(s, v)
        for bn in writes:
            b = self.buf(bn)
            if b.w is not None:
                s, v = b.w
                if not (is_pe and s == "pt"):
                    add(s, v)
            for s, v in b.r:
                if is_pe and s == "pt":
                    continue
                add(s, v)
        for s, v in sorted(todo.items()):
            self._wait(eng, ekey, s, v)

    def _record(self, reads, writes, stamp):
        for bn in writes:
            b = self.buf(bn)
            b.w = stamp
            b.r = []
        for bn in reads:
            self.buf(bn).r.append(stamp)

    def op(self, eng, tok, emit, reads=(), writes=()):
        self._dep_waits(eng, tok, reads, writes)
        ins = emit()
        ins.then_inc(self.sem[tok], 1)
        self.n[tok] += 1
        self._record(reads, writes, (tok, self.n[tok]))
        return self.n[tok]

    def mm(self, eng, emit_list, reads=(), writes=()):
        self._dep_waits(eng, "pe", reads, writes, is_pe=True)
        ins = None
        for emit in emit_list:
            ins = emit()
        ins.then_inc(self.sem["pt"], 1)
        self.n["pt"] += 1
        self._record(reads, writes, ("pt", self.n["pt"]))
        return self.n["pt"]

    def dma(self, eng, ekey, sem_name, out, in_, reads=(), writes=(), **kw):
        self._dep_waits(eng, ekey, reads, writes)
        ins = eng.dma_start(out=out, in_=in_, **kw)
        ins.then_inc(self.sem[sem_name], 16)
        self.n[sem_name] += 16
        self._record(reads, writes, (sem_name, self.n[sem_name]))
        return self.n[sem_name]

    def idma(self, eng, ekey, sem_name, reads=(), writes=(), **kw):
        self._dep_waits(eng, ekey, reads, writes)
        ins = eng.indirect_dma_start(**kw)
        ins.then_inc(self.sem[sem_name], 16)
        self.n[sem_name] += 16
        self._record(reads, writes, (sem_name, self.n[sem_name]))
        return self.n[sem_name]


def _declare_consts(p, consts_needed):
    hbs, tiles = {}, {}
    for nm in consts_needed:
        shape = list(_host_consts()[nm].shape)
        hbs[nm] = p.nc.dram_tensor("c_" + nm, shape, F32, kind="ExternalInput").ap()
        tiles[nm] = p.sbuf("s_" + nm, shape, _CONST_DTYPES[nm])
    return hbs, tiles


def _load_consts(p, blk, hbs, tiles):
    @blk.sync
    def _(sync):
        for nm, hb in hbs.items():
            if _CONST_DTYPES[nm] == F32:
                p.dma(sync, "sync", "sd", tiles[nm][:], hb, writes=("s_" + nm,))

    @blk.gpsimd
    def _(g):
        for nm, hb in hbs.items():
            if _CONST_DTYPES[nm] != F32:
                p.dma(g, "gp", "gd", tiles[nm][:], hb, writes=("s_" + nm,))


def const_inputs(consts_needed):
    hc = _host_consts()
    return {"c_" + nm: hc[nm] for nm in consts_needed}



L1_CONSTS = ["ident_f32", "ident_bf", "tstrictT", "ones_row", "ones_col",
             "iotaM", "iotaE", "kkmask"]


def build_launch1(nwin=NW, lvl=99, hlvl=99):
    p = P("l1")
    nc = p.nc
    x_h = nc.dram_tensor("x", [TPC, D], F32, kind="ExternalInput").ap()
    qw_h = nc.dram_tensor("q_w", [D, D], F32, kind="ExternalInput").ap()
    kvw_h = nc.dram_tensor("kv_w", [2 * HD, D], F32, kind="ExternalInput").ap()
    pw_h = nc.dram_tensor("proj_w", [D, D], F32, kind="ExternalInput").ap()
    pb_h = nc.dram_tensor("proj_b", [1, D], F32, kind="ExternalInput").ap()
    gw_h = nc.dram_tensor("gate_w", [NE, D], F32, kind="ExternalInput").ap()

    x1_h = nc.dram_tensor("x1", [TPC, D], F32, kind="ExternalOutput").ap()
    xln1_h = nc.dram_tensor("xln1b", [TPC, D], BF16, kind="ExternalOutput").ap()
    sdat_h = nc.dram_tensor("slotdat", [TPC, 3 * TOPK], F32, kind="ExternalOutput").ap()
    cnt_h = nc.dram_tensor("cnt", [1, NE], F32, kind="ExternalOutput").ap()

    hbs, C = _declare_consts(p, L1_CONSTS)

    with p.ctx:
        for s in ["sd", "gd", "x0", "x1", "pt", "vt", "at", "st"]:
            p.semaphore(s)
        qw_bf = p.sbuf("qw_bf", [128, 8 * D], BF16)
        qwT = p.sbuf("qwT", [128, 8 * D], BF16)
        pwT = p.sbuf("pwT", [128, 8 * D], BF16)
        kv_bf = p.sbuf("kv_bf", [128, D], BF16)
        kvwT = p.sbuf("kvwT", [128, 8 * 128], BF16)
        gw_f = p.sbuf("gw_f", [NE, D], F32)
        gwT = p.sbuf("gwT", [128, 8 * NE], F32)
        pb_bf = p.sbuf("pb_bf", [1, D], BF16)
        ones_bf = p.sbuf("ones_bf", [1, 128], BF16)
        xw = [p.sbuf(f"xw{i}", [128, 2 * D]) for i in range(2)]
        xc = p.sbuf("xc", [128, 2 * D])
        sqs = p.sbuf("sqs", [128, D])
        nmean = p.sbuf("nmean", [128, 2])
        rstd = p.sbuf("rstd", [128, 2])
        xlnb = p.sbuf("xlnb", [128, 2 * D], BF16)
        xlnT = p.sbuf("xlnT", [128, 8 * WIN], BF16)
        qT = p.sbuf("qTs", [64, NH * WIN], BF16)
        kvsb = p.sbuf("kvsb", [64, 2 * WIN], BF16)
        vtok = p.sbuf("vtok", [128, 2 * HD], BF16)
        Psb = p.sbuf("Psb", [128, 4 * WIN], BF16)
        PTsb = p.sbuf("PTsb", [128, 8 * 128], BF16)
        nmax = p.sbuf("nmax", [128, 4])
        rcp = p.sbuf("rcp", [128, NH])
        attn = p.sbuf("attn", [128, D], BF16)
        attnT = p.sbuf("attnT", [128, 8 * 128], BF16)
        x1sb = p.sbuf("x1sb", [128, 2 * D])
        xc1 = p.sbuf("xc1", [128, D])
        rstd1 = p.sbuf("rstd1", [128, 1])
        xln1sb = p.sbuf("xln1sb", [128, D], BF16)
        xc1T = p.sbuf("xc1T", [128, 8 * 128])
        lgt = p.sbuf("lgt", [128, NE])
        top8 = p.sbuf("top8", [128, TOPK])
        nmax0 = p.sbuf("nmax0", [128, 1])
        sd_t = p.sbuf("sd_t", [128, 3 * TOPK])
        eq3 = p.sbuf("eq3", [128, TOPK * NE])
        oh3 = p.sbuf("oh3", [128, TOPK * NE])
        cntT = p.sbuf("cntT", [128, NE])
        scr8 = p.sbuf("scr8", [128, TOPK])
        scr64 = p.sbuf("scr64", [128, TOPK * TOPK])
        base_sb = p.sbuf("base_sb", [128, NE])
        carry = p.sbuf("carry", [1, NE])
        c_eps = p.sbuf("c_eps", [128, 1])
        c_invD = p.sbuf("c_invD", [128, 1])
        c_mhalf = p.sbuf("c_mhalf", [128, 1])
        c_qs = p.sbuf("c_qs", [128, 1])

        ptA = p.psum("ptA", [128, 1024], BF16)
        ptB = p.psum("ptB", [128, 1024], BF16)
        psA = p.psum("psA", [128, 512])
        psB = p.psum("psB", [128, 512])
        psC = p.psum("psC", [128, 512])
        psD = p.psum("psD", [128, 512])
        psS = p.psum("psS", [128, 1024])

        blk = p.ctx.enter_context(nc.Block())
        _load_consts(p, blk, hbs, C)

        @blk.gpsimd
        def _(g):
            p.dma(g, "gp", "gd", qw_bf[:].rearrange("p (c d) -> p c d", c=8),
                  qw_h.rearrange("(c p) d -> p c d", p=128), writes=("qw_bf",))
            p.dma(g, "gp", "gd", kv_bf[:], kvw_h, writes=("kv_bf",))
            p.dma(g, "gp", "gd", pb_bf[:], pb_h, writes=("pb_bf",))
            p.dma(g, "gp", "gd", ones_bf[:], hbs["ones_row"], writes=("ones_bf",))

        @blk.sync
        def _(sync):
            p.dma(sync, "sync", "sd", gw_f[:], gw_h, writes=("gw_f",))

        for cdi in range(8):
            @blk.tensor
            def _(t):
                p.mm(t, [
                    (lambda cdo=cdo, cdi=cdi: t.transpose(
                        ptA[:, cdo * 128:(cdo + 1) * 128],
                        qw_bf[:, cdo * D + cdi * 128: cdo * D + (cdi + 1) * 128],
                        C["ident_bf"][:])) for cdo in range(8)],
                    reads=("qw_bf", "s_ident_bf"), writes=("ptA",))

            @blk.vector
            def _(v):
                p.op(v, "vt", lambda cdi=cdi: v.tensor_copy(
                    out=qwT[:, cdi * D:(cdi + 1) * D], in_=ptA[:]),
                    reads=("ptA",), writes=("qwT",))

        @blk.tensor
        def _(t):
            p.mm(t, [
                (lambda c=c: t.transpose(
                    ptB[:, c * 128:(c + 1) * 128],
                    kv_bf[:, c * 128:(c + 1) * 128],
                    C["ident_bf"][:])) for c in range(8)],
                reads=("kv_bf", "s_ident_bf"), writes=("ptB",))

        @blk.vector
        def _(v):
            p.op(v, "vt", lambda: v.tensor_copy(out=kvwT[:], in_=ptB[:]),
                 reads=("ptB",), writes=("kvwT",))
            p.op(v, "vt", lambda: v.memset(carry[:], 0.0), writes=("carry",))
            p.op(v, "vt", lambda: v.memset(c_eps[:], LN_EPS), writes=("c_eps",))
            p.op(v, "vt", lambda: v.memset(c_invD[:], 1.0 / D), writes=("c_invD",))
            p.op(v, "vt", lambda: v.memset(c_mhalf[:], -0.5), writes=("c_mhalf",))
            p.op(v, "vt", lambda: v.memset(c_qs[:], float(HD) ** -0.5), writes=("c_qs",))

        @blk.gpsimd
        def _(g):
            p.dma(g, "gp", "gd", qw_bf[:].rearrange("p (c d) -> p c d", c=8),
                  pw_h.rearrange("(c p) d -> p c d", p=128), writes=("qw_bf",))

        for cdi in range(8):
            @blk.tensor
            def _(t):
                p.mm(t, [
                    (lambda cdo=cdo, cdi=cdi: t.transpose(
                        ptA[:, cdo * 128:(cdo + 1) * 128],
                        qw_bf[:, cdo * D + cdi * 128: cdo * D + (cdi + 1) * 128],
                        C["ident_bf"][:])) for cdo in range(8)],
                    reads=("qw_bf", "s_ident_bf"), writes=("ptA",))

            @blk.vector
            def _(v):
                p.op(v, "vt", lambda cdi=cdi: v.tensor_copy(
                    out=pwT[:, cdi * D:(cdi + 1) * D], in_=ptA[:]),
                    reads=("ptA",), writes=("pwT",))

        @blk.tensor
        def _(t):
            p.mm(t, [
                (lambda c=c: t.transpose(
                    psD[:, c * 64:(c + 1) * 64],
                    gw_f[0:64, c * 128:(c + 1) * 128],
                    C["ident_f32"][0:64, 0:64])) for c in range(8)],
                reads=("gw_f", "s_ident_f32"), writes=("psD",))

        @blk.vector
        def _(v):
            p.op(v, "vt", lambda: v.tensor_copy(out=gwT[:], in_=psD[:]),
                 reads=("psD",), writes=("gwT",))

        for w in range(nwin):
            par = w % 2
            xwb = xw[par]
            xwn = f"xw{par}"
            xsem = "x0" if par == 0 else "x1"

            @blk.sync
            def _(sync):
                p.dma(sync, "sync", xsem,
                      xwb[:].rearrange("p (t d) -> p t d", t=2),
                      x_h.rearrange("(w t p) d -> w p t d", t=2, p=128)[w],
                      writes=(xwn,))

            if lvl < 1:
                continue
            @blk.vector
            def _(v):
                for i in range(2):
                    sl = slice(i * D, (i + 1) * D)
                    p.op(v, "vt", lambda sl=sl, i=i: v.reduce_sum(
                        out=nmean[:, i:i + 1], in_=xwb[:, sl], axis=AX.X,
                        negate=True), reads=(xwn,), writes=("nmean",))
                    p.op(v, "vt", lambda i=i: v.tensor_scalar_mul(
                        out=nmean[:, i:i + 1], in0=nmean[:, i:i + 1],
                        scalar1=1.0 / D), reads=("nmean",), writes=("nmean",))
                    p.op(v, "vt", lambda sl=sl, i=i: v.tensor_scalar_add(
                        out=xc[:, sl], in0=xwb[:, sl], scalar1=nmean[:, i:i + 1]),
                        reads=(xwn, "nmean"), writes=("xc",))

            @blk.scalar
            def _(s):
                for i in range(2):
                    sl = slice(i * D, (i + 1) * D)
                    p.op(s, "at", lambda sl=sl, i=i: s.activation(
                        out=sqs[:], in_=xc[:, sl], func=AF.Square,
                        accum_out=rstd[:, i:i + 1]),
                        reads=("xc",), writes=("sqs", "rstd"))
                    p.op(s, "at", lambda i=i: s.activation(
                        out=rstd[:, i:i + 1], in_=rstd[:, i:i + 1], func=AF.Ln,
                        scale=c_invD[:], bias=c_eps[:]),
                        reads=("rstd", "c_invD", "c_eps"), writes=("rstd",))
                    p.op(s, "at", lambda i=i: s.activation(
                        out=rstd[:, i:i + 1], in_=rstd[:, i:i + 1], func=AF.Exp,
                        scale=c_mhalf[:]), reads=("rstd", "c_mhalf"), writes=("rstd",))

            @blk.vector
            def _(v):
                for i in range(2):
                    sl = slice(i * D, (i + 1) * D)
                    p.op(v, "vt", lambda sl=sl, i=i: v.tensor_scalar_mul(
                        out=xlnb[:, sl], in0=xc[:, sl], scalar1=rstd[:, i:i + 1]),
                        reads=("xc", "rstd"), writes=("xlnb",))

            if lvl < 2:
                continue
            for i in range(2):
                ptn = "ptA" if i == 0 else "ptB"
                pt = ptA if i == 0 else ptB

                @blk.tensor
                def _(t):
                    p.mm(t, [
                        (lambda pt=pt, i=i, c=c: t.transpose(
                            pt[:, c * 128:(c + 1) * 128],
                            xlnb[:, i * D + c * 128: i * D + (c + 1) * 128],
                            C["ident_bf"][:])) for c in range(8)],
                        reads=("xlnb", "s_ident_bf"), writes=(ptn,))

                @blk.vector
                def _(v):
                    p.op(v, "vt", lambda pt=pt, i=i: v.tensor_copy(
                        out=xlnT[:].rearrange("p (c x) -> p c x", c=8)[
                            :, :, i * 128:(i + 1) * 128],
                        in_=pt[:].rearrange("p (c x) -> p c x", c=8)),
                        reads=(ptn,), writes=("xlnT",))

            if lvl < 3:
                continue
            for half in range(8):
                bank = psA if half % 2 == 0 else psB
                bn = "psA" if half % 2 == 0 else "psB"

                @blk.tensor
                def _(t):
                    for j in range(2):
                        h = half * 2 + j
                        p.mm(t, [
                            (lambda bank=bank, j=j, h=h, cdi=cdi: t.matmul(
                                bank[0:64, j * 256:(j + 1) * 256],
                                qwT[:, cdi * D + h * 64: cdi * D + h * 64 + 64],
                                xlnT[:, cdi * WIN:(cdi + 1) * WIN],
                                start=(cdi == 0), stop=(cdi == 7)))
                            for cdi in range(8)],
                            reads=("qwT", "xlnT"), writes=(bn,))

                @blk.scalar
                def _(s):
                    p.op(s, "at", lambda bank=bank, half=half: s.activation(
                        out=qT[0:64, half * 512:(half + 1) * 512],
                        in_=bank[0:64, :],
                        func=AF.Copy, scale=c_qs[0:64, :]),
                        reads=(bn, "c_qs"), writes=("qTs",))

            if lvl < 4:
                continue
            @blk.tensor
            def _(t):
                for kvh in range(2):
                    p.mm(t, [
                        (lambda kvh=kvh, cdi=cdi: t.matmul(
                            psC[0:64, kvh * 256:(kvh + 1) * 256],
                            kvwT[:, cdi * 128 + kvh * 64:
                                 cdi * 128 + kvh * 64 + 64],
                            xlnT[:, cdi * WIN:(cdi + 1) * WIN],
                            start=(cdi == 0), stop=(cdi == 7)))
                        for cdi in range(8)],
                        reads=("kvwT", "xlnT"), writes=("psC",))

            @blk.vector
            def _(v):
                p.op(v, "vt", lambda: v.tensor_copy(
                    out=kvsb[:], in_=psC[0:64, :]), reads=("psC",), writes=("kvsb",))

            if lvl < 5:
                continue
            @blk.tensor
            def _(t):
                p.mm(t, [
                    (lambda ktc=ktc: t.transpose(
                        ptA[:, ktc * 64:(ktc + 1) * 64],
                        kvsb[0:64, 256 + ktc * 128: 256 + (ktc + 1) * 128],
                        C["ident_bf"][0:64, 0:64])) for ktc in range(2)],
                    reads=("kvsb", "s_ident_bf"), writes=("ptA",))

            @blk.vector
            def _(v):
                p.op(v, "vt", lambda: v.tensor_copy(
                    out=vtok[:], in_=ptA[:, 0:128]), reads=("ptA",), writes=("vtok",))

            if lvl < 6:
                continue
            for qc in range(2):
                for hg in range(4 if lvl >= 7 else 0):
                    @blk.tensor
                    def _(t):
                        for j in range(4):
                            h = hg * 4 + j
                            p.mm(t, [lambda h=h, j=j, qc=qc: t.matmul(
                                psS[:, j * 256:(j + 1) * 256],
                                qT[0:64, h * WIN + qc * 128: h * WIN + qc * 128 + 128],
                                kvsb[0:64, 0:256],
                                start=True, stop=True)],
                                reads=("qTs", "kvsb"), writes=("psS",))

                    if hlvl < 2:
                        continue
                    @blk.vector
                    def _(v):
                        p.op(v, "vt", lambda: v.reduce_max(
                            out=nmax[:], in_=psS[:].rearrange(
                                "p (g k) -> p g k", g=4),
                            axis=AX.X, negate=True),
                            reads=("psS",), writes=("nmax",))
                        if hlvl < 3:
                            return
                        p.op(v, "vt", lambda: v.tensor_tensor(
                            out=psS[:], in0=psS[:],
                            in1=nmax[:].unsqueeze(2).to_broadcast(
                                [128, 4, 256]), op=ALU.add),
                            reads=("psS", "nmax"), writes=("psS",))

                    if hlvl < 4:
                        continue
                    @blk.scalar
                    def _(s):
                        p.op(s, "at", lambda: s.activation(
                            out=Psb[:], in_=psS[:], func=AF.Exp),
                            reads=("psS",), writes=("Psb",))

                    if hlvl < 5:
                        continue
                    @blk.vector
                    def _(v):
                        p.op(v, "vt", lambda hg=hg: v.reduce_sum(
                            out=rcp[:, hg * 4:(hg + 1) * 4],
                            in_=Psb[:].rearrange("p (g k) -> p g k", g=4),
                            axis=AX.X), reads=("Psb",), writes=("rcp",))
                        p.op(v, "vt", lambda hg=hg: v.reciprocal(
                            out=rcp[:, hg * 4:(hg + 1) * 4],
                            in_=rcp[:, hg * 4:(hg + 1) * 4]),
                            reads=("rcp",), writes=("rcp",))

                    if hlvl < 6:
                        continue
                    @blk.tensor
                    def _(t):
                        p.mm(t, [
                            (lambda j=j, ktc=ktc: t.transpose(
                                ptB[:, (j * 2 + ktc) * 128:(j * 2 + ktc + 1) * 128],
                                Psb[:, j * 256 + ktc * 128:
                                    j * 256 + ktc * 128 + 128],
                                C["ident_bf"][:]))
                            for j in range(4) for ktc in range(2)],
                            reads=("Psb", "s_ident_bf"), writes=("ptB",))

                    if hlvl < 7:
                        continue
                    @blk.vector
                    def _(v):
                        p.op(v, "vt", lambda: v.tensor_copy(
                            out=PTsb[:], in_=ptB[:]), reads=("ptB",), writes=("PTsb",))

                    if hlvl < 8:
                        continue
                    @blk.tensor
                    def _(t):
                        for j in range(4):
                            h = hg * 4 + j
                            bank = psA if h < 8 else psB
                            bn = "psA" if h < 8 else "psB"
                            col = (h % 8) * 64
                            p.mm(t, [
                                (lambda bank=bank, col=col, j=j, ktc=ktc: t.matmul(
                                    bank[:, col:col + 64],
                                    PTsb[:, (j * 2 + ktc) * 128:
                                         (j * 2 + ktc + 1) * 128],
                                    vtok[:, ktc * 64:(ktc + 1) * 64],
                                    start=(ktc == 0), stop=(ktc == 1)))
                                for ktc in range(2)],
                                reads=("PTsb", "vtok"), writes=(bn,))

                if lvl < 8:
                    continue
                @blk.vector
                def _(v):
                    for bi, (bank, bn) in enumerate([(psA, "psA"), (psB, "psB")]):
                        p.op(v, "vt", lambda bank=bank, bi=bi: v.tensor_tensor(
                            out=attn[:, bi * 512:(bi + 1) * 512], in0=bank[:],
                            in1=rcp[:, bi * 8:(bi + 1) * 8].unsqueeze(2).to_broadcast([128, 8, 64]),
                            op=ALU.mult), reads=(bn, "rcp"), writes=("attn",))

                if lvl < 9:
                    continue
                @blk.tensor
                def _(t):
                    p.mm(t, [
                        (lambda c=c: t.transpose(
                            ptA[:, c * 128:(c + 1) * 128],
                            attn[:, c * 128:(c + 1) * 128],
                            C["ident_bf"][:])) for c in range(8)],
                        reads=("attn", "s_ident_bf"), writes=("ptA",))

                @blk.vector
                def _(v):
                    p.op(v, "vt", lambda: v.tensor_copy(
                        out=attnT[:], in_=ptA[:]), reads=("ptA",), writes=("attnT",))

                @blk.tensor
                def _(t):
                    for nb in range(2):
                        bank = psA if nb == 0 else psB
                        bn = "psA" if nb == 0 else "psB"
                        emits = [(lambda bank=bank, nb=nb: t.matmul(
                            bank[:], ones_bf[:],
                            pb_bf[:, nb * 512:(nb + 1) * 512],
                            start=True, stop=False))]
                        emits += [
                            (lambda bank=bank, nb=nb, cdi=cdi: t.matmul(
                                bank[:],
                                attnT[:, cdi * 128:(cdi + 1) * 128],
                                pwT[:, cdi * D + nb * 512: cdi * D + (nb + 1) * 512],
                                start=False, stop=(cdi == 7)))
                            for cdi in range(8)]
                        p.mm(t, emits, reads=("ones_bf", "pb_bf", "attnT", "pwT"),
                             writes=(bn,))

                @blk.vector
                def _(v):
                    for nb, (bank, bn) in enumerate([(psA, "psA"), (psB, "psB")]):
                        p.op(v, "vt", lambda bank=bank, nb=nb: v.tensor_add(
                            out=x1sb[:, qc * D + nb * 512: qc * D + (nb + 1) * 512],
                            in0=xwb[:, qc * D + nb * 512: qc * D + (nb + 1) * 512],
                            in1=bank[:]), reads=(xwn, bn), writes=("x1sb",))

                if lvl < 10:
                    continue
                tile = 2 * w + qc
                sl = slice(qc * D, (qc + 1) * D)

                @blk.vector
                def _(v):
                    p.op(v, "vt", lambda: v.reduce_sum(
                        out=nmean[:, 0:1], in_=x1sb[:, sl], axis=AX.X,
                        negate=True), reads=("x1sb",), writes=("nmean",))
                    p.op(v, "vt", lambda: v.tensor_scalar_mul(
                        out=nmean[:, 0:1], in0=nmean[:, 0:1], scalar1=1.0 / D),
                        reads=("nmean",), writes=("nmean",))
                    p.op(v, "vt", lambda: v.tensor_scalar_add(
                        out=xc1[:], in0=x1sb[:, sl], scalar1=nmean[:, 0:1]),
                        reads=("x1sb", "nmean"), writes=("xc1",))

                @blk.scalar
                def _(s):
                    p.op(s, "at", lambda: s.activation(
                        out=sqs[:], in_=xc1[:], func=AF.Square,
                        accum_out=rstd1[:]), reads=("xc1",), writes=("sqs", "rstd1"))
                    p.op(s, "at", lambda: s.activation(
                        out=rstd1[:], in_=rstd1[:], func=AF.Ln,
                        scale=c_invD[:], bias=c_eps[:]),
                        reads=("rstd1", "c_invD", "c_eps"), writes=("rstd1",))
                    p.op(s, "at", lambda: s.activation(
                        out=rstd1[:], in_=rstd1[:], func=AF.Exp, scale=c_mhalf[:]),
                        reads=("rstd1", "c_mhalf"), writes=("rstd1",))

                @blk.vector
                def _(v):
                    p.op(v, "vt", lambda: v.tensor_scalar_mul(
                        out=xln1sb[:], in0=xc1[:], scalar1=rstd1[:]),
                        reads=("xc1", "rstd1"), writes=("xln1sb",))

                @blk.sync
                def _(sync):
                    p.dma(sync, "sync", "st",
                          x1_h[tile * 128:(tile + 1) * 128, :], x1sb[:, sl],
                          reads=("x1sb",))
                    p.dma(sync, "sync", "st",
                          xln1_h[tile * 128:(tile + 1) * 128, :], xln1sb[:],
                          reads=("xln1sb",))

                if lvl < 11:
                    continue
                @blk.tensor
                def _(t):
                    for bi, (bank, bn) in enumerate([(psC, "psC"), (psD, "psD")]):
                        p.mm(t, [
                            (lambda bank=bank, bi=bi, c=c: t.transpose(
                                bank[:, c * 128:(c + 1) * 128],
                                xc1[:, (bi * 4 + c) * 128:(bi * 4 + c + 1) * 128],
                                C["ident_f32"][:])) for c in range(4)],
                            reads=("xc1", "s_ident_f32"), writes=(bn,))

                @blk.vector
                def _(v):
                    for bi, (bank, bn) in enumerate([(psC, "psC"), (psD, "psD")]):
                        p.op(v, "vt", lambda bank=bank, bi=bi: v.tensor_copy(
                            out=xc1T[:, bi * 512:(bi + 1) * 512], in_=bank[:]),
                            reads=(bn,), writes=("xc1T",))

                @blk.tensor
                def _(t):
                    p.mm(t, [
                        (lambda cdi=cdi: t.matmul(
                            psC[:, 0:64],
                            xc1T[:, cdi * 128:(cdi + 1) * 128],
                            gwT[:, cdi * NE:(cdi + 1) * NE],
                            start=(cdi == 0), stop=(cdi == 7))) for cdi in range(8)],
                        reads=("xc1T", "gwT"), writes=("psC",))

                @blk.vector
                def _(v):
                    p.op(v, "vt", lambda: v.tensor_scalar_mul(
                        out=lgt[:], in0=psC[:, 0:64], scalar1=rstd1[:]),
                        reads=("psC", "rstd1"), writes=("lgt",))
                    p.op(v, "vt", lambda: v.max(out=top8[:], in_=lgt[:]),
                         reads=("lgt",), writes=("top8",))
                    p.op(v, "vt", lambda: v.tensor_scalar_mul(
                        out=nmax0[:], in0=top8[:, 0:1], scalar1=-1.0),
                        reads=("top8",), writes=("nmax0",))

                @blk.scalar
                def _(s):
                    p.op(s, "at", lambda: s.activation(
                        out=sd_t[:, TOPK:2 * TOPK], in_=top8[:], func=AF.Exp,
                        bias=nmax0[:]), reads=("top8", "nmax0"), writes=("sd_t",))

                @blk.vector
                def _(v):
                    p.op(v, "vt", lambda: v.tensor_tensor(
                        out=eq3[:],
                        in0=lgt[:].unsqueeze(1).to_broadcast(
                            [128, TOPK, NE]),
                        in1=top8[:].unsqueeze(2).to_broadcast(
                            [128, TOPK, NE]),
                        op=ALU.is_equal), reads=("lgt", "top8"), writes=("eq3",))
                    p.op(v, "vt", lambda: v.tensor_mul(
                        out=eq3[:], in0=eq3[:], in1=C["iotaM"][:]),
                        reads=("eq3", "s_iotaM"), writes=("eq3",))
                    p.op(v, "vt", lambda: v.tensor_reduce(
                        out=sd_t[:, 0:TOPK],
                        in_=eq3[:].rearrange("p (k e) -> p k e", k=TOPK),
                        axis=AX.X, op=ALU.min), reads=("eq3",), writes=("sd_t",))
                    p.op(v, "vt", lambda: v.tensor_scalar_add(
                        out=sd_t[:, 0:TOPK], in0=sd_t[:, 0:TOPK], scalar1=BIGIDX),
                        reads=("sd_t",), writes=("sd_t",))
                    p.op(v, "vt", lambda: v.tensor_tensor(
                        out=oh3[:],
                        in0=sd_t[:, 0:TOPK].unsqueeze(2).to_broadcast(
                            [128, TOPK, NE]),
                        in1=C["iotaE"][:].rearrange("p (k e) -> p k e", k=TOPK),
                        op=ALU.is_equal), reads=("sd_t", "s_iotaE"), writes=("oh3",))
                    p.op(v, "vt", lambda: v.tensor_reduce(
                        out=cntT[:],
                        in_=oh3[:].rearrange("p (k e) -> p e k", k=TOPK),
                        axis=AX.X, op=ALU.add), reads=("oh3",), writes=("cntT",))

                if lvl < 12:
                    continue
                @blk.tensor
                def _(t):
                    p.mm(t, [
                        lambda: t.matmul(psD[:, 0:64], C["tstrictT"][:], cntT[:],
                                         start=True, stop=False),
                        lambda: t.matmul(psD[:, 0:64], C["ones_row"][:], carry[:],
                                         start=False, stop=True)],
                        reads=("s_tstrictT", "cntT", "s_ones_row", "carry"),
                        writes=("psD",))
                    p.mm(t, [
                        lambda: t.matmul(psD[0:1, 128:192], C["ones_col"][:],
                                         cntT[:], start=True, stop=True)],
                        reads=("s_ones_col", "cntT"), writes=("psD",))

                @blk.vector
                def _(v):
                    p.op(v, "vt", lambda: v.tensor_copy(
                        out=base_sb[:], in_=psD[:, 0:64]),
                        reads=("psD",), writes=("base_sb",))
                    p.op(v, "vt", lambda: v.tensor_add(
                        out=carry[:], in0=carry[:], in1=psD[0:1, 128:192]),
                        reads=("carry", "psD"), writes=("carry",))
                    p.op(v, "vt", lambda: v.tensor_tensor(
                        out=eq3[:], in0=oh3[:],
                        in1=base_sb[:].unsqueeze(1).to_broadcast(
                            [128, TOPK, NE]), op=ALU.mult),
                        reads=("oh3", "base_sb"), writes=("eq3",))
                    p.op(v, "vt", lambda: v.tensor_reduce(
                        out=sd_t[:, 2 * TOPK:3 * TOPK],
                        in_=eq3[:].rearrange("p (k e) -> p k e", k=TOPK),
                        axis=AX.X, op=ALU.add), reads=("eq3",), writes=("sd_t",))
                    p.op(v, "vt", lambda: v.tensor_tensor(
                        out=scr64[:],
                        in0=sd_t[:, 0:TOPK].unsqueeze(2).to_broadcast(
                            [128, TOPK, TOPK]),
                        in1=sd_t[:, 0:TOPK].unsqueeze(1).to_broadcast(
                            [128, TOPK, TOPK]),
                        op=ALU.is_equal), reads=("sd_t",), writes=("scr64",))
                    p.op(v, "vt", lambda: v.tensor_mul(
                        out=scr64[:], in0=scr64[:], in1=C["kkmask"][:]),
                        reads=("scr64", "s_kkmask"), writes=("scr64",))
                    p.op(v, "vt", lambda: v.tensor_reduce(
                        out=scr8[:],
                        in_=scr64[:].rearrange("p (k j) -> p k j", k=TOPK),
                        axis=AX.X, op=ALU.add), reads=("scr64",), writes=("scr8",))
                    p.op(v, "vt", lambda: v.tensor_add(
                        out=sd_t[:, 2 * TOPK:3 * TOPK],
                        in0=sd_t[:, 2 * TOPK:3 * TOPK], in1=scr8[:]),
                        reads=("sd_t", "scr8"), writes=("sd_t",))

                @blk.sync
                def _(sync):
                    p.dma(sync, "sync", "st",
                          sdat_h[tile * 128:(tile + 1) * 128, :], sd_t[:],
                          reads=("sd_t",))

        @blk.sync
        def _(sync):
            p.dma(sync, "sync", "st", cnt_h, carry[:], reads=("carry",))

        @blk.gpsimd
        def _(g):
            g.wait_ge(p.sem["st"], p.n["st"])
            g.wait_ge(p.sem["sd"], p.n["sd"])
            g.wait_ge(p.sem["gd"], p.n["gd"])
            g.wait_ge(p.sem["x0"], p.n["x0"])
            g.wait_ge(p.sem["x1"], p.n["x1"])
    return p


def launch1_inputs(core, full):
    x = full["L0_attn_x"].reshape(B * S, D)
    d = {
        "x": np.ascontiguousarray(x[core * TPC:(core + 1) * TPC]),
        "q_w": full["L0_attn_q_w"],
        "kv_w": full["L0_attn_kv_w"],
        "proj_w": full["L0_attn_proj_w"],
        "proj_b": full["L0_attn_proj_b"].reshape(1, D),
        "gate_w": full["L0_ffn_gate_w"],
    }
    d.update(const_inputs(L1_CONSTS))
    return d


def kernel(**inputs):
    cores = list(range(NCORES))
    p1 = build_launch1()
    r1 = run_bass_kernel_spmd(p1.nc, [launch1_inputs(c, inputs) for c in cores],
                              core_ids=cores)
    L1r = [r1.results[c] for c in cores]
    cnts_all = np.concatenate([L1r[c]["cnt"] for c in cores], axis=0)

    p2 = build_launch2()
    r2 = run_bass_kernel_spmd(
        p2.nc,
        [launch2_inputs(c, cnts_all, L1r[c]["slotdat"], L1r[c]["xln1b"])
         for c in cores], core_ids=cores)
    L2r = [r2.results[c] for c in cores]
    contribs = [L2r[c]["contrib"] for c in cores]

    p3 = build_launch3()
    r3 = run_bass_kernel_spmd(p3.nc, [launch3_inputs(c, contribs, inputs)
                                      for c in cores], core_ids=cores)
    y_full = np.concatenate([r3.results[c]["y"] for c in cores], axis=0)

    p4a = build_launch4a()
    r4a = run_bass_kernel_spmd(
        p4a.nc,
        [launch4a_inputs(c, L1r[c]["x1"], L2r[c]["wg"], y_full, inputs)
         for c in cores], core_ids=cores)

    p4b = build_launch4b()
    r4b = run_bass_kernel_spmd(
        p4b.nc,
        [launch4b_inputs(c, r4a.results[c]["x3"], inputs) for c in cores],
        core_ids=cores)
    out = np.concatenate([r4b.results[c]["out"] for c in cores], axis=0)
    return out.reshape(B, S, D)



L2_CONSTS = ["ones_row", "iotaE"]


def build_launch2():
    p = P("l2")
    nc = p.nc
    cnts_h = nc.dram_tensor("cnts_all", [NCORES, NE], F32, kind="ExternalInput").ap()
    mask_h = nc.dram_tensor("coremask", [NCORES, 1], F32, kind="ExternalInput").ap()
    sdat_h = nc.dram_tensor("slotdat", [TPC, 3 * TOPK], F32, kind="ExternalInput").ap()
    xln1_h = nc.dram_tensor("xln1b", [TPC, D], BF16, kind="ExternalInput").ap()

    contrib_h = nc.dram_tensor("contrib", [NE * CAP, D], BF16, kind="ExternalOutput").ap()
    wg_h = nc.dram_tensor("wg", [TPC, 2 * TOPK], F32, kind="ExternalOutput").ap()

    hbs, C = _declare_consts(p, L2_CONSTS)
    with p.ctx:
        for s in ["sd", "gd", "zs", "id", "pt", "vt", "at", "st", "x0", "x1"]:
            p.semaphore(s)
        cnts = p.sbuf("cnts", [NCORES, NE])
        cmask = p.sbuf("cmask", [NCORES, 1])
        offs = p.sbuf("offs", [1, NE])
        offgb = p.sbuf("offgb", [128, NE])
        sdall = p.sbuf("sdall", [128, NT * 3 * TOPK])
        wgall = p.sbuf("wgall", [128, NT * 2 * TOPK])
        zt = p.sbuf("zt", [128, 8192], BF16)
        xt = [p.sbuf(f"xt{i}", [128, D], BF16) for i in range(2)]
        oh3 = p.sbuf("oh3", [128, TOPK * NE])
        selr = p.sbuf("selr", [128, TOPK])
        g_t = p.sbuf("g_t", [128, TOPK])
        gi2 = [p.sbuf(f"gi{i}", [128, TOPK], U32) for i in range(2)]
        idx256 = p.sbuf("idx256", [128, TOPK])
        valid = p.sbuf("valid", [128, TOPK])
        pv = p.sbuf("pv", [128, TOPK])
        wsum = p.sbuf("wsum", [128, 1])
        psA = p.psum("psA", [128, 512])
        blk = p.ctx.enter_context(nc.Block())
        _load_consts(p, blk, hbs, C)
        bcreg = nc.alloc_register(mybir.EngineType.Pool)

        @blk.gpsimd
        def _(g):
            g.reg_mov(bcreg, NE * CAP - 1)

        @blk.sync
        def _(sync):
            p.dma(sync, "sync", "sd", cnts[:], cnts_h, writes=("cnts",))
            p.dma(sync, "sync", "sd", cmask[:], mask_h, writes=("cmask",))
            p.dma(sync, "sync", "sd",
                  sdall[:].rearrange("p (t c) -> p t c", t=NT),
                  sdat_h.rearrange("(t p) c -> p t c", p=128), writes=("sdall",))

        @blk.tensor
        def _(t):
            p.mm(t, [lambda: t.matmul(psA[0:1, 0:64], cmask[:], cnts[:],
                                      start=True, stop=True)],
                 reads=("cmask", "cnts"), writes=("psA",))

        @blk.vector
        def _(v):
            p.op(v, "vt", lambda: v.tensor_copy(out=offs[:], in_=psA[0:1, 0:64]),
                 reads=("psA",), writes=("offs",))
            p.op(v, "vt", lambda: v.scalar_tensor_tensor(
                out=offs[:], in0=C["iotaE"][0:1, 0:NE], scalar=float(CAP),
                in1=offs[:], op0=ALU.mult, op1=ALU.add),
                reads=("offs", "s_iotaE"), writes=("offs",))

        @blk.tensor
        def _(t):
            p.mm(t, [lambda: t.matmul(psA[:, 64:128], C["ones_row"][:], offs[:],
                                      start=True, stop=True)],
                 reads=("s_ones_row", "offs"), writes=("psA",))

        @blk.vector
        def _(v):
            p.op(v, "vt", lambda: v.tensor_copy(out=offgb[:], in_=psA[:, 64:128]),
                 reads=("psA",), writes=("offgb",))
            p.op(v, "vt", lambda: v.memset(zt[:], 0.0), writes=("zt",))

        @blk.sync
        def _(sync):
            for i in range(16):
                p.dma(sync, "sync", "zs",
                      contrib_h[1024 * i:1024 * (i + 1), :].rearrange(
                          "(r p) d -> p r d", p=128),
                      zt[:].rearrange("p (r d) -> p r d", r=8),
                      reads=("zt",), writes=("contrib",))

        for tile in range(NT):
            par = tile % 2
            xtb = xt[par]
            xtn = f"xt{par}"
            xsem = "x0" if par == 0 else "x1"
            sd = sdall[:, tile * 24: tile * 24 + 24]
            wg = wgall[:, tile * 16: tile * 16 + 16]

            @blk.sync
            def _(sync):
                p.dma(sync, "sync", xsem, xtb[:],
                      xln1_h[tile * 128:(tile + 1) * 128, :], writes=(xtn,))

            @blk.vector
            def _(v):
                p.op(v, "vt", lambda sd=sd: v.tensor_tensor(
                    out=oh3[:],
                    in0=sd[:, 0:TOPK].unsqueeze(2).to_broadcast([128, TOPK, NE]),
                    in1=C["iotaE"][:].rearrange("p (k e) -> p k e", k=TOPK),
                    op=ALU.is_equal), reads=("sdall", "s_iotaE"), writes=("oh3",))
                p.op(v, "vt", lambda: v.tensor_tensor(
                    out=oh3[:], in0=oh3[:],
                    in1=offgb[:].unsqueeze(1).to_broadcast([128, TOPK, NE]),
                    op=ALU.mult), reads=("oh3", "offgb"), writes=("oh3",))
                p.op(v, "vt", lambda: v.tensor_reduce(
                    out=selr[:], in_=oh3[:].rearrange("p (k e) -> p k e", k=TOPK),
                    axis=AX.X, op=ALU.add), reads=("oh3",), writes=("selr",))
                p.op(v, "vt", lambda sd=sd: v.tensor_tensor(
                    out=g_t[:], in0=sd[:, 2 * TOPK:3 * TOPK], in1=selr[:],
                    op=ALU.add), reads=("sdall", "selr"), writes=("g_t",))
                p.op(v, "vt", lambda sd=sd: v.tensor_scalar_mul(
                    out=idx256[:], in0=sd[:, 0:TOPK], scalar1=float(CAP)),
                    reads=("sdall",), writes=("idx256",))
                p.op(v, "vt", lambda: v.tensor_sub(
                    out=valid[:], in0=g_t[:], in1=idx256[:]),
                    reads=("g_t", "idx256"), writes=("valid",))
                p.op(v, "vt", lambda: v.tensor_single_scalar(
                    out=valid[:], in_=valid[:], scalar=float(CAP), op=ALU.is_lt),
                    reads=("valid",), writes=("valid",))
                p.op(v, "vt", lambda sd=sd: v.tensor_tensor(
                    out=pv[:], in0=sd[:, TOPK:2 * TOPK], in1=valid[:],
                    op=ALU.mult), reads=("sdall", "valid"), writes=("pv",))
                p.op(v, "vt", lambda sd=sd: v.reduce_sum(
                    out=wsum[:], in_=sd[:, TOPK:2 * TOPK], axis=AX.X),
                    reads=("sdall",), writes=("wsum",))
                p.op(v, "vt", lambda: v.reciprocal(out=wsum[:], in_=wsum[:]),
                     reads=("wsum",), writes=("wsum",))
                p.op(v, "vt", lambda wg=wg: v.tensor_scalar_mul(
                    out=wg[:, 0:TOPK], in0=pv[:], scalar1=wsum[:]),
                    reads=("pv", "wsum"), writes=("wgall",))
                p.op(v, "vt", lambda: v.tensor_single_scalar(
                    out=pv[:], in_=valid[:], scalar=1.0, op=ALU.is_lt),
                    reads=("valid",), writes=("pv",))
                p.op(v, "vt", lambda wg=wg: v.scalar_tensor_tensor(
                    out=wg[:, TOPK:2 * TOPK], in0=pv[:], scalar=BIGIDX,
                    in1=g_t[:], op0=ALU.mult, op1=ALU.add),
                    reads=("pv", "g_t"), writes=("wgall",))
                p.op(v, "vt", lambda wg=wg, tile=tile: v.tensor_copy(
                    out=gi2[tile % 2][:], in_=wg[:, TOPK:2 * TOPK]),
                    reads=("wgall",), writes=(f"gi{tile % 2}",))

            @blk.gpsimd
            def _(g):
                for k in range(TOPK):
                    p.idma(g, "gp", "id",
                           out=contrib_h,
                           out_offset=IndirectOffsetOnAxis(
                               gi2[tile % 2][:, k:k + 1], 0),
                           in_=xtb[:], in_offset=None,
                           bounds_check=bcreg, oob_is_err=False,
                           reads=(f"gi{tile % 2}", xtn, "contrib"),
                           writes=("contrib",))

        @blk.sync
        def _(sync):
            p.dma(sync, "sync", "st",
                  wg_h.rearrange("(t p) c -> p t c", p=128),
                  wgall[:].rearrange("p (t c) -> p t c", t=NT),
                  reads=("wgall",))

        @blk.gpsimd
        def _(g):
            for s in ["st", "sd", "zs", "id", "x0", "x1", "gd"]:
                g.wait_ge(p.sem[s], p.n[s])
    return p


def launch2_inputs(core, cnts_all, sdat, xln1):
    mask = np.zeros((NCORES, 1), np.float32)
    mask[:core] = 1.0
    d = {"cnts_all": cnts_all, "coremask": mask, "slotdat": sdat, "xln1b": xln1}
    d.update(const_inputs(L2_CONSTS))
    return d



L3_CONSTS = ["ident_bf"]


def build_launch3():
    p = P("l3")
    nc = p.nc
    csl_h = nc.dram_tensor("contribs_sl", [NCORES, EPC * CAP, D], BF16,
                           kind="ExternalInput").ap()
    wg_h = nc.dram_tensor("ew_gate", [EPC, EI, D], F32, kind="ExternalInput").ap()
    wu_h = nc.dram_tensor("ew_up", [EPC, EI, D], F32, kind="ExternalInput").ap()
    wd_h = nc.dram_tensor("ew_down", [EPC, D, EI], F32, kind="ExternalInput").ap()
    y_h = nc.dram_tensor("y", [EPC * CAP, D], BF16, kind="ExternalOutput").ap()

    hbs, C = _declare_consts(p, L3_CONSTS)
    NIC = EI // 128
    with p.ctx:
        for s in ["sd", "gd", "ld0", "pt", "vt", "at", "st"]:
            p.semaphore(s)
        disp = p.sbuf("disp", [128, 2 * D], BF16)
        dispT = p.sbuf("dispT", [128, 8 * CAP], BF16)
        wgt = [p.sbuf(f"wgt{i}", [128, NIC * D], BF16) for i in range(2)]
        wgT = p.sbuf("wgT", [128, 8 * EI], BF16)
        wuT = p.sbuf("wuT", [128, 8 * EI], BF16)
        wdr = [p.sbuf(f"wdr{i}", [128, 8 * EI], BF16) for i in range(2)]
        wdT = p.sbuf("wdT", [128, NIC * D], BF16)
        gsb = p.sbuf("gsb", [128, 2 * EI])
        hsb = p.sbuf("hsb", [128, 2 * EI], BF16)
        hT = p.sbuf("hT", [128, NIC * CAP], BF16)
        ysb = p.sbuf("ysb", [128, 2 * D], BF16)
        ptA = p.psum("ptA", [128, 1024], BF16)
        ptB = p.psum("ptB", [128, 1024], BF16)
        psA = p.psum("psA", [128, 512])
        psB = p.psum("psB", [128, 512])
        psC = p.psum("psC", [128, 512])
        psD = p.psum("psD", [128, 512])
        psE = p.psum("psE", [128, 512])
        psF = p.psum("psF", [128, 512])
        blk = p.ctx.enter_context(nc.Block())
        _load_consts(p, blk, hbs, C)

        for e in range(EPC):
            wpar = e % 2
            wgt_b, wdr_b = wgt[wpar], wdr[wpar]
            wgt_n, wdr_n = f"wgt{wpar}", f"wdr{wpar}"

            @blk.sync
            def _(sync):
                p.dma(sync, "sync", "sd",
                      disp[:].rearrange("p (t d) -> p t d", t=2),
                      csl_h[0, e * CAP:(e + 1) * CAP, :].rearrange(
                          "(t p) d -> p t d", p=128),
                      writes=("disp",))

            @blk.gpsimd
            def _(g):
                for c2 in range(1, NCORES):
                    p.dma(g, "gp", "sd",
                          disp[:].rearrange("p (t d) -> p t d", t=2),
                          csl_h[c2, e * CAP:(e + 1) * CAP, :].rearrange(
                              "(t p) d -> p t d", p=128),
                          reads=("disp",), writes=("disp",), accum_op=ALU.add)
                p.dma(g, "gp", "ld0", wgt_b[:].rearrange("p (c d) -> p c d", c=NIC),
                      wg_h[e].rearrange("(c p) d -> p c d", p=128),
                      writes=(wgt_n,))

            @blk.tensor
            def _(t):
                for i in range(2):
                    p.mm(t, [
                        (lambda i=i, c=c: t.transpose(
                            ptA[:, c * 128:(c + 1) * 128],
                            disp[:, i * D + c * 128: i * D + (c + 1) * 128],
                            C["ident_bf"][:])) for c in range(8)],
                        reads=("disp", "s_ident_bf"), writes=("ptA",))
                    @blk.vector
                    def _(v):
                        p.op(v, "vt", lambda i=i: v.tensor_copy(
                            out=dispT[:].rearrange("p (c x) -> p c x", c=8)[
                                :, :, i * 128:(i + 1) * 128],
                            in_=ptA[:].rearrange("p (c x) -> p c x", c=8)),
                            reads=("ptA",), writes=("dispT",))

            @blk.tensor
            def _(t):
                for ch in range(8):
                    p.mm(t, [
                        (lambda ch=ch, ci=ci: t.transpose(
                            ptB[:, ci * 128:(ci + 1) * 128],
                            wgt_b[:, ci * D + ch * 128: ci * D + (ch + 1) * 128],
                            C["ident_bf"][:])) for ci in range(NIC)],
                        reads=(wgt_n, "s_ident_bf"), writes=("ptB",))

                    @blk.vector
                    def _(v):
                        p.op(v, "vt", lambda ch=ch: v.tensor_copy(
                            out=wgT[:, ch * EI:(ch + 1) * EI],
                            in_=ptB[:, 0:EI]), reads=("ptB",), writes=("wgT",))

            @blk.gpsimd
            def _(g):
                p.dma(g, "gp", "ld0", wgt_b[:].rearrange("p (c d) -> p c d", c=NIC),
                      wu_h[e].rearrange("(c p) d -> p c d", p=128),
                      writes=(wgt_n,))

            @blk.tensor
            def _(t):
                for ch in range(8):
                    p.mm(t, [
                        (lambda ch=ch, ci=ci: t.transpose(
                            ptB[:, ci * 128:(ci + 1) * 128],
                            wgt_b[:, ci * D + ch * 128: ci * D + (ch + 1) * 128],
                            C["ident_bf"][:])) for ci in range(NIC)],
                        reads=(wgt_n, "s_ident_bf"), writes=("ptB",))

                    @blk.vector
                    def _(v):
                        p.op(v, "vt", lambda ch=ch: v.tensor_copy(
                            out=wuT[:, ch * EI:(ch + 1) * EI],
                            in_=ptB[:, 0:EI]), reads=("ptB",), writes=("wuT",))

            @blk.gpsimd
            def _(g):
                p.dma(g, "gp", "ld0", wdr_b[:].rearrange("p (c d) -> p c d", c=8),
                      wd_h[e].rearrange("(c p) d -> p c d", p=128),
                      writes=(wdr_n,))

            @blk.tensor
            def _(t):
                for ci in range(NIC):
                    p.mm(t, [
                        (lambda ci=ci, ch=ch: t.transpose(
                            ptB[:, ch * 128:(ch + 1) * 128],
                            wdr_b[:, ch * EI + ci * 128: ch * EI + (ci + 1) * 128],
                            C["ident_bf"][:])) for ch in range(8)],
                        reads=(wdr_n, "s_ident_bf"), writes=("ptB",))

                    @blk.vector
                    def _(v):
                        p.op(v, "vt", lambda ci=ci: v.tensor_copy(
                            out=wdT[:, ci * D:(ci + 1) * D],
                            in_=ptB[:]), reads=("ptB",), writes=("wdT",))

            for cc in range(2):
                @blk.tensor
                def _(t):
                    for nb in range(2):
                        nw = 512 if nb == 0 else EI - 512
                        bankG = [psA, psB][nb]
                        bankU = [psC, psD][nb]
                        bG = ["psA", "psB"][nb]
                        bU = ["psC", "psD"][nb]
                        p.mm(t, [
                            (lambda cc=cc, nb=nb, nw=nw, bankG=bankG, ch=ch: t.matmul(
                                bankG[:, 0:nw],
                                dispT[:, ch * CAP + cc * 128: ch * CAP + cc * 128 + 128],
                                wgT[:, ch * EI + nb * 512: ch * EI + nb * 512 + nw],
                                start=(ch == 0), stop=(ch == 7))) for ch in range(8)],
                            reads=("dispT", "wgT"), writes=(bG,))
                        p.mm(t, [
                            (lambda cc=cc, nb=nb, nw=nw, bankU=bankU, ch=ch: t.matmul(
                                bankU[:, 0:nw],
                                dispT[:, ch * CAP + cc * 128: ch * CAP + cc * 128 + 128],
                                wuT[:, ch * EI + nb * 512: ch * EI + nb * 512 + nw],
                                start=(ch == 0), stop=(ch == 7))) for ch in range(8)],
                            reads=("dispT", "wuT"), writes=(bU,))

                @blk.scalar
                def _(s):
                    p.op(s, "at", lambda cc=cc: s.activation(
                        out=gsb[:, cc * EI: cc * EI + 512], in_=psA[:],
                        func=AF.Silu), reads=("psA",), writes=("gsb",))
                    p.op(s, "at", lambda cc=cc: s.activation(
                        out=gsb[:, cc * EI + 512: (cc + 1) * EI], in_=psB[:, 0:EI - 512],
                        func=AF.Silu), reads=("psB",), writes=("gsb",))

                @blk.vector
                def _(v):
                    p.op(v, "vt", lambda cc=cc: v.tensor_tensor(
                        out=hsb[:, cc * EI: cc * EI + 512],
                        in0=gsb[:, cc * EI: cc * EI + 512], in1=psC[:],
                        op=ALU.mult), reads=("gsb", "psC"), writes=("hsb",))
                    p.op(v, "vt", lambda cc=cc: v.tensor_tensor(
                        out=hsb[:, cc * EI + 512: (cc + 1) * EI],
                        in0=gsb[:, cc * EI + 512: (cc + 1) * EI],
                        in1=psD[:, 0:EI - 512],
                        op=ALU.mult), reads=("gsb", "psD"), writes=("hsb",))

            @blk.tensor
            def _(t):
                for ci in range(NIC):
                    p.mm(t, [
                        (lambda ci=ci, cc=cc: t.transpose(
                            ptA[:, cc * 128:(cc + 1) * 128],
                            hsb[:, cc * EI + ci * 128: cc * EI + (ci + 1) * 128],
                            C["ident_bf"][:])) for cc in range(2)],
                        reads=("hsb", "s_ident_bf"), writes=("ptA",))

                    @blk.vector
                    def _(v):
                        p.op(v, "vt", lambda ci=ci: v.tensor_copy(
                            out=hT[:, ci * CAP:(ci + 1) * CAP],
                            in_=ptA[:, 0:CAP]), reads=("ptA",), writes=("hT",))

            for cc in range(2):
                @blk.tensor
                def _(t):
                    for nb in range(2):
                        bank = [psE, psF][nb]
                        bnm = ["psE", "psF"][nb]
                        p.mm(t, [
                            (lambda cc=cc, nb=nb, bank=bank, ci=ci: t.matmul(
                                bank[:],
                                hT[:, ci * CAP + cc * 128: ci * CAP + cc * 128 + 128],
                                wdT[:, ci * D + nb * 512: ci * D + (nb + 1) * 512],
                                start=(ci == 0), stop=(ci == NIC - 1)))
                            for ci in range(NIC)],
                            reads=("hT", "wdT"), writes=(bnm,))

                @blk.vector
                def _(v):
                    for nb, bnm in enumerate(["psE", "psF"]):
                        bank = [psE, psF][nb]
                        p.op(v, "vt", lambda cc=cc, nb=nb, bank=bank: v.tensor_copy(
                            out=ysb[:, cc * D + nb * 512: cc * D + (nb + 1) * 512],
                            in_=bank[:]), reads=(bnm,), writes=("ysb",))

            @blk.sync
            def _(sync):
                p.dma(sync, "sync", "st",
                      y_h[e * CAP:(e + 1) * CAP, :].rearrange(
                          "(t p) d -> p t d", p=128),
                      ysb[:].rearrange("p (t d) -> p t d", t=2),
                      reads=("ysb",))

        @blk.gpsimd
        def _(g):
            for s in ["st", "sd", "ld0", "gd"]:
                g.wait_ge(p.sem[s], p.n[s])
    return p


def launch3_inputs(core, contribs, full):
    sl = np.stack([contribs[c2][core * EPC * CAP:(core + 1) * EPC * CAP]
                   for c2 in range(NCORES)])
    d = {"contribs_sl": sl,
         "ew_gate": full["L0_ffn_w_gate"][core * EPC:(core + 1) * EPC],
         "ew_up": full["L0_ffn_w_up"][core * EPC:(core + 1) * EPC],
         "ew_down": full["L0_ffn_w_down"][core * EPC:(core + 1) * EPC]}
    d.update(const_inputs(L3_CONSTS))
    return d




def _emit_ln(p, blk, src, src_n, dst, dst_n, rstd_t, rstd_n, xcbuf, xc_n,
             nmean, sqs, c_eps, c_invD, c_mhalf):
    @blk.vector
    def _(v):
        p.op(v, "vt", lambda: v.reduce_sum(
            out=nmean[:, 0:1], in_=src, axis=AX.X, negate=True),
            reads=(src_n,), writes=("nmean",))
        p.op(v, "vt", lambda: v.tensor_scalar_mul(
            out=nmean[:, 0:1], in0=nmean[:, 0:1], scalar1=1.0 / D),
            reads=("nmean",), writes=("nmean",))
        p.op(v, "vt", lambda: v.tensor_scalar_add(
            out=xcbuf, in0=src, scalar1=nmean[:, 0:1]),
            reads=(src_n, "nmean"), writes=(xc_n,))

    @blk.scalar
    def _(s):
        p.op(s, "at", lambda: s.activation(
            out=sqs[:], in_=xcbuf, func=AF.Square, accum_out=rstd_t),
            reads=(xc_n,), writes=("sqs", rstd_n))
        p.op(s, "at", lambda: s.activation(
            out=rstd_t, in_=rstd_t, func=AF.Ln, scale=c_invD[:], bias=c_eps[:]),
            reads=(rstd_n, "c_invD", "c_eps"), writes=(rstd_n,))
        p.op(s, "at", lambda: s.activation(
            out=rstd_t, in_=rstd_t, func=AF.Exp, scale=c_mhalf[:]),
            reads=(rstd_n, "c_mhalf"), writes=(rstd_n,))

    if dst is not None:
        @blk.vector
        def _(v):
            p.op(v, "vt", lambda: v.tensor_scalar_mul(
                out=dst, in0=xcbuf, scalar1=rstd_t),
                reads=(xc_n, rstd_n), writes=(dst_n,))



def _p_sbuf_push(p, name, shape, dt=F32):
    guard = p.nc.sbuf_tensor(name, shape, dt)
    t = guard.__enter__()
    if not hasattr(p, "_guards"):
        p._guards = []
        p._free_stamps = []
    p._guards.append((name, guard))
    b = Buf(name)
    b.r = list(p._free_stamps)
    p.bufs[name] = b
    return t


def _p_sbuf_pop(p, upto):
    while p._guards:
        name, guard = p._guards.pop()
        b = p.bufs.pop(name)
        if b.w is not None:
            p._free_stamps.append(b.w)
        p._free_stamps.extend(b.r)
        guard.__exit__(None, None, None)
        if name == upto:
            return
    raise KeyError(upto)



L4A_CONSTS = ["ident_bf", "ones_row"]
NG = S // 512
NQB = S // 128
NKB = S // 512


def build_launch4a():
    p = P("l4a")
    nc = p.nc
    x1_h = nc.dram_tensor("x1", [TPC, D], F32, kind="ExternalInput").ap()
    wgs_h = nc.dram_tensor("wg", [TPC, 2 * TOPK], F32, kind="ExternalInput").ap()
    y_h = nc.dram_tensor("y_all", [NE * CAP, D], BF16, kind="ExternalInput").ap()
    Wq_h = nc.dram_tensor("Wq", [D, D], F32, kind="ExternalInput").ap()
    Wk_h = nc.dram_tensor("Wk", [D, D], F32, kind="ExternalInput").ap()
    Wv_h = nc.dram_tensor("Wv", [D, D], F32, kind="ExternalInput").ap()
    Wo_h = nc.dram_tensor("Wo", [D, D], F32, kind="ExternalInput").ap()
    bq_h = nc.dram_tensor("bq", [1, D], F32, kind="ExternalInput").ap()
    bk_h = nc.dram_tensor("bk", [1, D], F32, kind="ExternalInput").ap()
    bv_h = nc.dram_tensor("bv", [1, D], F32, kind="ExternalInput").ap()
    bo_h = nc.dram_tensor("bo", [1, D], F32, kind="ExternalInput").ap()
    x3_h = nc.dram_tensor("x3", [TPC, D], F32, kind="ExternalOutput").ap()

    x2_h = nc.dram_tensor("x2sp", [TPC, D], F32).ap()
    xl2_h = nc.dram_tensor("xl2sp", [TPC, D], BF16).ap()
    qt_h = nc.dram_tensor("qtsp", [D, S], BF16).ap()

    hbs, C = _declare_consts(p, L4A_CONSTS)
    with p.ctx:
        for s in ["sd", "gd", "id", "x0", "x1s", "pt", "vt", "at", "st"]:
            p.semaphore(s)
        p.DMA_SEMS = frozenset(["sd", "gd", "st", "x0", "x1s", "id"])
        bcreg = nc.alloc_register(mybir.EngineType.Pool)

        ones_bf = p.sbuf("ones_bf", [1, 128], BF16)
        onesc_bf = p.sbuf("onesc_bf", [128, 1], BF16)
        brow = p.sbuf("brow", [1, 2 * D], BF16)
        bqc = p.sbuf("bqc", [128, 16])
        c_eps = p.sbuf("c_eps", [128, 1])
        c_invD = p.sbuf("c_invD", [128, 1])
        c_mhalf = p.sbuf("c_mhalf", [128, 1])
        c_two = p.sbuf("c_two", [128, 1])
        c_m1 = p.sbuf("c_m1", [128, 1])
        nmean = p.sbuf("nmean", [128, 2])
        rstd = p.sbuf("rstd", [128, 2])

        ptA = p.psum("ptA", [128, 1024], BF16)
        ptB = p.psum("ptB", [128, 1024], BF16)
        psA = p.psum("psA", [128, 512])
        psB = p.psum("psB", [128, 512])
        psC = p.psum("psC", [128, 512])
        psD = p.psum("psD", [128, 512])
        psE = p.psum("psE", [128, 512])
        psF = p.psum("psF", [128, 512])

        blk = p.ctx.enter_context(nc.Block())
        _load_consts(p, blk, hbs, C)

        onesc = p.sbuf("onesc", [1, 1])
        bqr = _p_sbuf_push(p, "bqr", [1, 2 * D])

        @blk.gpsimd
        def _(g):
            g.reg_mov(bcreg, NE * CAP - 1)
            p.dma(g, "gp", "gd", ones_bf[:], hbs["ones_row"], writes=("ones_bf",))
            p.dma(g, "gp", "gd", onesc_bf[:], hbs["ones_row"].rearrange("o p -> p o"),
                  writes=("onesc_bf",))
            p.dma(g, "gp", "gd", brow[0:1, 0:D], bv_h, writes=("brow",))
            p.dma(g, "gp", "gd", brow[0:1, D:2 * D], bo_h, writes=("brow",))

        @blk.sync
        def _(sync):
            p.dma(sync, "sync", "sd", bqr[0:1, 0:D], bq_h, writes=("bqr",))
            p.dma(sync, "sync", "sd", bqr[0:1, D:2 * D], bk_h, writes=("bqr",))

        @blk.vector
        def _(v):
            p.op(v, "vt", lambda: v.memset(c_eps[:], LN_EPS), writes=("c_eps",))
            p.op(v, "vt", lambda: v.memset(c_invD[:], 1.0 / D), writes=("c_invD",))
            p.op(v, "vt", lambda: v.memset(c_mhalf[:], -0.5), writes=("c_mhalf",))
            p.op(v, "vt", lambda: v.memset(c_two[:], 2.0), writes=("c_two",))
            p.op(v, "vt", lambda: v.memset(c_m1[:], -1.0), writes=("c_m1",))
            p.op(v, "vt", lambda: v.memset(onesc[:], 1.0), writes=("onesc",))

        @blk.tensor
        def _(t):
            for c in range(16):
                p.mm(t, [lambda c=c: t.matmul(
                    psA[:, c:c + 1], bqr[0:1, c * 128:(c + 1) * 128],
                    onesc[:], start=True, stop=True)],
                    reads=("bqr", "onesc"), writes=("psA",))

        @blk.vector
        def _(v):
            p.op(v, "vt", lambda: v.tensor_copy(out=bqc[:], in_=psA[:, 0:16]),
                 reads=("psA",), writes=("bqc",))
        _p_sbuf_pop(p, "bqr")

        yti = [_p_sbuf_push(p, f"yti{k}", [128, D], BF16)
               for k in range(2 * TOPK)]
        x1t = [_p_sbuf_push(p, f"x1t{i}", [128, D]) for i in range(2)]
        wgt_t2 = [_p_sbuf_push(p, f"wgt_t{i}", [128, 2 * TOPK]) for i in range(2)]
        gidx2 = [_p_sbuf_push(p, f"gidx{i}", [128, TOPK], U32) for i in range(2)]
        acc = _p_sbuf_push(p, "acc", [128, D])
        x2t = _p_sbuf_push(p, "x2t", [128, D])
        xcb = _p_sbuf_push(p, "xcb", [128, D])
        sqs = _p_sbuf_push(p, "sqs", [128, D])
        xlnb = _p_sbuf_push(p, "xlnb", [128, D], BF16)

        @blk.vector
        def _(v):
            for k in range(2 * TOPK):
                p.op(v, "vt", lambda k=k: v.memset(yti[k][:], 0.0),
                     writes=(f"yti{k}",))

        for tile in range(NT):
            gtok = tile * 128
            par = tile % 2
            x1b, x1n = x1t[par], f"x1t{par}"
            xsem = "x0" if par == 0 else "x1s"

            wgt_t = wgt_t2[tile % 2]
            wgn = f"wgt_t{tile % 2}"

            @blk.sync
            def _(sync):
                p.dma(sync, "sync", xsem, x1b[:], x1_h[gtok:gtok + 128, :],
                      writes=(x1n,))
                p.dma(sync, "sync", xsem, wgt_t[:], wgs_h[gtok:gtok + 128, :],
                      writes=(wgn,))

            gidx = gidx2[tile % 2]
            gxn = f"gidx{tile % 2}"

            @blk.vector
            def _(v):
                p.op(v, "vt", lambda gidx=gidx, wgt_t=wgt_t: v.tensor_copy(
                    out=gidx[:], in_=wgt_t[:, TOPK:2 * TOPK]),
                    reads=(wgn,), writes=(gxn,))

            @blk.gpsimd
            def _(g):
                for k in range(TOPK):
                    p.idma(g, "gp", "id",
                           out=yti[k][:], out_offset=None,
                           in_=y_h,
                           in_offset=IndirectOffsetOnAxis(gidx[:, k:k + 1], 0),
                           bounds_check=bcreg, oob_is_err=False,
                           reads=("gidx", f"yti{k}"), writes=(f"yti{k}",))

            @blk.vector
            def _(v):
                p.op(v, "vt", lambda: v.tensor_scalar_mul(
                    out=acc[:], in0=yti[0][:], scalar1=wgt_t[:, 0:1]),
                    reads=("yti0", "wgt_t"), writes=("acc",))
                for k in range(1, TOPK):
                    p.op(v, "vt", lambda k=k: v.scalar_tensor_tensor(
                        out=acc[:], in0=yti[k][:], scalar=wgt_t[:, k:k + 1],
                        in1=acc[:], op0=ALU.mult, op1=ALU.add),
                        reads=(f"yti{k}", "wgt_t", "acc"), writes=("acc",))
                p.op(v, "vt", lambda: v.tensor_add(
                    out=x2t[:], in0=x1b[:], in1=acc[:]),
                    reads=(x1n, "acc"), writes=("x2t",))

            _emit_ln(p, blk, x2t[:], "x2t", xlnb[:], "xlnb", rstd[:, 0:1], "rstd",
                     xcb[:], "xcb", nmean, sqs, c_eps, c_invD, c_mhalf)

            @blk.sync
            def _(sync):
                p.dma(sync, "sync", "st", x2_h[gtok:gtok + 128, :], x2t[:],
                      reads=("x2t",))
                p.dma(sync, "sync", "st", xl2_h[gtok:gtok + 128, :], xlnb[:],
                      reads=("xlnb",))

        _p_sbuf_pop(p, "yti0")

        Kt = _p_sbuf_push(p, "Kt", [128, 8 * S], BF16)
        Vt = _p_sbuf_push(p, "Vt", [128, (S // 128) * D], BF16)
        nk2h = _p_sbuf_push(p, "nk2h", [1, S], BF16)
        q2row = _p_sbuf_push(p, "q2row", [1, S], BF16)
        onesc2 = _p_sbuf_push(p, "onesc2", [1, 1], BF16)
        wa = [_p_sbuf_push(p, f"wa{i}", [128, 8 * D], BF16) for i in range(2)]
        xlg = [_p_sbuf_push(p, "xlg0", [128, D], BF16)] * 2
        xlTg = _p_sbuf_push(p, "xlTg", [128, 8 * 512], BF16)
        ksq = _p_sbuf_push(p, "ksq", [128, 512], BF16)
        q2c = _p_sbuf_push(p, "q2c", [128, 1])
        Qg = _p_sbuf_push(p, "Qg", [128, 1024], BF16)
        Psb = [_p_sbuf_push(p, f"PsbB{i}", [128, 512], BF16) for i in range(2)]
        PTs = [_p_sbuf_push(p, f"PTs{i}", [128, 512], BF16) for i in range(2)]
        dcol = _p_sbuf_push(p, "dcol", [128, NKB])
        rcpc = _p_sbuf_push(p, "rcpc", [128, 1])
        attn = _p_sbuf_push(p, "attn4", [128, D], BF16)
        attnT = _p_sbuf_push(p, "attnT4", [128, 8 * 128], BF16)
        x3t = _p_sbuf_push(p, "x3t", [128, D])

        @blk.vector
        def _(v):
            p.op(v, "vt", lambda: v.memset(onesc2[:], 1.0), writes=("onesc2",))

        for sq in range(2):
            toff = sq * (TPC // 2)

            @blk.gpsimd
            def _(g):
                p.dma(g, "gp", "gd", wa[0][:].rearrange("p (c d) -> p c d", c=8),
                      Wk_h.rearrange("(c p) d -> p c d", p=128), writes=("wa0",))
                p.dma(g, "gp", "gd", wa[1][:].rearrange("p (c d) -> p c d", c=8),
                      Wv_h.rearrange("(c p) d -> p c d", p=128), writes=("wa1",))

            for grp in range(NG):
                for ti in range(4):
                    gtok = toff + (grp * 4 + ti) * 128
                    par = ti % 2
                    xb, xn_ = xlg[0], "xlg0"
                    xsem = "x0" if par == 0 else "x1s"

                    @blk.sync
                    def _(sync):
                        p.dma(sync, "sync", xsem, xb[:],
                              xl2_h[gtok:gtok + 128, :], writes=(xn_,))

                    ptn = "ptA" if ti % 2 == 0 else "ptB"
                    pt = ptA if ti % 2 == 0 else ptB

                    @blk.tensor
                    def _(t):
                        p.mm(t, [
                            (lambda pt=pt, c=c, xb=xb: t.transpose(
                                pt[:, c * 128:(c + 1) * 128],
                                xb[:, c * 128:(c + 1) * 128],
                                C["ident_bf"][:])) for c in range(8)],
                            reads=(xn_, "s_ident_bf"), writes=(ptn,))

                    @blk.vector
                    def _(v):
                        p.op(v, "vt", lambda pt=pt, ti=ti: v.tensor_copy(
                            out=xlTg[:].rearrange("p (c x) -> p c x", c=8)[
                                :, :, ti * 128:(ti + 1) * 128],
                            in_=pt[:].rearrange("p (c x) -> p c x", c=8)),
                            reads=(ptn,), writes=("xlTg",))

                for wave in range(4):
                    @blk.tensor
                    def _(t):
                        for j in range(2):
                            oc = wave * 2 + j
                            bank = psE if j == 0 else psF
                            bnm = "psE" if j == 0 else "psF"
                            p.mm(t, [
                                (lambda bank=bank, oc=oc, cdi=cdi: t.matmul(
                                    bank[:],
                                    wa[0][:, cdi * D + oc * 128:
                                          cdi * D + (oc + 1) * 128],
                                    xlTg[:, cdi * 512:(cdi + 1) * 512],
                                    start=(cdi == 0), stop=(cdi == 7)))
                                for cdi in range(8)],
                                reads=("wa0", "xlTg"), writes=(bnm,))

                    @blk.vector
                    def _(v):
                        for j in range(2):
                            oc = wave * 2 + j
                            bank = psE if j == 0 else psF
                            bnm = "psE" if j == 0 else "psF"
                            p.op(v, "vt", lambda bank=bank, oc=oc, grp=grp:
                                 v.tensor_scalar_add(
                                     out=Kt[:, oc * S + grp * 512:
                                            oc * S + (grp + 1) * 512],
                                     in0=bank[:], scalar1=bqc[:, 8 + oc: 9 + oc]),
                                 reads=(bnm, "bqc"), writes=("Kt",))

                for ti in range(4):
                    tile = grp * 4 + ti
                    @blk.tensor
                    def _(t):
                        for nb in range(2):
                            bank = psC if nb == 0 else psD
                            bnm = "psC" if nb == 0 else "psD"
                            emits = [(lambda bank=bank, nb=nb: t.matmul(
                                bank[:], ones_bf[:],
                                brow[0:1, nb * 512:(nb + 1) * 512],
                                start=True, stop=False))]
                            emits += [
                                (lambda bank=bank, nb=nb, cdi=cdi, ti=ti: t.matmul(
                                    bank[:],
                                    xlTg[:, cdi * 512 + ti * 128:
                                         cdi * 512 + (ti + 1) * 128],
                                    wa[1][:, cdi * D + nb * 512:
                                          cdi * D + (nb + 1) * 512],
                                    start=False, stop=(cdi == 7)))
                                for cdi in range(8)]
                            p.mm(t, emits,
                                 reads=("ones_bf", "brow", "xlTg", "wa1"),
                                 writes=(bnm,))

                    @blk.vector
                    def _(v):
                        for nb, bnm in enumerate(["psC", "psD"]):
                            bank = psC if nb == 0 else psD
                            p.op(v, "vt", lambda bank=bank, nb=nb, tile=tile:
                                 v.tensor_copy(
                                     out=Vt[:, tile * D + nb * 512:
                                            tile * D + (nb + 1) * 512],
                                     in_=bank[:]), reads=(bnm,), writes=("Vt",))

            @blk.gpsimd
            def _(g):
                p.dma(g, "gp", "gd", wa[0][:].rearrange("p (c d) -> p c d", c=8),
                      Wq_h.rearrange("(c p) d -> p c d", p=128), writes=("wa0",))

            for grp in range(NG):
                for ti in range(4):
                    gtok = toff + (grp * 4 + ti) * 128
                    par = ti % 2
                    xb, xn_ = xlg[0], "xlg0"
                    xsem = "x0" if par == 0 else "x1s"

                    @blk.sync
                    def _(sync):
                        p.dma(sync, "sync", xsem, xb[:],
                              xl2_h[gtok:gtok + 128, :], writes=(xn_,))

                    ptn = "ptA" if ti % 2 == 0 else "ptB"
                    pt = ptA if ti % 2 == 0 else ptB

                    @blk.tensor
                    def _(t):
                        p.mm(t, [
                            (lambda pt=pt, c=c, xb=xb: t.transpose(
                                pt[:, c * 128:(c + 1) * 128],
                                xb[:, c * 128:(c + 1) * 128],
                                C["ident_bf"][:])) for c in range(8)],
                            reads=(xn_, "s_ident_bf"), writes=(ptn,))

                    @blk.vector
                    def _(v):
                        p.op(v, "vt", lambda pt=pt, ti=ti: v.tensor_copy(
                            out=xlTg[:].rearrange("p (c x) -> p c x", c=8)[
                                :, :, ti * 128:(ti + 1) * 128],
                            in_=pt[:].rearrange("p (c x) -> p c x", c=8)),
                            reads=(ptn,), writes=("xlTg",))

                for wave in range(4):
                    @blk.tensor
                    def _(t):
                        for j in range(2):
                            oc = wave * 2 + j
                            bank = psE if j == 0 else psF
                            bnm = "psE" if j == 0 else "psF"
                            p.mm(t, [
                                (lambda bank=bank, oc=oc, cdi=cdi: t.matmul(
                                    bank[:],
                                    wa[0][:, cdi * D + oc * 128:
                                          cdi * D + (oc + 1) * 128],
                                    xlTg[:, cdi * 512:(cdi + 1) * 512],
                                    start=(cdi == 0), stop=(cdi == 7)))
                                for cdi in range(8)],
                                reads=("wa0", "xlTg"), writes=(bnm,))

                    @blk.vector
                    def _(v):
                        for j in range(2):
                            oc = wave * 2 + j
                            bank = psE if j == 0 else psF
                            bnm = "psE" if j == 0 else "psF"
                            p.op(v, "vt", lambda bank=bank, oc=oc: v.tensor_scalar_add(
                                out=Qg[:, (oc % 2) * 512:(oc % 2 + 1) * 512],
                                in0=bank[:], scalar1=bqc[:, oc:oc + 1]),
                                reads=(bnm, "bqc"), writes=("Qg",))

                    @blk.scalar
                    def _(s):
                        for j in range(2):
                            oc = wave * 2 + j
                            p.op(s, "at", lambda oc=oc: s.activation(
                                out=ksq[:], in_=Qg[:, (oc % 2) * 512:
                                                  (oc % 2 + 1) * 512],
                                func=AF.Square), reads=("Qg",), writes=("ksq",))

                            @blk.tensor
                            def _(t):
                                p.mm(t, [lambda oc=oc: t.matmul(
                                    psB[0:1, 0:512], onesc_bf[:], ksq[:],
                                    start=(oc == 0), stop=(oc == 7))],
                                    reads=("onesc_bf", "ksq"), writes=("psB",))

                    @blk.sync
                    def _(sync):
                        for j in range(2):
                            oc = wave * 2 + j
                            p.dma(sync, "sync", "st",
                                  qt_h[oc * 128:(oc + 1) * 128,
                                       grp * 512:(grp + 1) * 512],
                                  Qg[:, (oc % 2) * 512:(oc % 2 + 1) * 512],
                                  reads=("Qg",))

                @blk.vector
                def _(v):
                    p.op(v, "vt", lambda grp=grp: v.tensor_copy(
                        out=q2row[0:1, grp * 512:(grp + 1) * 512],
                        in_=psB[0:1, 0:512]), reads=("psB",), writes=("q2row",))

            for kb in range(NKB):
                for c in range(8):
                    @blk.scalar
                    def _(s):
                        p.op(s, "at", lambda c=c, kb=kb: s.activation(
                            out=ksq[:], in_=Kt[:, c * S + kb * 512:
                                              c * S + (kb + 1) * 512],
                            func=AF.Square), reads=("Kt",), writes=("ksq",))

                    @blk.tensor
                    def _(t):
                        p.mm(t, [lambda c=c: t.matmul(
                            psB[0:1, 0:512], onesc_bf[:], ksq[:],
                            start=(c == 0), stop=(c == 7))],
                            reads=("onesc_bf", "ksq"), writes=("psB",))

                @blk.vector
                def _(v):
                    p.op(v, "vt", lambda kb=kb: v.tensor_scalar_mul(
                        out=nk2h[0:1, kb * 512:(kb + 1) * 512],
                        in_=psB[0:1, 0:512], scalar1=-0.5)
                        if False else v.tensor_scalar_mul(
                        out=nk2h[0:1, kb * 512:(kb + 1) * 512],
                        in0=psB[0:1, 0:512], scalar1=-0.5),
                        reads=("psB",), writes=("nk2h",))

            @blk.gpsimd
            def _(g):
                p.dma(g, "gp", "gd", wa[0][:].rearrange("p (c d) -> p c d", c=8),
                      Wo_h.rearrange("(c p) d -> p c d", p=128), writes=("wa0",))

            for qb in range(NQB):
                gtok = toff + qb * 128

                @blk.sync
                def _(sync):
                    p.dma(sync, "sync", "x0",
                          Qg[:].rearrange("p (c x) -> p c x", c=8),
                          qt_h.rearrange("(c p) x -> p c x", p=128)[
                              :, :, qb * 128:(qb + 1) * 128],
                          writes=("Qg",))

                @blk.tensor
                def _(t):
                    p.mm(t, [lambda qb=qb: t.matmul(
                        psB[:, 128:129], q2row[0:1, qb * 128:(qb + 1) * 128],
                        onesc2[:], start=True, stop=True)],
                        reads=("q2row", "onesc2"), writes=("psB",))

                @blk.vector
                def _(v):
                    p.op(v, "vt", lambda: v.tensor_scalar_mul(
                        out=q2c[:, 0:1], in0=psB[:, 128:129], scalar1=-1.0),
                        reads=("psB",), writes=("q2c",))
                    p.op(v, "vt", lambda: v.memset(dcol[:], 0.0), writes=("dcol",))

                def emit_qk(kb):
                    par_b = kb % 2
                    qkb = psA if par_b == 0 else psC
                    qkn = "psA" if par_b == 0 else "psC"
                    vb_t, vb_n = (psB, "psB") if par_b == 0 else (psD, "psD")
                    Pb, Pn = Psb[par_b], f"PsbB{par_b}"

                    @blk.tensor
                    def _(t):
                        emits = [
                            (lambda qkb=qkb, kb=kb, cdi=cdi: t.matmul(
                                qkb[:],
                                Qg[:, cdi * 128:(cdi + 1) * 128],
                                Kt[:, cdi * S + kb * 512:cdi * S + (kb + 1) * 512],
                                start=(cdi == 0), stop=False))
                            for cdi in range(8)]
                        emits.append(lambda qkb=qkb, kb=kb: t.matmul(
                            qkb[:], ones_bf[:],
                            nk2h[0:1, kb * 512:(kb + 1) * 512],
                            start=False, stop=True))
                        p.mm(t, emits, reads=("Qg", "Kt", "ones_bf", "nk2h"),
                             writes=(qkn,))

                    @blk.scalar
                    def _(s):
                        p.op(s, "at", lambda qkb=qkb, vb_t=vb_t: s.activation(
                            out=vb_t[:], in_=qkb[:], func=AF.Exp,
                            scale=c_two[:], bias=q2c[:, 0:1]),
                            reads=(qkn, "c_two", "q2c"), writes=(vb_n,))
                        p.op(s, "at", lambda kb=kb, vb_t=vb_t, Pb=Pb: s.activation(
                            out=Pb[:], in_=vb_t[:], func=AF.Exp,
                            bias=c_m1[:],
                            accum_out=dcol[:, kb:kb + 1]),
                            reads=(vb_n, "c_m1"), writes=(Pn, "dcol"))

                def emit_ptpv(kb):
                    par_b = kb % 2
                    Pb, Pn = Psb[par_b], f"PsbB{par_b}"
                    PTb, PTn = PTs[par_b], f"PTs{par_b}"
                    ptb_t, ptb_n = (ptA, "ptA") if par_b == 0 else (ptB, "ptB")

                    @blk.tensor
                    def _(t):
                        p.mm(t, [
                            (lambda j=j, Pb=Pb, ptb_t=ptb_t: t.transpose(
                                ptb_t[:, j * 128:(j + 1) * 128],
                                Pb[:, j * 128:(j + 1) * 128],
                                C["ident_bf"][:])) for j in range(4)],
                            reads=(Pn, "s_ident_bf"), writes=(ptb_n,))

                    @blk.vector
                    def _(v):
                        p.op(v, "vt", lambda PTb=PTb, ptb_t=ptb_t: v.tensor_copy(
                            out=PTb[:], in_=ptb_t[:, 0:512]),
                            reads=(ptb_n,), writes=(PTn,))

                    @blk.tensor
                    def _(t):
                        for nb in range(2):
                            bank = psE if nb == 0 else psF
                            bnm = "psE" if nb == 0 else "psF"
                            p.mm(t, [
                                (lambda bank=bank, nb=nb, kb=kb, j=j, PTb=PTb:
                                 t.matmul(
                                    bank[:],
                                    PTb[:, j * 128:(j + 1) * 128],
                                    Vt[:, (kb * 4 + j) * D + nb * 512:
                                       (kb * 4 + j) * D + (nb + 1) * 512],
                                    start=(kb == 0 and j == 0),
                                    stop=(kb == NKB - 1 and j == 3)))
                                for j in range(4)],
                                reads=(PTn, "Vt"), writes=(bnm,))

                for kb in range(NKB + 1):
                    if kb < NKB:
                        emit_qk(kb)
                    if kb > 0:
                        emit_ptpv(kb - 1)

                @blk.vector
                def _(v):
                    p.op(v, "vt", lambda: v.reduce_sum(
                        out=rcpc[:], in_=dcol[:], axis=AX.X),
                        reads=("dcol",), writes=("rcpc",))
                    p.op(v, "vt", lambda: v.reciprocal(out=rcpc[:], in_=rcpc[:]),
                         reads=("rcpc",), writes=("rcpc",))
                    for nb, bnm in enumerate(["psE", "psF"]):
                        bank = psE if nb == 0 else psF
                        p.op(v, "vt", lambda bank=bank, nb=nb: v.tensor_scalar_mul(
                            out=attn[:, nb * 512:(nb + 1) * 512], in0=bank[:],
                            scalar1=rcpc[:]), reads=(bnm, "rcpc"), writes=("attn4",))

                @blk.tensor
                def _(t):
                    p.mm(t, [
                        (lambda c=c: t.transpose(
                            ptB[:, c * 128:(c + 1) * 128],
                            attn[:, c * 128:(c + 1) * 128],
                            C["ident_bf"][:])) for c in range(8)],
                        reads=("attn4", "s_ident_bf"), writes=("ptB",))

                @blk.vector
                def _(v):
                    p.op(v, "vt", lambda: v.tensor_copy(
                        out=attnT[:], in_=ptB[:]), reads=("ptB",), writes=("attnT4",))

                @blk.sync
                def _(sync):
                    p.dma(sync, "sync", "x1s", x3t[:], x2_h[gtok:gtok + 128, :],
                          writes=("x3t",))

                @blk.tensor
                def _(t):
                    for nb in range(2):
                        bank = psA if nb == 0 else psC
                        bnm = "psA" if nb == 0 else "psC"
                        emits = [(lambda bank=bank, nb=nb: t.matmul(
                            bank[:], ones_bf[:],
                            brow[0:1, D + nb * 512: D + (nb + 1) * 512],
                            start=True, stop=False))]
                        emits += [
                            (lambda bank=bank, nb=nb, cdi=cdi: t.matmul(
                                bank[:],
                                attnT[:, cdi * 128:(cdi + 1) * 128],
                                wa[0][:, cdi * D + nb * 512:
                                      cdi * D + (nb + 1) * 512],
                                start=False, stop=(cdi == 7)))
                            for cdi in range(8)]
                        p.mm(t, emits, reads=("ones_bf", "brow", "attnT4", "wa0"),
                             writes=(bnm,))

                @blk.vector
                def _(v):
                    for nb, bnm in enumerate(["psA", "psC"]):
                        bank = psA if nb == 0 else psC
                        p.op(v, "vt", lambda bank=bank, nb=nb: v.tensor_add(
                            out=x3t[:, nb * 512:(nb + 1) * 512],
                            in0=x3t[:, nb * 512:(nb + 1) * 512], in1=bank[:]),
                            reads=("x3t", bnm), writes=("x3t",))

                @blk.sync
                def _(sync):
                    p.dma(sync, "sync", "st", x3_h[gtok:gtok + 128, :], x3t[:],
                          reads=("x3t",))

        @blk.gpsimd
        def _(g):
            for s in ["st", "sd", "gd", "id", "x0", "x1s"]:
                g.wait_ge(p.sem[s], p.n[s])
        _p_sbuf_pop(p, "Kt")
    return p


def launch4a_inputs(core, x1, wg, y_full, full):
    d = {"x1": x1, "wg": wg, "y_all": y_full,
         "Wq": full["L1_attn_Wq"], "Wk": full["L1_attn_Wk"],
         "Wv": full["L1_attn_Wv"], "Wo": full["L1_attn_Wo"],
         "bq": full["L1_attn_bq"].reshape(1, D),
         "bk": full["L1_attn_bk"].reshape(1, D),
         "bv": full["L1_attn_bv"].reshape(1, D),
         "bo": full["L1_attn_bo"].reshape(1, D)}
    d.update(const_inputs(L4A_CONSTS))
    return d



L4B_CONSTS = ["ident_bf"]
NIC1 = I1 // 128
FFN_NB = [(0, 512), (512, 512), (1024, 512), (1536, 512), (2048, 512), (2560, 256)]


QCW = 1024


def _emit_quant(p, blk, q, src, src_n, dst, dst_n, W):
    NB = W // 16
    n = q["_n"]
    sg, xn, s5, s7 = q["sg"], q["xn"], q["g5"], q["g7"]
    bm, hs, rcp = q["bm"], q["hs"], q["rcp"]
    i32 = q["i32"]
    i32_n = q["_i32n"]
    cq = q["_cq"]

    @blk.vector
    def _(v):
        p.op(v, "vt", lambda: v.tensor_reduce(
            out=bm[:, 0:NB], in_=src.rearrange("p (b x) -> p b x", x=16),
            axis=AX.X, op=ALU.max, apply_absolute_value=True),
            reads=(src_n,), writes=(n + "bm",))
        p.op(v, "vt", lambda: v.tensor_scalar(
            out=hs[:, 0:NB], in0=bm[:, 0:NB], scalar1=1.0 / 12.0,
            scalar2=5e-13, op0=ALU.mult, op1=ALU.max),
            reads=(n + "bm",), writes=(n + "hs",))
        p.op(v, "vt", lambda: v.reciprocal(
            out=rcp[:, 0:NB], in_=hs[:, 0:NB]),
            reads=(n + "hs",), writes=(n + "rcp",))

    @blk.scalar
    def _(s):
        p.op(s, "at", lambda: s.sign(out=sg[:, 0:W], in_=src),
             reads=(src_n,), writes=(n + "sg",))
        p.op(s, "at", lambda: s.activation(out=src, in_=src, func=AF.Abs),
             reads=(src_n,), writes=(src_n,))

    @blk.vector
    def _(v):
        p.op(v, "vt", lambda: v.tensor_tensor(
            out=xn[:, 0:W], in0=src,
            in1=rcp[:, 0:NB].unsqueeze(2).to_broadcast([128, NB, 16]),
            op=ALU.mult), reads=(src_n, n + "rcp"), writes=(n + "xn",))

    @blk.scalar
    def _(s):
        p.op(s, "at", lambda: s.activation(
            out=s5[:, 0:W], in_=xn[:, 0:W], func=AF.Sign, bias=cq["m5"][:]),
            reads=(n + "xn", "c_qm5"), writes=(n + "g5",))
        p.op(s, "at", lambda: s.activation(
            out=s7[:, 0:W], in_=xn[:, 0:W], func=AF.Sign, bias=cq["m7"][:]),
            reads=(n + "xn", "c_qm7"), writes=(n + "g7",))

    @blk.vector
    def _(v):
        p.op(v, "vt", lambda: v.tensor_add(
            out=s5[:, 0:W], in0=s5[:, 0:W], in1=s7[:, 0:W]),
            reads=(n + "g5", n + "g7"), writes=(n + "g5",))

    @blk.scalar
    def _(s):
        p.op(s, "at", lambda: s.activation(
            out=s7[:, 0:W], in_=xn[:, 0:W], func=AF.Sign, bias=cq["m10"][:]),
            reads=(n + "xn", "c_qm10"), writes=(n + "g7",))

    @blk.vector
    def _(v):
        p.op(v, "vt", lambda: v.scalar_tensor_tensor(
            out=s5[:, 0:W], in0=s7[:, 0:W], scalar=2.0, in1=s5[:, 0:W],
            op0=ALU.mult, op1=ALU.add),
            reads=(n + "g5", n + "g7"), writes=(n + "g5",))

    @blk.scalar
    def _(s):
        p.op(s, "at", lambda: s.activation(
            out=s7[:, 0:W], in_=xn[:, 0:W], func=AF.Relu,
            scale=cq["m1"][:], bias=cq["c449"][:]),
            reads=(n + "xn", "c_qm1", "c_q449"), writes=(n + "g7",))
        p.op(s, "at", lambda: s.activation(
            out=i32[:, 0:W], in_=s7[:, 0:W], func=AF.Copy,
            scale=-1.0, bias=8.49),
            reads=(n + "g7", "c_qm1", "c_q849"), writes=(i32_n,))
        p.op(s, "at", lambda: s.activation(
            out=xn[:, 0:W], in_=i32[:, 0:W], func=AF.Copy),
            reads=(i32_n,), writes=(n + "xn",))

    @blk.vector
    def _(v):
        p.op(v, "vt", lambda: v.tensor_add(
            out=xn[:, 0:W], in0=xn[:, 0:W], in1=s5[:, 0:W]),
            reads=(n + "xn", n + "g5"), writes=(n + "xn",))
        p.op(v, "vt", lambda: v.tensor_tensor(
            out=sg[:, 0:W], in0=sg[:, 0:W],
            in1=hs[:, 0:NB].unsqueeze(2).to_broadcast([128, NB, 16]),
            op=ALU.mult), reads=(n + "sg", n + "hs"), writes=(n + "sg",))
        p.op(v, "vt", lambda: v.tensor_tensor(
            out=dst, in0=xn[:, 0:W], in1=sg[:, 0:W], op=ALU.mult),
            reads=(n + "xn", n + "sg"), writes=(dst_n,))


def _init_quant_consts(p, blk, qs):
    cq = qs[0]["_cq"]
    vals = {"m5": -5.0, "m7": -7.0, "m10": -10.0, "m1": -1.0,
            "c449": 4.49, "c849": 8.49}

    @blk.vector
    def _(v):
        for nm, val in vals.items():
            p.op(v, "vt", lambda nm=nm, val=val: v.memset(cq[nm][:], val),
                 writes=("c_q" + nm,))
    p.nc.const_aps.register(F32, -1.0, cq["m1"][:])
    p.nc.const_aps.register(F32, 8.49, cq["c849"][:])


def _emit_quant_chunks(p, blk, qs, src_t, src_n, dst_t, dst_n, W):
    off = 0
    while off < W:
        cw = min(QCW, W - off)
        q = qs[qs[-1] % 2]
        qs[-1] += 1
        _emit_quant(p, blk, q, src_t[:, off:off + cw], src_n,
                    dst_t[:, off:off + cw], dst_n, cw)
        off += cw


def _push_quant_scratch(p):
    i32 = _p_sbuf_push(p, "qi32", [128, QCW], mybir.dt.int32)
    cq = {}
    for nm in ["m5", "m7", "m10", "m1", "c449", "c849"]:
        cq[nm] = _p_sbuf_push(p, "c_q" + nm, [128, 1])
    sets = []
    for pref in ("q0_", "q1_"):
        q = {"_n": pref, "i32": i32, "_i32n": "qi32", "_cq": cq}
        for nm in ["sg", "xn", "g5", "g7"]:
            q[nm] = _p_sbuf_push(p, pref + nm, [128, QCW])
        for nm in ["bm", "hs", "rcp"]:
            q[nm] = _p_sbuf_push(p, pref + nm, [128, QCW // 16])
        sets.append(q)
    return sets + [0]


def build_launch4b():
    p = P("l4b")
    nc = p.nc
    x3_h = nc.dram_tensor("x3", [TPC, D], F32, kind="ExternalInput").ap()
    fg_h = nc.dram_tensor("f_gate", [I1, D], F32, kind="ExternalInput").ap()
    fu_h = nc.dram_tensor("f_up", [I1, D], F32, kind="ExternalInput").ap()
    fd_h = nc.dram_tensor("f_down", [D, I1], F32, kind="ExternalInput").ap()
    out_h = nc.dram_tensor("out", [TPC, D], F32, kind="ExternalOutput").ap()
    hq_h = nc.dram_tensor("hqsp", [TPC, I1], BF16).ap()

    hbs, C = _declare_consts(p, L4B_CONSTS)
    with p.ctx:
        for s in ["sd", "gd", "x0", "x1s", "pt", "vt", "at", "st", "gt"]:
            p.semaphore(s)
        p.DMA_SEMS = frozenset(["sd", "gd", "st", "x0", "x1s"])
        c_eps = p.sbuf("c_eps", [128, 1])
        c_invD = p.sbuf("c_invD", [128, 1])
        c_mhalf = p.sbuf("c_mhalf", [128, 1])
        nmean = p.sbuf("nmean", [128, 2])
        rstd = p.sbuf("rstd", [128, 2])

        ptA = p.psum("ptA", [128, 1024], BF16)
        ptB = p.psum("ptB", [128, 1024], BF16)
        psA = p.psum("psA", [128, 512])
        psB = p.psum("psB", [128, 512])
        psC = p.psum("psC", [128, 512])
        psD = p.psum("psD", [128, 512])

        blk = p.ctx.enter_context(nc.Block())
        _load_consts(p, blk, hbs, C)

        @blk.vector
        def _(v):
            p.op(v, "vt", lambda: v.memset(c_eps[:], LN_EPS), writes=("c_eps",))
            p.op(v, "vt", lambda: v.memset(c_invD[:], 1.0 / D), writes=("c_invD",))
            p.op(v, "vt", lambda: v.memset(c_mhalf[:], -0.5), writes=("c_mhalf",))

        qs = _push_quant_scratch(p)
        _init_quant_consts(p, blk, qs)
        wgT = _p_sbuf_push(p, "wgT", [128, 8 * I1], BF16)
        wuT = _p_sbuf_push(p, "wuT", [128, 8 * I1], BF16)
        wrow = [_p_sbuf_push(p, f"wrow{i}", [128, I1]) for i in range(2)]
        wqb2p = [_p_sbuf_push(p, f"wqb{i}", [128, I1], BF16) for i in range(2)]

        def prep_gate_up(w_h, dstT, dst_n):
            for r in range(NIC1):
                par = r % 2
                wb, wn = wrow[par], f"wrow{par}"
                xsem = "x0" if par == 0 else "x1s"

                @blk.sync
                def _(sync):
                    p.dma(sync, "sync", xsem, wb[:, 0:D],
                          w_h[r * 128:(r + 1) * 128, :], writes=(wn,))

                wqb, wqn = wqb2p[par], f"wqb{par}"
                _emit_quant_chunks(p, blk, qs, wb[:, 0:D], wn, wqb[:, 0:D], wqn, D)

                @blk.tensor
                def _(t):
                    p.mm(t, [
                        (lambda c=c, wqb=wqb: t.transpose(
                            ptA[:, c * 128:(c + 1) * 128],
                            wqb[:, c * 128:(c + 1) * 128],
                            C["ident_bf"][:])) for c in range(8)],
                        reads=(wqn, "s_ident_bf"), writes=("ptA",))

                @blk.vector
                def _(v):
                    p.op(v, "vt", lambda r=r: v.tensor_copy(
                        out=dstT[:].rearrange("p (c x) -> p c x", c=8)[
                            :, :, r * 128:(r + 1) * 128],
                        in_=ptA[:].rearrange("p (c x) -> p c x", c=8)),
                        reads=("ptA",), writes=(dst_n,))

        prep_gate_up(fg_h, wgT, "wgT")
        prep_gate_up(fu_h, wuT, "wuT")

        _p_sbuf_pop(p, "wrow0")

        x3t = [_p_sbuf_push(p, f"x3t{i}", [128, D]) for i in range(2)]
        xln3p = [_p_sbuf_push(p, f"xln3{i}", [128, D]) for i in range(2)]
        stats = _p_sbuf_push(p, "stats", [128, 2 * NT])
        xcb = _p_sbuf_push(p, "xcb", [128, D])
        sqs = _p_sbuf_push(p, "sqs", [128, D])
        for tile in range(NT):
            gtok = tile * 128
            par = tile % 2
            xb, xn_ = x3t[par], f"x3t{par}"
            xsem = "x0" if par == 0 else "x1s"

            @blk.sync
            def _(sync):
                p.dma(sync, "sync", xsem, xb[:], x3_h[gtok:gtok + 128, :],
                      writes=(xn_,))

            @blk.vector
            def _(v):
                p.op(v, "vt", lambda xb=xb, tile=tile: v.reduce_sum(
                    out=stats[:, 2 * tile:2 * tile + 1], in_=xb[:], axis=AX.X,
                    negate=True), reads=(xn_,), writes=("stats",))
                p.op(v, "vt", lambda tile=tile: v.tensor_scalar_mul(
                    out=stats[:, 2 * tile:2 * tile + 1],
                    in0=stats[:, 2 * tile:2 * tile + 1], scalar1=1.0 / D),
                    reads=("stats",), writes=("stats",))
                p.op(v, "vt", lambda xb=xb, tile=tile: v.tensor_scalar_add(
                    out=xcb[:], in0=xb[:],
                    scalar1=stats[:, 2 * tile:2 * tile + 1]),
                    reads=(xn_, "stats"), writes=("xcb",))

            @blk.scalar
            def _(s):
                p.op(s, "at", lambda tile=tile: s.activation(
                    out=sqs[:], in_=xcb[:], func=AF.Square,
                    accum_out=stats[:, 2 * tile + 1:2 * tile + 2]),
                    reads=("xcb",), writes=("sqs", "stats"))
                p.op(s, "at", lambda tile=tile: s.activation(
                    out=stats[:, 2 * tile + 1:2 * tile + 2],
                    in_=stats[:, 2 * tile + 1:2 * tile + 2], func=AF.Ln,
                    scale=c_invD[:], bias=c_eps[:]),
                    reads=("stats", "c_invD", "c_eps"), writes=("stats",))
                p.op(s, "at", lambda tile=tile: s.activation(
                    out=stats[:, 2 * tile + 1:2 * tile + 2],
                    in_=stats[:, 2 * tile + 1:2 * tile + 2], func=AF.Exp,
                    scale=c_mhalf[:]),
                    reads=("stats", "c_mhalf"), writes=("stats",))

        _p_sbuf_pop(p, "xcb")

        xqp = [_p_sbuf_push(p, f"xq{i}", [128, D], BF16) for i in range(2)]
        xqTp = [_p_sbuf_push(p, f"xqT{i}", [128, 8 * 128], BF16) for i in range(2)]
        gsp = [_p_sbuf_push(p, f"gs{i}", [128, 512]) for i in range(2)]
        hbufp = [_p_sbuf_push(p, f"hbuf{i}", [128, I1]) for i in range(2)]
        hqp = [_p_sbuf_push(p, f"hq{i}", [128, I1], BF16) for i in range(2)]

        for tile in range(NT):
            gtok = tile * 128
            par = tile % 2
            xb, xn_ = x3t[par], f"x3t{par}"
            xsem = "x0" if par == 0 else "x1s"
            xln3 = xln3p[par]
            xq = xqp[par]
            xqT = xqTp[par]
            gs = gsp[par]
            hbuf = hbufp[par]
            hq = hqp[par]
            nm_xln3, nm_xq, nm_xqT = f"xln3{par}", f"xq{par}", f"xqT{par}"
            nm_gs, nm_hbuf, nm_hq = f"gs{par}", f"hbuf{par}", f"hq{par}"

            @blk.sync
            def _(sync):
                p.dma(sync, "sync", xsem, xb[:], x3_h[gtok:gtok + 128, :],
                      writes=(xn_,))

            @blk.vector
            def _(v):
                p.op(v, "vt", lambda xb=xb, tile=tile, xln3=xln3: v.tensor_scalar(
                    out=xln3[:], in0=xb[:],
                    scalar1=stats[:, 2 * tile:2 * tile + 1],
                    scalar2=stats[:, 2 * tile + 1:2 * tile + 2],
                    op0=ALU.add, op1=ALU.mult),
                    reads=(xn_, "stats"), writes=(nm_xln3,))

            _emit_quant_chunks(p, blk, qs, xln3[:], nm_xln3, xq[:], nm_xq, D)

            ptx = ptA if par == 0 else ptB
            ptxn = "ptA" if par == 0 else "ptB"

            @blk.tensor
            def _(t):
                p.mm(t, [
                    (lambda c=c, xq=xq, ptx=ptx: t.transpose(
                        ptx[:, c * 128:(c + 1) * 128],
                        xq[:, c * 128:(c + 1) * 128],
                        C["ident_bf"][:])) for c in range(8)],
                    reads=(nm_xq, "s_ident_bf"), writes=(ptxn,))

            @blk.vector
            def _(v):
                p.op(v, "vt", lambda xqT=xqT, ptx=ptx: v.tensor_copy(
                    out=xqT[:], in_=ptx[:]), reads=(ptxn,), writes=(nm_xqT,))

            for inb, (noff, nw) in enumerate(FFN_NB):
                bankG = psA if inb % 2 == 0 else psC
                bG = "psA" if inb % 2 == 0 else "psC"
                bankU = psB if inb % 2 == 0 else psD
                bU = "psB" if inb % 2 == 0 else "psD"

                @blk.tensor
                def _(t):
                    p.mm(t, [
                        (lambda bankG=bankG, noff=noff, nw=nw, c=c, xqT=xqT:
                         t.matmul(
                            bankG[:, 0:nw],
                            xqT[:, c * 128:(c + 1) * 128],
                            wgT[:, c * I1 + noff: c * I1 + noff + nw],
                            start=(c == 0), stop=(c == 7))) for c in range(8)],
                        reads=(nm_xqT, "wgT"), writes=(bG,))
                    p.mm(t, [
                        (lambda bankU=bankU, noff=noff, nw=nw, c=c, xqT=xqT:
                         t.matmul(
                            bankU[:, 0:nw],
                            xqT[:, c * 128:(c + 1) * 128],
                            wuT[:, c * I1 + noff: c * I1 + noff + nw],
                            start=(c == 0), stop=(c == 7))) for c in range(8)],
                        reads=(nm_xqT, "wuT"), writes=(bU,))

                @blk.scalar
                def _(s):
                    p.op(s, "at", lambda bankG=bankG, nw=nw, gs=gs: s.activation(
                        out=gs[:, 0:nw], in_=bankG[:, 0:nw], func=AF.Silu),
                        reads=(bG,), writes=(nm_gs,))

                @blk.vector
                def _(v):
                    p.op(v, "vt", lambda bankU=bankU, noff=noff, nw=nw, gs=gs,
                         hbuf=hbuf: v.tensor_tensor(
                             out=hbuf[:, noff:noff + nw], in0=gs[:, 0:nw],
                             in1=bankU[:, 0:nw], op=ALU.mult),
                         reads=(nm_gs, bU), writes=(nm_hbuf,))

            _emit_quant_chunks(p, blk, qs, hbuf[:], nm_hbuf, hq[:], nm_hq, I1)

            @blk.sync
            def _(sync):
                p.dma(sync, "sync", "st", hq_h[gtok:gtok + 128, :], hq[:],
                      reads=(nm_hq,))

        _p_sbuf_pop(p, "x3t0")
        _p_sbuf_pop(p, "wgT")

        wdT2 = _p_sbuf_push(p, "wdT2", [128, NIC1 * D], BF16)
        hq2 = [_p_sbuf_push(p, f"hq2{i}", [128, I1], BF16) for i in range(2)]
        hqT = _p_sbuf_push(p, "hqT", [128, NIC1 * 128], BF16)
        x3r = _p_sbuf_push(p, "x3r", [128, D])
        outt = _p_sbuf_push(p, "outt", [128, D])

        wrow2 = _p_sbuf_push(p, "wrow2", [128, I1])
        wqb2 = _p_sbuf_push(p, "wqb2", [128, I1], BF16)
        for r in range(8):
            @blk.sync
            def _(sync):
                p.dma(sync, "sync", "x0", wrow2[:],
                      fd_h[r * 128:(r + 1) * 128, :], writes=("wrow2",))

            _emit_quant_chunks(p, blk, qs, wrow2[:], "wrow2", wqb2[:], "wqb2", I1)

            for bt in range(3):
                nt_ = min(8, NIC1 - bt * 8)

                @blk.tensor
                def _(t):
                    p.mm(t, [
                        (lambda bt=bt, j=j: t.transpose(
                            ptA[:, j * 128:(j + 1) * 128],
                            wqb2[:, (bt * 8 + j) * 128:(bt * 8 + j + 1) * 128],
                            C["ident_bf"][:])) for j in range(nt_)],
                        reads=("wqb2", "s_ident_bf"), writes=("ptA",))

                @blk.vector
                def _(v):
                    p.op(v, "vt", lambda r=r, bt=bt, nt_=nt_: v.tensor_copy(
                        out=wdT2[:].rearrange("p (c x) -> p c x", c=NIC1)[
                            :, bt * 8: bt * 8 + nt_, r * 128:(r + 1) * 128],
                        in_=ptA[:, 0:nt_ * 128].rearrange(
                            "p (c x) -> p c x", x=128)),
                        reads=("ptA",), writes=("wdT2",))

        for tile in range(NT):
            gtok = tile * 128
            par = tile % 2
            hb, hn = hq2[par], f"hq2{par}"
            xsem = "x0" if par == 0 else "x1s"

            @blk.sync
            def _(sync):
                p.dma(sync, "sync", xsem, hb[:], hq_h[gtok:gtok + 128, :],
                      writes=(hn,))
                p.dma(sync, "sync", xsem, x3r[:], x3_h[gtok:gtok + 128, :],
                      writes=("x3r",))

            for bt in range(3):
                nt_ = min(8, NIC1 - bt * 8)
                pt = ptA if bt % 2 == 0 else ptB
                ptn = "ptA" if bt % 2 == 0 else "ptB"

                @blk.tensor
                def _(t):
                    p.mm(t, [
                        (lambda pt=pt, bt=bt, j=j, hb=hb: t.transpose(
                            pt[:, j * 128:(j + 1) * 128],
                            hb[:, (bt * 8 + j) * 128:(bt * 8 + j + 1) * 128],
                            C["ident_bf"][:])) for j in range(nt_)],
                        reads=(hn, "s_ident_bf"), writes=(ptn,))

                @blk.vector
                def _(v):
                    p.op(v, "vt", lambda pt=pt, bt=bt, nt_=nt_: v.tensor_copy(
                        out=hqT[:, bt * 8 * 128: (bt * 8 + nt_) * 128],
                        in_=pt[:, 0:nt_ * 128]), reads=(ptn,), writes=("hqT",))

            @blk.tensor
            def _(t):
                for nb in range(2):
                    bank = psA if nb == 0 else psB
                    bnm = "psA" if nb == 0 else "psB"
                    p.mm(t, [
                        (lambda bank=bank, nb=nb, ci=ci: t.matmul(
                            bank[:],
                            hqT[:, ci * 128:(ci + 1) * 128],
                            wdT2[:, ci * D + nb * 512: ci * D + (nb + 1) * 512],
                            start=(ci == 0), stop=(ci == NIC1 - 1)))
                        for ci in range(NIC1)],
                        reads=("hqT", "wdT2"), writes=(bnm,))

            @blk.vector
            def _(v):
                for nb, bnm in enumerate(["psA", "psB"]):
                    bank = psA if nb == 0 else psB
                    p.op(v, "vt", lambda bank=bank, nb=nb: v.tensor_add(
                        out=outt[:, nb * 512:(nb + 1) * 512],
                        in0=x3r[:, nb * 512:(nb + 1) * 512], in1=bank[:]),
                        reads=("x3r", bnm), writes=("outt",))

            @blk.sync
            def _(sync):
                p.dma(sync, "sync", "st", out_h[gtok:gtok + 128, :], outt[:],
                      reads=("outt",))

        @blk.gpsimd
        def _(g):
            for s in ["st", "sd", "gd", "x0", "x1s"]:
                g.wait_ge(p.sem[s], p.n[s])
        _p_sbuf_pop(p, "wdT2")
        _p_sbuf_pop(p, "qi32")
    return p


def launch4b_inputs(core, x3, full):
    d = {"x3": x3, "f_gate": full["L1_ffn_w_gate"], "f_up": full["L1_ffn_w_up"],
         "f_down": full["L1_ffn_w_down"]}
    d.update(const_inputs(L4B_CONSTS))
    return d


# version 13
# speedup vs baseline: 1.6504x; 1.0550x over previous
import math
from contextlib import ExitStack

import numpy as np
import ml_dtypes
import concourse.bass as bass
import concourse.mybir as mybir
from concourse.bass_utils import run_bass_kernel_spmd
from concourse.bass_core_rust import IndirectOffsetOnAxis

F32 = mybir.dt.float32
BF16 = mybir.dt.bfloat16
U32 = mybir.dt.uint32
AF = mybir.ActivationFunctionType
ALU = mybir.AluOpType
AX = mybir.AxisListType

NCORES = 8
B, S, D = 16, 4096, 1024
TPC = B * S // NCORES
NT = TPC // 128
WIN, NW = 256, TPC // 256
NH, HD = 16, 64
NE, TOPK, CAP, EI = 64, 8, 256, 768
EPC = NE // NCORES
I1 = 2816
LN_EPS = 1e-5
BIGIDX = float(1 << 24)


def _host_consts():
    c = {}
    c["ident_f32"] = np.eye(128, dtype=np.float32)
    c["ident_bf"] = np.eye(128, dtype=np.float32)
    c["tstrictT"] = np.triu(np.ones((128, 128), np.float32), 1)
    c["ones_row"] = np.ones((1, 128), np.float32)
    c["ones_col"] = np.ones((128, 1), np.float32)
    c["iotaM"] = np.tile((np.arange(NE, dtype=np.float32) - BIGIDX)[None, :],
                         (128, TOPK)).reshape(128, TOPK * NE)
    c["iotaE"] = np.tile(np.arange(NE, dtype=np.float32), (128, TOPK))
    kk = (np.arange(TOPK)[None, :] < np.arange(TOPK)[:, None]).astype(np.float32)
    c["kkmask"] = np.tile(kk.reshape(1, TOPK * TOPK), (128, 1))
    return c

_CONST_DTYPES = {"ident_f32": F32, "ident_bf": BF16, "tstrictT": F32,
                 "ones_row": F32, "ones_col": F32, "iotaM": F32,
                 "iotaE": F32, "kkmask": F32}


class Buf:
    __slots__ = ("name", "w", "r")

    def __init__(self, name):
        self.name = name
        self.w = None
        self.r = []


class P:

    def __init__(self, name):
        self.nc = bass.Bass("TRN2", target_bir_lowering=False, num_devices=NCORES)
        self.ctx = ExitStack()
        self.name = name
        self.n = {}
        self.sem = {}
        self.bufs = {}
        self.seen = {}

    def semaphore(self, name):
        self.sem[name] = self.ctx.enter_context(self.nc.semaphore(name))
        self.n[name] = 0
        return self.sem[name]

    def sbuf(self, name, shape, dt=F32):
        t = self.ctx.enter_context(self.nc.sbuf_tensor(name, shape, dt))
        self.bufs[name] = Buf(name)
        return t

    def psum(self, name, shape, dt=F32):
        t = self.ctx.enter_context(self.nc.psum_tensor(name, shape, dt))
        self.bufs[name] = Buf(name)
        return t

    def buf(self, name):
        if name not in self.bufs:
            self.bufs[name] = Buf(name)
        return self.bufs[name]

    def _wait(self, eng, ekey, sem_name, val):
        if val <= 0:
            return
        seen = self.seen.setdefault(ekey, {})
        if seen.get(sem_name, -1) >= val:
            return
        eng.wait_ge(self.sem[sem_name], val)
        seen[sem_name] = val

    DMA_SEMS = frozenset(["sd", "gd", "st", "x0", "x1", "ld0", "ld1", "ld2",
                          "id", "zs"])

    def _dep_waits(self, eng, ekey, reads, writes, is_pe=False):
        todo = {}

        def add(s, v):
            if s in self.DMA_SEMS:
                v = self.n[s]
            todo[s] = max(todo.get(s, 0), v)

        for bn in reads:
            b = self.buf(bn)
            if b.w is not None:
                s, v = b.w
                if is_pe and s == "pt":
                    continue
                add(s, v)
        for bn in writes:
            b = self.buf(bn)
            if b.w is not None:
                s, v = b.w
                if not (is_pe and s == "pt"):
                    add(s, v)
            for s, v in b.r:
                if is_pe and s == "pt":
                    continue
                add(s, v)
        for s, v in sorted(todo.items()):
            self._wait(eng, ekey, s, v)

    def _record(self, reads, writes, stamp):
        for bn in writes:
            b = self.buf(bn)
            b.w = stamp
            b.r = []
        for bn in reads:
            self.buf(bn).r.append(stamp)

    def op(self, eng, tok, emit, reads=(), writes=()):
        self._dep_waits(eng, tok, reads, writes)
        ins = emit()
        ins.then_inc(self.sem[tok], 1)
        self.n[tok] += 1
        self._record(reads, writes, (tok, self.n[tok]))
        return self.n[tok]

    def mm(self, eng, emit_list, reads=(), writes=()):
        self._dep_waits(eng, "pe", reads, writes, is_pe=True)
        ins = None
        for emit in emit_list:
            ins = emit()
        ins.then_inc(self.sem["pt"], 1)
        self.n["pt"] += 1
        self._record(reads, writes, ("pt", self.n["pt"]))
        return self.n["pt"]

    def dma(self, eng, ekey, sem_name, out, in_, reads=(), writes=(), **kw):
        self._dep_waits(eng, ekey, reads, writes)
        ins = eng.dma_start(out=out, in_=in_, **kw)
        ins.then_inc(self.sem[sem_name], 16)
        self.n[sem_name] += 16
        self._record(reads, writes, (sem_name, self.n[sem_name]))
        return self.n[sem_name]

    def idma(self, eng, ekey, sem_name, reads=(), writes=(), **kw):
        self._dep_waits(eng, ekey, reads, writes)
        ins = eng.indirect_dma_start(**kw)
        ins.then_inc(self.sem[sem_name], 16)
        self.n[sem_name] += 16
        self._record(reads, writes, (sem_name, self.n[sem_name]))
        return self.n[sem_name]


def _declare_consts(p, consts_needed):
    hbs, tiles = {}, {}
    for nm in consts_needed:
        shape = list(_host_consts()[nm].shape)
        hbs[nm] = p.nc.dram_tensor("c_" + nm, shape, F32, kind="ExternalInput").ap()
        tiles[nm] = p.sbuf("s_" + nm, shape, _CONST_DTYPES[nm])
    return hbs, tiles


def _load_consts(p, blk, hbs, tiles):
    @blk.sync
    def _(sync):
        for nm, hb in hbs.items():
            if _CONST_DTYPES[nm] == F32:
                p.dma(sync, "sync", "sd", tiles[nm][:], hb, writes=("s_" + nm,))

    @blk.gpsimd
    def _(g):
        for nm, hb in hbs.items():
            if _CONST_DTYPES[nm] != F32:
                p.dma(g, "gp", "gd", tiles[nm][:], hb, writes=("s_" + nm,))


def const_inputs(consts_needed):
    hc = _host_consts()
    return {"c_" + nm: hc[nm] for nm in consts_needed}



L1_CONSTS = ["ident_f32", "ident_bf", "tstrictT", "ones_row", "ones_col",
             "iotaM", "iotaE", "kkmask"]


def build_launch1(nwin=NW, lvl=99, hlvl=99):
    p = P("l1")
    nc = p.nc
    x_h = nc.dram_tensor("x", [TPC, D], F32, kind="ExternalInput").ap()
    qw_h = nc.dram_tensor("q_w", [D, D], F32, kind="ExternalInput").ap()
    kvw_h = nc.dram_tensor("kv_w", [2 * HD, D], F32, kind="ExternalInput").ap()
    pw_h = nc.dram_tensor("proj_w", [D, D], F32, kind="ExternalInput").ap()
    pb_h = nc.dram_tensor("proj_b", [1, D], F32, kind="ExternalInput").ap()
    gw_h = nc.dram_tensor("gate_w", [NE, D], F32, kind="ExternalInput").ap()

    x1_h = nc.dram_tensor("x1", [TPC, D], F32, kind="ExternalOutput").ap()
    xln1_h = nc.dram_tensor("xln1b", [TPC, D], BF16, kind="ExternalOutput").ap()
    sdat_h = nc.dram_tensor("slotdat", [TPC, 3 * TOPK], F32, kind="ExternalOutput").ap()
    cnt_h = nc.dram_tensor("cnt", [1, NE], F32, kind="ExternalOutput").ap()

    hbs, C = _declare_consts(p, L1_CONSTS)

    with p.ctx:
        for s in ["sd", "gd", "x0", "x1", "pt", "vt", "at", "st"]:
            p.semaphore(s)
        qw_bf = p.sbuf("qw_bf", [128, 8 * D], BF16)
        qwT = p.sbuf("qwT", [128, 8 * D], BF16)
        pwT = p.sbuf("pwT", [128, 8 * D], BF16)
        kv_bf = p.sbuf("kv_bf", [128, D], BF16)
        kvwT = p.sbuf("kvwT", [128, 8 * 128], BF16)
        gw_f = p.sbuf("gw_f", [NE, D], F32)
        gwT = p.sbuf("gwT", [128, 8 * NE], F32)
        pb_bf = p.sbuf("pb_bf", [1, D], BF16)
        ones_bf = p.sbuf("ones_bf", [1, 128], BF16)
        xw = [p.sbuf(f"xw{i}", [128, 2 * D]) for i in range(2)]
        xc = p.sbuf("xc", [128, 2 * D])
        sqs = p.sbuf("sqs", [128, D])
        nmean = p.sbuf("nmean", [128, 2])
        rstd = p.sbuf("rstd", [128, 2])
        xlnb = p.sbuf("xlnb", [128, 2 * D], BF16)
        xlnT = p.sbuf("xlnT", [128, 8 * WIN], BF16)
        qT = p.sbuf("qTs", [64, NH * WIN], BF16)
        kvsb = p.sbuf("kvsb", [64, 2 * WIN], BF16)
        vtok = p.sbuf("vtok", [128, 2 * HD], BF16)
        Psb = p.sbuf("Psb", [128, 4 * WIN], BF16)
        PTsb = p.sbuf("PTsb", [128, 8 * 128], BF16)
        nmax = p.sbuf("nmax", [128, 4])
        rcp = p.sbuf("rcp", [128, NH])
        attn = p.sbuf("attn", [128, D], BF16)
        attnT = p.sbuf("attnT", [128, 8 * 128], BF16)
        x1sb = p.sbuf("x1sb", [128, 2 * D])
        xc1 = p.sbuf("xc1", [128, D])
        rstd1 = p.sbuf("rstd1", [128, 1])
        xln1sb = p.sbuf("xln1sb", [128, D], BF16)
        xc1T = p.sbuf("xc1T", [128, 8 * 128])
        lgt = p.sbuf("lgt", [128, NE])
        top8 = p.sbuf("top8", [128, TOPK])
        nmax0 = p.sbuf("nmax0", [128, 1])
        sd_t = p.sbuf("sd_t", [128, 3 * TOPK])
        eq3 = p.sbuf("eq3", [128, TOPK * NE])
        oh3 = p.sbuf("oh3", [128, TOPK * NE])
        cntT = p.sbuf("cntT", [128, NE])
        scr8 = p.sbuf("scr8", [128, TOPK])
        scr64 = p.sbuf("scr64", [128, TOPK * TOPK])
        base_sb = p.sbuf("base_sb", [128, NE])
        carry = p.sbuf("carry", [1, NE])
        c_eps = p.sbuf("c_eps", [128, 1])
        c_invD = p.sbuf("c_invD", [128, 1])
        c_mhalf = p.sbuf("c_mhalf", [128, 1])
        c_qs = p.sbuf("c_qs", [128, 1])

        ptA = p.psum("ptA", [128, 1024], BF16)
        ptB = p.psum("ptB", [128, 1024], BF16)
        psA = p.psum("psA", [128, 512])
        psB = p.psum("psB", [128, 512])
        psC = p.psum("psC", [128, 512])
        psD = p.psum("psD", [128, 512])
        psS = p.psum("psS", [128, 1024])

        blk = p.ctx.enter_context(nc.Block())
        _load_consts(p, blk, hbs, C)

        @blk.gpsimd
        def _(g):
            p.dma(g, "gp", "gd", qw_bf[:].rearrange("p (c d) -> p c d", c=8),
                  qw_h.rearrange("(c p) d -> p c d", p=128), writes=("qw_bf",))
            p.dma(g, "gp", "gd", kv_bf[:], kvw_h, writes=("kv_bf",))
            p.dma(g, "gp", "gd", pb_bf[:], pb_h, writes=("pb_bf",))
            p.dma(g, "gp", "gd", ones_bf[:], hbs["ones_row"], writes=("ones_bf",))

        @blk.sync
        def _(sync):
            p.dma(sync, "sync", "sd", gw_f[:], gw_h, writes=("gw_f",))

        for cdi in range(8):
            @blk.tensor
            def _(t):
                p.mm(t, [
                    (lambda cdo=cdo, cdi=cdi: t.transpose(
                        ptA[:, cdo * 128:(cdo + 1) * 128],
                        qw_bf[:, cdo * D + cdi * 128: cdo * D + (cdi + 1) * 128],
                        C["ident_bf"][:])) for cdo in range(8)],
                    reads=("qw_bf", "s_ident_bf"), writes=("ptA",))

            @blk.vector
            def _(v):
                p.op(v, "vt", lambda cdi=cdi: v.tensor_copy(
                    out=qwT[:, cdi * D:(cdi + 1) * D], in_=ptA[:]),
                    reads=("ptA",), writes=("qwT",))

        @blk.tensor
        def _(t):
            p.mm(t, [
                (lambda c=c: t.transpose(
                    ptB[:, c * 128:(c + 1) * 128],
                    kv_bf[:, c * 128:(c + 1) * 128],
                    C["ident_bf"][:])) for c in range(8)],
                reads=("kv_bf", "s_ident_bf"), writes=("ptB",))

        @blk.vector
        def _(v):
            p.op(v, "vt", lambda: v.tensor_copy(out=kvwT[:], in_=ptB[:]),
                 reads=("ptB",), writes=("kvwT",))
            p.op(v, "vt", lambda: v.memset(carry[:], 0.0), writes=("carry",))
            p.op(v, "vt", lambda: v.memset(c_eps[:], LN_EPS), writes=("c_eps",))
            p.op(v, "vt", lambda: v.memset(c_invD[:], 1.0 / D), writes=("c_invD",))
            p.op(v, "vt", lambda: v.memset(c_mhalf[:], -0.5), writes=("c_mhalf",))
            p.op(v, "vt", lambda: v.memset(c_qs[:], float(HD) ** -0.5), writes=("c_qs",))

        @blk.gpsimd
        def _(g):
            p.dma(g, "gp", "gd", qw_bf[:].rearrange("p (c d) -> p c d", c=8),
                  pw_h.rearrange("(c p) d -> p c d", p=128), writes=("qw_bf",))

        for cdi in range(8):
            @blk.tensor
            def _(t):
                p.mm(t, [
                    (lambda cdo=cdo, cdi=cdi: t.transpose(
                        ptA[:, cdo * 128:(cdo + 1) * 128],
                        qw_bf[:, cdo * D + cdi * 128: cdo * D + (cdi + 1) * 128],
                        C["ident_bf"][:])) for cdo in range(8)],
                    reads=("qw_bf", "s_ident_bf"), writes=("ptA",))

            @blk.vector
            def _(v):
                p.op(v, "vt", lambda cdi=cdi: v.tensor_copy(
                    out=pwT[:, cdi * D:(cdi + 1) * D], in_=ptA[:]),
                    reads=("ptA",), writes=("pwT",))

        @blk.tensor
        def _(t):
            p.mm(t, [
                (lambda c=c: t.transpose(
                    psD[:, c * 64:(c + 1) * 64],
                    gw_f[0:64, c * 128:(c + 1) * 128],
                    C["ident_f32"][0:64, 0:64])) for c in range(8)],
                reads=("gw_f", "s_ident_f32"), writes=("psD",))

        @blk.vector
        def _(v):
            p.op(v, "vt", lambda: v.tensor_copy(out=gwT[:], in_=psD[:]),
                 reads=("psD",), writes=("gwT",))

        for w in range(nwin):
            par = w % 2
            xwb = xw[par]
            xwn = f"xw{par}"
            xsem = "x0" if par == 0 else "x1"

            @blk.sync
            def _(sync):
                p.dma(sync, "sync", xsem,
                      xwb[:].rearrange("p (t d) -> p t d", t=2),
                      x_h.rearrange("(w t p) d -> w p t d", t=2, p=128)[w],
                      writes=(xwn,))

            if lvl < 1:
                continue
            @blk.vector
            def _(v):
                for i in range(2):
                    sl = slice(i * D, (i + 1) * D)
                    p.op(v, "vt", lambda sl=sl, i=i: v.reduce_sum(
                        out=nmean[:, i:i + 1], in_=xwb[:, sl], axis=AX.X,
                        negate=True), reads=(xwn,), writes=("nmean",))
                    p.op(v, "vt", lambda i=i: v.tensor_scalar_mul(
                        out=nmean[:, i:i + 1], in0=nmean[:, i:i + 1],
                        scalar1=1.0 / D), reads=("nmean",), writes=("nmean",))
                    p.op(v, "vt", lambda sl=sl, i=i: v.tensor_scalar_add(
                        out=xc[:, sl], in0=xwb[:, sl], scalar1=nmean[:, i:i + 1]),
                        reads=(xwn, "nmean"), writes=("xc",))

            @blk.scalar
            def _(s):
                for i in range(2):
                    sl = slice(i * D, (i + 1) * D)
                    p.op(s, "at", lambda sl=sl, i=i: s.activation(
                        out=sqs[:], in_=xc[:, sl], func=AF.Square,
                        accum_out=rstd[:, i:i + 1]),
                        reads=("xc",), writes=("sqs", "rstd"))
                    p.op(s, "at", lambda i=i: s.activation(
                        out=rstd[:, i:i + 1], in_=rstd[:, i:i + 1], func=AF.Ln,
                        scale=c_invD[:], bias=c_eps[:]),
                        reads=("rstd", "c_invD", "c_eps"), writes=("rstd",))
                    p.op(s, "at", lambda i=i: s.activation(
                        out=rstd[:, i:i + 1], in_=rstd[:, i:i + 1], func=AF.Exp,
                        scale=c_mhalf[:]), reads=("rstd", "c_mhalf"), writes=("rstd",))

            @blk.vector
            def _(v):
                for i in range(2):
                    sl = slice(i * D, (i + 1) * D)
                    p.op(v, "vt", lambda sl=sl, i=i: v.tensor_scalar_mul(
                        out=xlnb[:, sl], in0=xc[:, sl], scalar1=rstd[:, i:i + 1]),
                        reads=("xc", "rstd"), writes=("xlnb",))

            if lvl < 2:
                continue
            for i in range(2):
                ptn = "ptA" if i == 0 else "ptB"
                pt = ptA if i == 0 else ptB

                @blk.tensor
                def _(t):
                    p.mm(t, [
                        (lambda pt=pt, i=i, c=c: t.transpose(
                            pt[:, c * 128:(c + 1) * 128],
                            xlnb[:, i * D + c * 128: i * D + (c + 1) * 128],
                            C["ident_bf"][:])) for c in range(8)],
                        reads=("xlnb", "s_ident_bf"), writes=(ptn,))

                @blk.vector
                def _(v):
                    p.op(v, "vt", lambda pt=pt, i=i: v.tensor_copy(
                        out=xlnT[:].rearrange("p (c x) -> p c x", c=8)[
                            :, :, i * 128:(i + 1) * 128],
                        in_=pt[:].rearrange("p (c x) -> p c x", c=8)),
                        reads=(ptn,), writes=("xlnT",))

            if lvl < 3:
                continue
            for half in range(8):
                bank = psA if half % 2 == 0 else psB
                bn = "psA" if half % 2 == 0 else "psB"

                @blk.tensor
                def _(t):
                    for j in range(2):
                        h = half * 2 + j
                        p.mm(t, [
                            (lambda bank=bank, j=j, h=h, cdi=cdi: t.matmul(
                                bank[0:64, j * 256:(j + 1) * 256],
                                qwT[:, cdi * D + h * 64: cdi * D + h * 64 + 64],
                                xlnT[:, cdi * WIN:(cdi + 1) * WIN],
                                start=(cdi == 0), stop=(cdi == 7)))
                            for cdi in range(8)],
                            reads=("qwT", "xlnT"), writes=(bn,))

                @blk.scalar
                def _(s):
                    p.op(s, "at", lambda bank=bank, half=half: s.activation(
                        out=qT[0:64, half * 512:(half + 1) * 512],
                        in_=bank[0:64, :],
                        func=AF.Copy, scale=c_qs[0:64, :]),
                        reads=(bn, "c_qs"), writes=("qTs",))

            if lvl < 4:
                continue
            @blk.tensor
            def _(t):
                for kvh in range(2):
                    p.mm(t, [
                        (lambda kvh=kvh, cdi=cdi: t.matmul(
                            psC[0:64, kvh * 256:(kvh + 1) * 256],
                            kvwT[:, cdi * 128 + kvh * 64:
                                 cdi * 128 + kvh * 64 + 64],
                            xlnT[:, cdi * WIN:(cdi + 1) * WIN],
                            start=(cdi == 0), stop=(cdi == 7)))
                        for cdi in range(8)],
                        reads=("kvwT", "xlnT"), writes=("psC",))

            @blk.vector
            def _(v):
                p.op(v, "vt", lambda: v.tensor_copy(
                    out=kvsb[:], in_=psC[0:64, :]), reads=("psC",), writes=("kvsb",))

            if lvl < 5:
                continue
            @blk.tensor
            def _(t):
                p.mm(t, [
                    (lambda ktc=ktc: t.transpose(
                        ptA[:, ktc * 64:(ktc + 1) * 64],
                        kvsb[0:64, 256 + ktc * 128: 256 + (ktc + 1) * 128],
                        C["ident_bf"][0:64, 0:64])) for ktc in range(2)],
                    reads=("kvsb", "s_ident_bf"), writes=("ptA",))

            @blk.vector
            def _(v):
                p.op(v, "vt", lambda: v.tensor_copy(
                    out=vtok[:], in_=ptA[:, 0:128]), reads=("ptA",), writes=("vtok",))

            if lvl < 6:
                continue
            for qc in range(2):
                for hg in range(4 if lvl >= 7 else 0):
                    @blk.tensor
                    def _(t):
                        for j in range(4):
                            h = hg * 4 + j
                            p.mm(t, [lambda h=h, j=j, qc=qc: t.matmul(
                                psS[:, j * 256:(j + 1) * 256],
                                qT[0:64, h * WIN + qc * 128: h * WIN + qc * 128 + 128],
                                kvsb[0:64, 0:256],
                                start=True, stop=True)],
                                reads=("qTs", "kvsb"), writes=("psS",))

                    if hlvl < 2:
                        continue
                    @blk.vector
                    def _(v):
                        p.op(v, "vt", lambda: v.reduce_max(
                            out=nmax[:], in_=psS[:].rearrange(
                                "p (g k) -> p g k", g=4),
                            axis=AX.X, negate=True),
                            reads=("psS",), writes=("nmax",))
                        if hlvl < 3:
                            return
                        p.op(v, "vt", lambda: v.tensor_tensor(
                            out=psS[:], in0=psS[:],
                            in1=nmax[:].unsqueeze(2).to_broadcast(
                                [128, 4, 256]), op=ALU.add),
                            reads=("psS", "nmax"), writes=("psS",))

                    if hlvl < 4:
                        continue
                    @blk.scalar
                    def _(s):
                        p.op(s, "at", lambda: s.activation(
                            out=Psb[:], in_=psS[:], func=AF.Exp),
                            reads=("psS",), writes=("Psb",))

                    if hlvl < 5:
                        continue
                    @blk.vector
                    def _(v):
                        p.op(v, "vt", lambda hg=hg: v.reduce_sum(
                            out=rcp[:, hg * 4:(hg + 1) * 4],
                            in_=Psb[:].rearrange("p (g k) -> p g k", g=4),
                            axis=AX.X), reads=("Psb",), writes=("rcp",))
                        p.op(v, "vt", lambda hg=hg: v.reciprocal(
                            out=rcp[:, hg * 4:(hg + 1) * 4],
                            in_=rcp[:, hg * 4:(hg + 1) * 4]),
                            reads=("rcp",), writes=("rcp",))

                    if hlvl < 6:
                        continue
                    @blk.tensor
                    def _(t):
                        p.mm(t, [
                            (lambda j=j, ktc=ktc: t.transpose(
                                ptB[:, (j * 2 + ktc) * 128:(j * 2 + ktc + 1) * 128],
                                Psb[:, j * 256 + ktc * 128:
                                    j * 256 + ktc * 128 + 128],
                                C["ident_bf"][:]))
                            for j in range(4) for ktc in range(2)],
                            reads=("Psb", "s_ident_bf"), writes=("ptB",))

                    if hlvl < 7:
                        continue
                    @blk.vector
                    def _(v):
                        p.op(v, "vt", lambda: v.tensor_copy(
                            out=PTsb[:], in_=ptB[:]), reads=("ptB",), writes=("PTsb",))

                    if hlvl < 8:
                        continue
                    @blk.tensor
                    def _(t):
                        for j in range(4):
                            h = hg * 4 + j
                            bank = psA if h < 8 else psB
                            bn = "psA" if h < 8 else "psB"
                            col = (h % 8) * 64
                            p.mm(t, [
                                (lambda bank=bank, col=col, j=j, ktc=ktc: t.matmul(
                                    bank[:, col:col + 64],
                                    PTsb[:, (j * 2 + ktc) * 128:
                                         (j * 2 + ktc + 1) * 128],
                                    vtok[:, ktc * 64:(ktc + 1) * 64],
                                    start=(ktc == 0), stop=(ktc == 1)))
                                for ktc in range(2)],
                                reads=("PTsb", "vtok"), writes=(bn,))

                if lvl < 8:
                    continue
                @blk.vector
                def _(v):
                    for bi, (bank, bn) in enumerate([(psA, "psA"), (psB, "psB")]):
                        p.op(v, "vt", lambda bank=bank, bi=bi: v.tensor_tensor(
                            out=attn[:, bi * 512:(bi + 1) * 512], in0=bank[:],
                            in1=rcp[:, bi * 8:(bi + 1) * 8].unsqueeze(2).to_broadcast([128, 8, 64]),
                            op=ALU.mult), reads=(bn, "rcp"), writes=("attn",))

                if lvl < 9:
                    continue
                @blk.tensor
                def _(t):
                    p.mm(t, [
                        (lambda c=c: t.transpose(
                            ptA[:, c * 128:(c + 1) * 128],
                            attn[:, c * 128:(c + 1) * 128],
                            C["ident_bf"][:])) for c in range(8)],
                        reads=("attn", "s_ident_bf"), writes=("ptA",))

                @blk.vector
                def _(v):
                    p.op(v, "vt", lambda: v.tensor_copy(
                        out=attnT[:], in_=ptA[:]), reads=("ptA",), writes=("attnT",))

                @blk.tensor
                def _(t):
                    for nb in range(2):
                        bank = psA if nb == 0 else psB
                        bn = "psA" if nb == 0 else "psB"
                        emits = [(lambda bank=bank, nb=nb: t.matmul(
                            bank[:], ones_bf[:],
                            pb_bf[:, nb * 512:(nb + 1) * 512],
                            start=True, stop=False))]
                        emits += [
                            (lambda bank=bank, nb=nb, cdi=cdi: t.matmul(
                                bank[:],
                                attnT[:, cdi * 128:(cdi + 1) * 128],
                                pwT[:, cdi * D + nb * 512: cdi * D + (nb + 1) * 512],
                                start=False, stop=(cdi == 7)))
                            for cdi in range(8)]
                        p.mm(t, emits, reads=("ones_bf", "pb_bf", "attnT", "pwT"),
                             writes=(bn,))

                @blk.vector
                def _(v):
                    for nb, (bank, bn) in enumerate([(psA, "psA"), (psB, "psB")]):
                        p.op(v, "vt", lambda bank=bank, nb=nb: v.tensor_add(
                            out=x1sb[:, qc * D + nb * 512: qc * D + (nb + 1) * 512],
                            in0=xwb[:, qc * D + nb * 512: qc * D + (nb + 1) * 512],
                            in1=bank[:]), reads=(xwn, bn), writes=("x1sb",))

                if lvl < 10:
                    continue
                tile = 2 * w + qc
                sl = slice(qc * D, (qc + 1) * D)

                @blk.vector
                def _(v):
                    p.op(v, "vt", lambda: v.reduce_sum(
                        out=nmean[:, 0:1], in_=x1sb[:, sl], axis=AX.X,
                        negate=True), reads=("x1sb",), writes=("nmean",))
                    p.op(v, "vt", lambda: v.tensor_scalar_mul(
                        out=nmean[:, 0:1], in0=nmean[:, 0:1], scalar1=1.0 / D),
                        reads=("nmean",), writes=("nmean",))
                    p.op(v, "vt", lambda: v.tensor_scalar_add(
                        out=xc1[:], in0=x1sb[:, sl], scalar1=nmean[:, 0:1]),
                        reads=("x1sb", "nmean"), writes=("xc1",))

                @blk.scalar
                def _(s):
                    p.op(s, "at", lambda: s.activation(
                        out=sqs[:], in_=xc1[:], func=AF.Square,
                        accum_out=rstd1[:]), reads=("xc1",), writes=("sqs", "rstd1"))
                    p.op(s, "at", lambda: s.activation(
                        out=rstd1[:], in_=rstd1[:], func=AF.Ln,
                        scale=c_invD[:], bias=c_eps[:]),
                        reads=("rstd1", "c_invD", "c_eps"), writes=("rstd1",))
                    p.op(s, "at", lambda: s.activation(
                        out=rstd1[:], in_=rstd1[:], func=AF.Exp, scale=c_mhalf[:]),
                        reads=("rstd1", "c_mhalf"), writes=("rstd1",))

                @blk.vector
                def _(v):
                    p.op(v, "vt", lambda: v.tensor_scalar_mul(
                        out=xln1sb[:], in0=xc1[:], scalar1=rstd1[:]),
                        reads=("xc1", "rstd1"), writes=("xln1sb",))

                @blk.sync
                def _(sync):
                    p.dma(sync, "sync", "st",
                          x1_h[tile * 128:(tile + 1) * 128, :], x1sb[:, sl],
                          reads=("x1sb",))
                    p.dma(sync, "sync", "st",
                          xln1_h[tile * 128:(tile + 1) * 128, :], xln1sb[:],
                          reads=("xln1sb",))

                if lvl < 11:
                    continue
                @blk.tensor
                def _(t):
                    for bi, (bank, bn) in enumerate([(psC, "psC"), (psD, "psD")]):
                        p.mm(t, [
                            (lambda bank=bank, bi=bi, c=c: t.transpose(
                                bank[:, c * 128:(c + 1) * 128],
                                xc1[:, (bi * 4 + c) * 128:(bi * 4 + c + 1) * 128],
                                C["ident_f32"][:])) for c in range(4)],
                            reads=("xc1", "s_ident_f32"), writes=(bn,))

                @blk.vector
                def _(v):
                    for bi, (bank, bn) in enumerate([(psC, "psC"), (psD, "psD")]):
                        p.op(v, "vt", lambda bank=bank, bi=bi: v.tensor_copy(
                            out=xc1T[:, bi * 512:(bi + 1) * 512], in_=bank[:]),
                            reads=(bn,), writes=("xc1T",))

                @blk.tensor
                def _(t):
                    p.mm(t, [
                        (lambda cdi=cdi: t.matmul(
                            psC[:, 0:64],
                            xc1T[:, cdi * 128:(cdi + 1) * 128],
                            gwT[:, cdi * NE:(cdi + 1) * NE],
                            start=(cdi == 0), stop=(cdi == 7))) for cdi in range(8)],
                        reads=("xc1T", "gwT"), writes=("psC",))

                @blk.vector
                def _(v):
                    p.op(v, "vt", lambda: v.tensor_scalar_mul(
                        out=lgt[:], in0=psC[:, 0:64], scalar1=rstd1[:]),
                        reads=("psC", "rstd1"), writes=("lgt",))
                    p.op(v, "vt", lambda: v.max(out=top8[:], in_=lgt[:]),
                         reads=("lgt",), writes=("top8",))
                    p.op(v, "vt", lambda: v.tensor_scalar_mul(
                        out=nmax0[:], in0=top8[:, 0:1], scalar1=-1.0),
                        reads=("top8",), writes=("nmax0",))

                @blk.scalar
                def _(s):
                    p.op(s, "at", lambda: s.activation(
                        out=sd_t[:, TOPK:2 * TOPK], in_=top8[:], func=AF.Exp,
                        bias=nmax0[:]), reads=("top8", "nmax0"), writes=("sd_t",))

                @blk.vector
                def _(v):
                    p.op(v, "vt", lambda: v.tensor_tensor(
                        out=eq3[:],
                        in0=lgt[:].unsqueeze(1).to_broadcast(
                            [128, TOPK, NE]),
                        in1=top8[:].unsqueeze(2).to_broadcast(
                            [128, TOPK, NE]),
                        op=ALU.is_equal), reads=("lgt", "top8"), writes=("eq3",))
                    p.op(v, "vt", lambda: v.tensor_mul(
                        out=eq3[:], in0=eq3[:], in1=C["iotaM"][:]),
                        reads=("eq3", "s_iotaM"), writes=("eq3",))
                    p.op(v, "vt", lambda: v.tensor_reduce(
                        out=sd_t[:, 0:TOPK],
                        in_=eq3[:].rearrange("p (k e) -> p k e", k=TOPK),
                        axis=AX.X, op=ALU.min), reads=("eq3",), writes=("sd_t",))
                    p.op(v, "vt", lambda: v.tensor_scalar_add(
                        out=sd_t[:, 0:TOPK], in0=sd_t[:, 0:TOPK], scalar1=BIGIDX),
                        reads=("sd_t",), writes=("sd_t",))
                    p.op(v, "vt", lambda: v.tensor_tensor(
                        out=oh3[:],
                        in0=sd_t[:, 0:TOPK].unsqueeze(2).to_broadcast(
                            [128, TOPK, NE]),
                        in1=C["iotaE"][:].rearrange("p (k e) -> p k e", k=TOPK),
                        op=ALU.is_equal), reads=("sd_t", "s_iotaE"), writes=("oh3",))
                    p.op(v, "vt", lambda: v.tensor_reduce(
                        out=cntT[:],
                        in_=oh3[:].rearrange("p (k e) -> p e k", k=TOPK),
                        axis=AX.X, op=ALU.add), reads=("oh3",), writes=("cntT",))

                if lvl < 12:
                    continue
                @blk.tensor
                def _(t):
                    p.mm(t, [
                        lambda: t.matmul(psD[:, 0:64], C["tstrictT"][:], cntT[:],
                                         start=True, stop=False),
                        lambda: t.matmul(psD[:, 0:64], C["ones_row"][:], carry[:],
                                         start=False, stop=True)],
                        reads=("s_tstrictT", "cntT", "s_ones_row", "carry"),
                        writes=("psD",))
                    p.mm(t, [
                        lambda: t.matmul(psD[0:1, 128:192], C["ones_col"][:],
                                         cntT[:], start=True, stop=True)],
                        reads=("s_ones_col", "cntT"), writes=("psD",))

                @blk.vector
                def _(v):
                    p.op(v, "vt", lambda: v.tensor_copy(
                        out=base_sb[:], in_=psD[:, 0:64]),
                        reads=("psD",), writes=("base_sb",))
                    p.op(v, "vt", lambda: v.tensor_add(
                        out=carry[:], in0=carry[:], in1=psD[0:1, 128:192]),
                        reads=("carry", "psD"), writes=("carry",))
                    p.op(v, "vt", lambda: v.tensor_tensor(
                        out=eq3[:], in0=oh3[:],
                        in1=base_sb[:].unsqueeze(1).to_broadcast(
                            [128, TOPK, NE]), op=ALU.mult),
                        reads=("oh3", "base_sb"), writes=("eq3",))
                    p.op(v, "vt", lambda: v.tensor_reduce(
                        out=sd_t[:, 2 * TOPK:3 * TOPK],
                        in_=eq3[:].rearrange("p (k e) -> p k e", k=TOPK),
                        axis=AX.X, op=ALU.add), reads=("eq3",), writes=("sd_t",))
                    p.op(v, "vt", lambda: v.tensor_tensor(
                        out=scr64[:],
                        in0=sd_t[:, 0:TOPK].unsqueeze(2).to_broadcast(
                            [128, TOPK, TOPK]),
                        in1=sd_t[:, 0:TOPK].unsqueeze(1).to_broadcast(
                            [128, TOPK, TOPK]),
                        op=ALU.is_equal), reads=("sd_t",), writes=("scr64",))
                    p.op(v, "vt", lambda: v.tensor_mul(
                        out=scr64[:], in0=scr64[:], in1=C["kkmask"][:]),
                        reads=("scr64", "s_kkmask"), writes=("scr64",))
                    p.op(v, "vt", lambda: v.tensor_reduce(
                        out=scr8[:],
                        in_=scr64[:].rearrange("p (k j) -> p k j", k=TOPK),
                        axis=AX.X, op=ALU.add), reads=("scr64",), writes=("scr8",))
                    p.op(v, "vt", lambda: v.tensor_add(
                        out=sd_t[:, 2 * TOPK:3 * TOPK],
                        in0=sd_t[:, 2 * TOPK:3 * TOPK], in1=scr8[:]),
                        reads=("sd_t", "scr8"), writes=("sd_t",))

                @blk.sync
                def _(sync):
                    p.dma(sync, "sync", "st",
                          sdat_h[tile * 128:(tile + 1) * 128, :], sd_t[:],
                          reads=("sd_t",))

        @blk.sync
        def _(sync):
            p.dma(sync, "sync", "st", cnt_h, carry[:], reads=("carry",))

        @blk.gpsimd
        def _(g):
            g.wait_ge(p.sem["st"], p.n["st"])
            g.wait_ge(p.sem["sd"], p.n["sd"])
            g.wait_ge(p.sem["gd"], p.n["gd"])
            g.wait_ge(p.sem["x0"], p.n["x0"])
            g.wait_ge(p.sem["x1"], p.n["x1"])
    return p


def launch1_inputs(core, full):
    x = full["L0_attn_x"].reshape(B * S, D)
    d = {
        "x": np.ascontiguousarray(x[core * TPC:(core + 1) * TPC]),
        "q_w": full["L0_attn_q_w"],
        "kv_w": full["L0_attn_kv_w"],
        "proj_w": full["L0_attn_proj_w"],
        "proj_b": full["L0_attn_proj_b"].reshape(1, D),
        "gate_w": full["L0_ffn_gate_w"],
    }
    d.update(const_inputs(L1_CONSTS))
    return d


def kernel(**inputs):
    cores = list(range(NCORES))
    p1 = build_launch1()
    r1 = run_bass_kernel_spmd(p1.nc, [launch1_inputs(c, inputs) for c in cores],
                              core_ids=cores)
    L1r = [r1.results[c] for c in cores]
    cnts_all = np.concatenate([L1r[c]["cnt"] for c in cores], axis=0)

    p2 = build_launch2()
    r2 = run_bass_kernel_spmd(
        p2.nc,
        [launch2_inputs(c, cnts_all, L1r[c]["slotdat"], L1r[c]["xln1b"])
         for c in cores], core_ids=cores)
    L2r = [r2.results[c] for c in cores]
    contribs = [L2r[c]["contrib"] for c in cores]

    p3 = build_launch3()
    r3 = run_bass_kernel_spmd(p3.nc, [launch3_inputs(c, contribs, inputs)
                                      for c in cores], core_ids=cores)
    y_full = np.concatenate([r3.results[c]["y"] for c in cores], axis=0)

    p4a = build_launch4a()
    r4a = run_bass_kernel_spmd(
        p4a.nc,
        [launch4a_inputs(c, L1r[c]["x1"], L2r[c]["wg"], y_full, inputs)
         for c in cores], core_ids=cores)

    p4b = build_launch4b()
    r4b = run_bass_kernel_spmd(
        p4b.nc,
        [launch4b_inputs(c, r4a.results[c]["x3"], inputs) for c in cores],
        core_ids=cores)
    out = np.concatenate([r4b.results[c]["out"] for c in cores], axis=0)
    return out.reshape(B, S, D)



L2_CONSTS = ["ones_row", "iotaE"]


def build_launch2():
    p = P("l2")
    nc = p.nc
    cnts_h = nc.dram_tensor("cnts_all", [NCORES, NE], F32, kind="ExternalInput").ap()
    mask_h = nc.dram_tensor("coremask", [NCORES, 1], F32, kind="ExternalInput").ap()
    sdat_h = nc.dram_tensor("slotdat", [TPC, 3 * TOPK], F32, kind="ExternalInput").ap()
    xln1_h = nc.dram_tensor("xln1b", [TPC, D], BF16, kind="ExternalInput").ap()

    contrib_h = nc.dram_tensor("contrib", [NE * CAP, D], BF16, kind="ExternalOutput").ap()
    wg_h = nc.dram_tensor("wg", [TPC, 2 * TOPK], F32, kind="ExternalOutput").ap()

    hbs, C = _declare_consts(p, L2_CONSTS)
    with p.ctx:
        for s in ["sd", "gd", "zs", "id", "pt", "vt", "at", "st", "x0", "x1"]:
            p.semaphore(s)
        cnts = p.sbuf("cnts", [NCORES, NE])
        cmask = p.sbuf("cmask", [NCORES, 1])
        offs = p.sbuf("offs", [1, NE])
        offgb = p.sbuf("offgb", [128, NE])
        sdall = p.sbuf("sdall", [128, NT * 3 * TOPK])
        wgall = p.sbuf("wgall", [128, NT * 2 * TOPK])
        zt = p.sbuf("zt", [128, 8192], BF16)
        xt = [p.sbuf(f"xt{i}", [128, D], BF16) for i in range(2)]
        oh3 = p.sbuf("oh3", [128, TOPK * NE])
        selr = p.sbuf("selr", [128, TOPK])
        g_t = p.sbuf("g_t", [128, TOPK])
        gi2 = [p.sbuf(f"gi{i}", [128, TOPK], U32) for i in range(2)]
        idx256 = p.sbuf("idx256", [128, TOPK])
        valid = p.sbuf("valid", [128, TOPK])
        pv = p.sbuf("pv", [128, TOPK])
        wsum = p.sbuf("wsum", [128, 1])
        psA = p.psum("psA", [128, 512])
        blk = p.ctx.enter_context(nc.Block())
        _load_consts(p, blk, hbs, C)
        bcreg = nc.alloc_register(mybir.EngineType.Pool)

        @blk.gpsimd
        def _(g):
            g.reg_mov(bcreg, NE * CAP - 1)

        @blk.sync
        def _(sync):
            p.dma(sync, "sync", "sd", cnts[:], cnts_h, writes=("cnts",))
            p.dma(sync, "sync", "sd", cmask[:], mask_h, writes=("cmask",))
            p.dma(sync, "sync", "sd",
                  sdall[:].rearrange("p (t c) -> p t c", t=NT),
                  sdat_h.rearrange("(t p) c -> p t c", p=128), writes=("sdall",))

        @blk.tensor
        def _(t):
            p.mm(t, [lambda: t.matmul(psA[0:1, 0:64], cmask[:], cnts[:],
                                      start=True, stop=True)],
                 reads=("cmask", "cnts"), writes=("psA",))

        @blk.vector
        def _(v):
            p.op(v, "vt", lambda: v.tensor_copy(out=offs[:], in_=psA[0:1, 0:64]),
                 reads=("psA",), writes=("offs",))
            p.op(v, "vt", lambda: v.scalar_tensor_tensor(
                out=offs[:], in0=C["iotaE"][0:1, 0:NE], scalar=float(CAP),
                in1=offs[:], op0=ALU.mult, op1=ALU.add),
                reads=("offs", "s_iotaE"), writes=("offs",))

        @blk.tensor
        def _(t):
            p.mm(t, [lambda: t.matmul(psA[:, 64:128], C["ones_row"][:], offs[:],
                                      start=True, stop=True)],
                 reads=("s_ones_row", "offs"), writes=("psA",))

        @blk.vector
        def _(v):
            p.op(v, "vt", lambda: v.tensor_copy(out=offgb[:], in_=psA[:, 64:128]),
                 reads=("psA",), writes=("offgb",))
            p.op(v, "vt", lambda: v.memset(zt[:], 0.0), writes=("zt",))

        @blk.sync
        def _(sync):
            for i in range(16):
                p.dma(sync, "sync", "zs",
                      contrib_h[1024 * i:1024 * (i + 1), :].rearrange(
                          "(r p) d -> p r d", p=128),
                      zt[:].rearrange("p (r d) -> p r d", r=8),
                      reads=("zt",), writes=("contrib",))

        for tile in range(NT):
            par = tile % 2
            xtb = xt[par]
            xtn = f"xt{par}"
            xsem = "x0" if par == 0 else "x1"
            sd = sdall[:, tile * 24: tile * 24 + 24]
            wg = wgall[:, tile * 16: tile * 16 + 16]

            @blk.sync
            def _(sync):
                p.dma(sync, "sync", xsem, xtb[:],
                      xln1_h[tile * 128:(tile + 1) * 128, :], writes=(xtn,))

            @blk.vector
            def _(v):
                p.op(v, "vt", lambda sd=sd: v.tensor_tensor(
                    out=oh3[:],
                    in0=sd[:, 0:TOPK].unsqueeze(2).to_broadcast([128, TOPK, NE]),
                    in1=C["iotaE"][:].rearrange("p (k e) -> p k e", k=TOPK),
                    op=ALU.is_equal), reads=("sdall", "s_iotaE"), writes=("oh3",))
                p.op(v, "vt", lambda: v.tensor_tensor(
                    out=oh3[:], in0=oh3[:],
                    in1=offgb[:].unsqueeze(1).to_broadcast([128, TOPK, NE]),
                    op=ALU.mult), reads=("oh3", "offgb"), writes=("oh3",))
                p.op(v, "vt", lambda: v.tensor_reduce(
                    out=selr[:], in_=oh3[:].rearrange("p (k e) -> p k e", k=TOPK),
                    axis=AX.X, op=ALU.add), reads=("oh3",), writes=("selr",))
                p.op(v, "vt", lambda sd=sd: v.tensor_tensor(
                    out=g_t[:], in0=sd[:, 2 * TOPK:3 * TOPK], in1=selr[:],
                    op=ALU.add), reads=("sdall", "selr"), writes=("g_t",))
                p.op(v, "vt", lambda sd=sd: v.tensor_scalar_mul(
                    out=idx256[:], in0=sd[:, 0:TOPK], scalar1=float(CAP)),
                    reads=("sdall",), writes=("idx256",))
                p.op(v, "vt", lambda: v.tensor_sub(
                    out=valid[:], in0=g_t[:], in1=idx256[:]),
                    reads=("g_t", "idx256"), writes=("valid",))
                p.op(v, "vt", lambda: v.tensor_single_scalar(
                    out=valid[:], in_=valid[:], scalar=float(CAP), op=ALU.is_lt),
                    reads=("valid",), writes=("valid",))
                p.op(v, "vt", lambda sd=sd: v.tensor_tensor(
                    out=pv[:], in0=sd[:, TOPK:2 * TOPK], in1=valid[:],
                    op=ALU.mult), reads=("sdall", "valid"), writes=("pv",))
                p.op(v, "vt", lambda sd=sd: v.reduce_sum(
                    out=wsum[:], in_=sd[:, TOPK:2 * TOPK], axis=AX.X),
                    reads=("sdall",), writes=("wsum",))
                p.op(v, "vt", lambda: v.reciprocal(out=wsum[:], in_=wsum[:]),
                     reads=("wsum",), writes=("wsum",))
                p.op(v, "vt", lambda wg=wg: v.tensor_scalar_mul(
                    out=wg[:, 0:TOPK], in0=pv[:], scalar1=wsum[:]),
                    reads=("pv", "wsum"), writes=("wgall",))
                p.op(v, "vt", lambda: v.tensor_single_scalar(
                    out=pv[:], in_=valid[:], scalar=1.0, op=ALU.is_lt),
                    reads=("valid",), writes=("pv",))
                p.op(v, "vt", lambda wg=wg: v.scalar_tensor_tensor(
                    out=wg[:, TOPK:2 * TOPK], in0=pv[:], scalar=BIGIDX,
                    in1=g_t[:], op0=ALU.mult, op1=ALU.add),
                    reads=("pv", "g_t"), writes=("wgall",))
                p.op(v, "vt", lambda wg=wg, tile=tile: v.tensor_copy(
                    out=gi2[tile % 2][:], in_=wg[:, TOPK:2 * TOPK]),
                    reads=("wgall",), writes=(f"gi{tile % 2}",))

            @blk.gpsimd
            def _(g):
                for k in range(TOPK):
                    p.idma(g, "gp", "id",
                           out=contrib_h,
                           out_offset=IndirectOffsetOnAxis(
                               gi2[tile % 2][:, k:k + 1], 0),
                           in_=xtb[:], in_offset=None,
                           bounds_check=bcreg, oob_is_err=False,
                           reads=(f"gi{tile % 2}", xtn, "contrib"),
                           writes=("contrib",))

        @blk.sync
        def _(sync):
            p.dma(sync, "sync", "st",
                  wg_h.rearrange("(t p) c -> p t c", p=128),
                  wgall[:].rearrange("p (t c) -> p t c", t=NT),
                  reads=("wgall",))

        @blk.gpsimd
        def _(g):
            for s in ["st", "sd", "zs", "id", "x0", "x1", "gd"]:
                g.wait_ge(p.sem[s], p.n[s])
    return p


def launch2_inputs(core, cnts_all, sdat, xln1):
    mask = np.zeros((NCORES, 1), np.float32)
    mask[:core] = 1.0
    d = {"cnts_all": cnts_all, "coremask": mask, "slotdat": sdat, "xln1b": xln1}
    d.update(const_inputs(L2_CONSTS))
    return d



L3_CONSTS = ["ident_bf"]


def build_launch3():
    p = P("l3")
    nc = p.nc
    csl_h = nc.dram_tensor("contribs_sl", [NCORES, EPC * CAP, D], BF16,
                           kind="ExternalInput").ap()
    wg_h = nc.dram_tensor("ew_gate", [EPC, EI, D], F32, kind="ExternalInput").ap()
    wu_h = nc.dram_tensor("ew_up", [EPC, EI, D], F32, kind="ExternalInput").ap()
    wd_h = nc.dram_tensor("ew_down", [EPC, D, EI], F32, kind="ExternalInput").ap()
    y_h = nc.dram_tensor("y", [EPC * CAP, D], BF16, kind="ExternalOutput").ap()

    hbs, C = _declare_consts(p, L3_CONSTS)
    NIC = EI // 128
    with p.ctx:
        for s in ["sd", "gd", "ld0", "pt", "vt", "at", "st"]:
            p.semaphore(s)
        disp = p.sbuf("disp", [128, 2 * D], BF16)
        dispT = p.sbuf("dispT", [128, 8 * CAP], BF16)
        wgt = [p.sbuf(f"wgt{i}", [128, NIC * D], BF16) for i in range(2)]
        wgT = p.sbuf("wgT", [128, 8 * EI], BF16)
        wuT = p.sbuf("wuT", [128, 8 * EI], BF16)
        wdr = [p.sbuf(f"wdr{i}", [128, 8 * EI], BF16) for i in range(2)]
        wdT = p.sbuf("wdT", [128, NIC * D], BF16)
        gsb = p.sbuf("gsb", [128, 2 * EI])
        hsb = p.sbuf("hsb", [128, 2 * EI], BF16)
        hT = p.sbuf("hT", [128, NIC * CAP], BF16)
        ysb = p.sbuf("ysb", [128, 2 * D], BF16)
        ptA = p.psum("ptA", [128, 1024], BF16)
        ptB = p.psum("ptB", [128, 1024], BF16)
        psA = p.psum("psA", [128, 512])
        psB = p.psum("psB", [128, 512])
        psC = p.psum("psC", [128, 512])
        psD = p.psum("psD", [128, 512])
        psE = p.psum("psE", [128, 512])
        psF = p.psum("psF", [128, 512])
        blk = p.ctx.enter_context(nc.Block())
        _load_consts(p, blk, hbs, C)

        for e in range(EPC):
            wpar = e % 2
            wgt_b, wdr_b = wgt[wpar], wdr[wpar]
            wgt_n, wdr_n = f"wgt{wpar}", f"wdr{wpar}"

            @blk.sync
            def _(sync):
                p.dma(sync, "sync", "sd",
                      disp[:].rearrange("p (t d) -> p t d", t=2),
                      csl_h[0, e * CAP:(e + 1) * CAP, :].rearrange(
                          "(t p) d -> p t d", p=128),
                      writes=("disp",))

            @blk.gpsimd
            def _(g):
                for c2 in range(1, NCORES):
                    p.dma(g, "gp", "sd",
                          disp[:].rearrange("p (t d) -> p t d", t=2),
                          csl_h[c2, e * CAP:(e + 1) * CAP, :].rearrange(
                              "(t p) d -> p t d", p=128),
                          reads=("disp",), writes=("disp",), accum_op=ALU.add)
                p.dma(g, "gp", "ld0", wgt_b[:].rearrange("p (c d) -> p c d", c=NIC),
                      wg_h[e].rearrange("(c p) d -> p c d", p=128),
                      writes=(wgt_n,))

            @blk.tensor
            def _(t):
                for i in range(2):
                    p.mm(t, [
                        (lambda i=i, c=c: t.transpose(
                            ptA[:, c * 128:(c + 1) * 128],
                            disp[:, i * D + c * 128: i * D + (c + 1) * 128],
                            C["ident_bf"][:])) for c in range(8)],
                        reads=("disp", "s_ident_bf"), writes=("ptA",))
                    @blk.vector
                    def _(v):
                        p.op(v, "vt", lambda i=i: v.tensor_copy(
                            out=dispT[:].rearrange("p (c x) -> p c x", c=8)[
                                :, :, i * 128:(i + 1) * 128],
                            in_=ptA[:].rearrange("p (c x) -> p c x", c=8)),
                            reads=("ptA",), writes=("dispT",))

            @blk.tensor
            def _(t):
                for ch in range(8):
                    p.mm(t, [
                        (lambda ch=ch, ci=ci: t.transpose(
                            ptB[:, ci * 128:(ci + 1) * 128],
                            wgt_b[:, ci * D + ch * 128: ci * D + (ch + 1) * 128],
                            C["ident_bf"][:])) for ci in range(NIC)],
                        reads=(wgt_n, "s_ident_bf"), writes=("ptB",))

                    @blk.vector
                    def _(v):
                        p.op(v, "vt", lambda ch=ch: v.tensor_copy(
                            out=wgT[:, ch * EI:(ch + 1) * EI],
                            in_=ptB[:, 0:EI]), reads=("ptB",), writes=("wgT",))

            @blk.gpsimd
            def _(g):
                p.dma(g, "gp", "ld0", wgt_b[:].rearrange("p (c d) -> p c d", c=NIC),
                      wu_h[e].rearrange("(c p) d -> p c d", p=128),
                      writes=(wgt_n,))

            @blk.tensor
            def _(t):
                for ch in range(8):
                    p.mm(t, [
                        (lambda ch=ch, ci=ci: t.transpose(
                            ptB[:, ci * 128:(ci + 1) * 128],
                            wgt_b[:, ci * D + ch * 128: ci * D + (ch + 1) * 128],
                            C["ident_bf"][:])) for ci in range(NIC)],
                        reads=(wgt_n, "s_ident_bf"), writes=("ptB",))

                    @blk.vector
                    def _(v):
                        p.op(v, "vt", lambda ch=ch: v.tensor_copy(
                            out=wuT[:, ch * EI:(ch + 1) * EI],
                            in_=ptB[:, 0:EI]), reads=("ptB",), writes=("wuT",))

            @blk.gpsimd
            def _(g):
                p.dma(g, "gp", "ld0", wdr_b[:].rearrange("p (c d) -> p c d", c=8),
                      wd_h[e].rearrange("(c p) d -> p c d", p=128),
                      writes=(wdr_n,))

            @blk.tensor
            def _(t):
                for ci in range(NIC):
                    p.mm(t, [
                        (lambda ci=ci, ch=ch: t.transpose(
                            ptB[:, ch * 128:(ch + 1) * 128],
                            wdr_b[:, ch * EI + ci * 128: ch * EI + (ci + 1) * 128],
                            C["ident_bf"][:])) for ch in range(8)],
                        reads=(wdr_n, "s_ident_bf"), writes=("ptB",))

                    @blk.vector
                    def _(v):
                        p.op(v, "vt", lambda ci=ci: v.tensor_copy(
                            out=wdT[:, ci * D:(ci + 1) * D],
                            in_=ptB[:]), reads=("ptB",), writes=("wdT",))

            for cc in range(2):
                @blk.tensor
                def _(t):
                    for nb in range(2):
                        nw = 512 if nb == 0 else EI - 512
                        bankG = [psA, psB][nb]
                        bankU = [psC, psD][nb]
                        bG = ["psA", "psB"][nb]
                        bU = ["psC", "psD"][nb]
                        p.mm(t, [
                            (lambda cc=cc, nb=nb, nw=nw, bankG=bankG, ch=ch: t.matmul(
                                bankG[:, 0:nw],
                                dispT[:, ch * CAP + cc * 128: ch * CAP + cc * 128 + 128],
                                wgT[:, ch * EI + nb * 512: ch * EI + nb * 512 + nw],
                                start=(ch == 0), stop=(ch == 7))) for ch in range(8)],
                            reads=("dispT", "wgT"), writes=(bG,))
                        p.mm(t, [
                            (lambda cc=cc, nb=nb, nw=nw, bankU=bankU, ch=ch: t.matmul(
                                bankU[:, 0:nw],
                                dispT[:, ch * CAP + cc * 128: ch * CAP + cc * 128 + 128],
                                wuT[:, ch * EI + nb * 512: ch * EI + nb * 512 + nw],
                                start=(ch == 0), stop=(ch == 7))) for ch in range(8)],
                            reads=("dispT", "wuT"), writes=(bU,))

                @blk.scalar
                def _(s):
                    p.op(s, "at", lambda cc=cc: s.activation(
                        out=gsb[:, cc * EI: cc * EI + 512], in_=psA[:],
                        func=AF.Silu), reads=("psA",), writes=("gsb",))
                    p.op(s, "at", lambda cc=cc: s.activation(
                        out=gsb[:, cc * EI + 512: (cc + 1) * EI], in_=psB[:, 0:EI - 512],
                        func=AF.Silu), reads=("psB",), writes=("gsb",))

                @blk.vector
                def _(v):
                    p.op(v, "vt", lambda cc=cc: v.tensor_tensor(
                        out=hsb[:, cc * EI: cc * EI + 512],
                        in0=gsb[:, cc * EI: cc * EI + 512], in1=psC[:],
                        op=ALU.mult), reads=("gsb", "psC"), writes=("hsb",))
                    p.op(v, "vt", lambda cc=cc: v.tensor_tensor(
                        out=hsb[:, cc * EI + 512: (cc + 1) * EI],
                        in0=gsb[:, cc * EI + 512: (cc + 1) * EI],
                        in1=psD[:, 0:EI - 512],
                        op=ALU.mult), reads=("gsb", "psD"), writes=("hsb",))

            @blk.tensor
            def _(t):
                for ci in range(NIC):
                    p.mm(t, [
                        (lambda ci=ci, cc=cc: t.transpose(
                            ptA[:, cc * 128:(cc + 1) * 128],
                            hsb[:, cc * EI + ci * 128: cc * EI + (ci + 1) * 128],
                            C["ident_bf"][:])) for cc in range(2)],
                        reads=("hsb", "s_ident_bf"), writes=("ptA",))

                    @blk.vector
                    def _(v):
                        p.op(v, "vt", lambda ci=ci: v.tensor_copy(
                            out=hT[:, ci * CAP:(ci + 1) * CAP],
                            in_=ptA[:, 0:CAP]), reads=("ptA",), writes=("hT",))

            for cc in range(2):
                @blk.tensor
                def _(t):
                    for nb in range(2):
                        bank = [psE, psF][nb]
                        bnm = ["psE", "psF"][nb]
                        p.mm(t, [
                            (lambda cc=cc, nb=nb, bank=bank, ci=ci: t.matmul(
                                bank[:],
                                hT[:, ci * CAP + cc * 128: ci * CAP + cc * 128 + 128],
                                wdT[:, ci * D + nb * 512: ci * D + (nb + 1) * 512],
                                start=(ci == 0), stop=(ci == NIC - 1)))
                            for ci in range(NIC)],
                            reads=("hT", "wdT"), writes=(bnm,))

                @blk.vector
                def _(v):
                    for nb, bnm in enumerate(["psE", "psF"]):
                        bank = [psE, psF][nb]
                        p.op(v, "vt", lambda cc=cc, nb=nb, bank=bank: v.tensor_copy(
                            out=ysb[:, cc * D + nb * 512: cc * D + (nb + 1) * 512],
                            in_=bank[:]), reads=(bnm,), writes=("ysb",))

            @blk.sync
            def _(sync):
                p.dma(sync, "sync", "st",
                      y_h[e * CAP:(e + 1) * CAP, :].rearrange(
                          "(t p) d -> p t d", p=128),
                      ysb[:].rearrange("p (t d) -> p t d", t=2),
                      reads=("ysb",))

        @blk.gpsimd
        def _(g):
            for s in ["st", "sd", "ld0", "gd"]:
                g.wait_ge(p.sem[s], p.n[s])
    return p


def launch3_inputs(core, contribs, full):
    sl = np.stack([contribs[c2][core * EPC * CAP:(core + 1) * EPC * CAP]
                   for c2 in range(NCORES)])
    d = {"contribs_sl": sl,
         "ew_gate": full["L0_ffn_w_gate"][core * EPC:(core + 1) * EPC],
         "ew_up": full["L0_ffn_w_up"][core * EPC:(core + 1) * EPC],
         "ew_down": full["L0_ffn_w_down"][core * EPC:(core + 1) * EPC]}
    d.update(const_inputs(L3_CONSTS))
    return d




def _emit_ln(p, blk, src, src_n, dst, dst_n, rstd_t, rstd_n, xcbuf, xc_n,
             nmean, sqs, c_eps, c_invD, c_mhalf):
    @blk.vector
    def _(v):
        p.op(v, "vt", lambda: v.reduce_sum(
            out=nmean[:, 0:1], in_=src, axis=AX.X, negate=True),
            reads=(src_n,), writes=("nmean",))
        p.op(v, "vt", lambda: v.tensor_scalar_mul(
            out=nmean[:, 0:1], in0=nmean[:, 0:1], scalar1=1.0 / D),
            reads=("nmean",), writes=("nmean",))
        p.op(v, "vt", lambda: v.tensor_scalar_add(
            out=xcbuf, in0=src, scalar1=nmean[:, 0:1]),
            reads=(src_n, "nmean"), writes=(xc_n,))

    @blk.scalar
    def _(s):
        p.op(s, "at", lambda: s.activation(
            out=sqs[:], in_=xcbuf, func=AF.Square, accum_out=rstd_t),
            reads=(xc_n,), writes=("sqs", rstd_n))
        p.op(s, "at", lambda: s.activation(
            out=rstd_t, in_=rstd_t, func=AF.Ln, scale=c_invD[:], bias=c_eps[:]),
            reads=(rstd_n, "c_invD", "c_eps"), writes=(rstd_n,))
        p.op(s, "at", lambda: s.activation(
            out=rstd_t, in_=rstd_t, func=AF.Exp, scale=c_mhalf[:]),
            reads=(rstd_n, "c_mhalf"), writes=(rstd_n,))

    if dst is not None:
        @blk.vector
        def _(v):
            p.op(v, "vt", lambda: v.tensor_scalar_mul(
                out=dst, in0=xcbuf, scalar1=rstd_t),
                reads=(xc_n, rstd_n), writes=(dst_n,))



def _p_sbuf_push(p, name, shape, dt=F32):
    guard = p.nc.sbuf_tensor(name, shape, dt)
    t = guard.__enter__()
    if not hasattr(p, "_guards"):
        p._guards = []
        p._free_stamps = []
    p._guards.append((name, guard))
    b = Buf(name)
    b.r = list(p._free_stamps)
    p.bufs[name] = b
    return t


def _p_sbuf_pop(p, upto):
    while p._guards:
        name, guard = p._guards.pop()
        b = p.bufs.pop(name)
        if b.w is not None:
            p._free_stamps.append(b.w)
        p._free_stamps.extend(b.r)
        guard.__exit__(None, None, None)
        if name == upto:
            return
    raise KeyError(upto)



L4A_CONSTS = ["ident_bf", "ones_row"]
NG = S // 512
NQB = S // 128
NKB = S // 512


def build_launch4a():
    p = P("l4a")
    nc = p.nc
    x1_h = nc.dram_tensor("x1", [TPC, D], F32, kind="ExternalInput").ap()
    wgs_h = nc.dram_tensor("wg", [TPC, 2 * TOPK], F32, kind="ExternalInput").ap()
    y_h = nc.dram_tensor("y_all", [NE * CAP, D], BF16, kind="ExternalInput").ap()
    Wq_h = nc.dram_tensor("Wq", [D, D], F32, kind="ExternalInput").ap()
    Wk_h = nc.dram_tensor("Wk", [D, D], F32, kind="ExternalInput").ap()
    Wv_h = nc.dram_tensor("Wv", [D, D], F32, kind="ExternalInput").ap()
    Wo_h = nc.dram_tensor("Wo", [D, D], F32, kind="ExternalInput").ap()
    bq_h = nc.dram_tensor("bq", [1, D], F32, kind="ExternalInput").ap()
    bk_h = nc.dram_tensor("bk", [1, D], F32, kind="ExternalInput").ap()
    bv_h = nc.dram_tensor("bv", [1, D], F32, kind="ExternalInput").ap()
    bo_h = nc.dram_tensor("bo", [1, D], F32, kind="ExternalInput").ap()
    x3_h = nc.dram_tensor("x3", [TPC, D], F32, kind="ExternalOutput").ap()

    x2_h = nc.dram_tensor("x2sp", [TPC, D], F32).ap()
    xl2_h = nc.dram_tensor("xl2sp", [TPC, D], BF16).ap()
    qt_h = nc.dram_tensor("qtsp", [D, S], BF16).ap()

    hbs, C = _declare_consts(p, L4A_CONSTS)
    with p.ctx:
        for s in ["sd", "gd", "id", "x0", "x1s", "pt", "vt", "at", "st"]:
            p.semaphore(s)
        p.DMA_SEMS = frozenset(["sd", "gd", "st", "x0", "x1s", "id"])
        bcreg = nc.alloc_register(mybir.EngineType.Pool)

        ones_bf = p.sbuf("ones_bf", [1, 128], BF16)
        onesc_bf = p.sbuf("onesc_bf", [128, 1], BF16)
        brow = p.sbuf("brow", [1, 2 * D], BF16)
        bqc = p.sbuf("bqc", [128, 16])
        c_eps = p.sbuf("c_eps", [128, 1])
        c_invD = p.sbuf("c_invD", [128, 1])
        c_mhalf = p.sbuf("c_mhalf", [128, 1])
        c_two = p.sbuf("c_two", [128, 1])
        c_m1 = p.sbuf("c_m1", [128, 1])
        nmean = p.sbuf("nmean", [128, 2])
        rstd = p.sbuf("rstd", [128, 2])

        ptA = p.psum("ptA", [128, 1024], BF16)
        ptB = p.psum("ptB", [128, 1024], BF16)
        psA = p.psum("psA", [128, 512])
        psB = p.psum("psB", [128, 512])
        psC = p.psum("psC", [128, 512])
        psD = p.psum("psD", [128, 512])
        psE = p.psum("psE", [128, 512])
        psF = p.psum("psF", [128, 512])

        blk = p.ctx.enter_context(nc.Block())
        _load_consts(p, blk, hbs, C)

        onesc = p.sbuf("onesc", [1, 1])
        bqr = _p_sbuf_push(p, "bqr", [1, 2 * D])

        @blk.gpsimd
        def _(g):
            g.reg_mov(bcreg, NE * CAP - 1)
            p.dma(g, "gp", "gd", ones_bf[:], hbs["ones_row"], writes=("ones_bf",))
            p.dma(g, "gp", "gd", onesc_bf[:], hbs["ones_row"].rearrange("o p -> p o"),
                  writes=("onesc_bf",))
            p.dma(g, "gp", "gd", brow[0:1, 0:D], bv_h, writes=("brow",))
            p.dma(g, "gp", "gd", brow[0:1, D:2 * D], bo_h, writes=("brow",))

        @blk.sync
        def _(sync):
            p.dma(sync, "sync", "sd", bqr[0:1, 0:D], bq_h, writes=("bqr",))
            p.dma(sync, "sync", "sd", bqr[0:1, D:2 * D], bk_h, writes=("bqr",))

        @blk.vector
        def _(v):
            p.op(v, "vt", lambda: v.memset(c_eps[:], LN_EPS), writes=("c_eps",))
            p.op(v, "vt", lambda: v.memset(c_invD[:], 1.0 / D), writes=("c_invD",))
            p.op(v, "vt", lambda: v.memset(c_mhalf[:], -0.5), writes=("c_mhalf",))
            p.op(v, "vt", lambda: v.memset(c_two[:], 2.0), writes=("c_two",))
            p.op(v, "vt", lambda: v.memset(c_m1[:], -1.0), writes=("c_m1",))
            p.op(v, "vt", lambda: v.memset(onesc[:], 1.0), writes=("onesc",))

        @blk.tensor
        def _(t):
            for c in range(16):
                p.mm(t, [lambda c=c: t.matmul(
                    psA[:, c:c + 1], bqr[0:1, c * 128:(c + 1) * 128],
                    onesc[:], start=True, stop=True)],
                    reads=("bqr", "onesc"), writes=("psA",))

        @blk.vector
        def _(v):
            p.op(v, "vt", lambda: v.tensor_copy(out=bqc[:], in_=psA[:, 0:16]),
                 reads=("psA",), writes=("bqc",))
        _p_sbuf_pop(p, "bqr")

        yti = [_p_sbuf_push(p, f"yti{k}", [128, D], BF16)
               for k in range(2 * TOPK)]
        x1t = [_p_sbuf_push(p, f"x1t{i}", [128, D]) for i in range(2)]
        wgt_t2 = [_p_sbuf_push(p, f"wgt_t{i}", [128, 2 * TOPK]) for i in range(2)]
        gidx2 = [_p_sbuf_push(p, f"gidx{i}", [128, TOPK], U32) for i in range(2)]
        acc = _p_sbuf_push(p, "acc", [128, D])
        x2t = _p_sbuf_push(p, "x2t", [128, D])
        xcb = _p_sbuf_push(p, "xcb", [128, D])
        sqs = _p_sbuf_push(p, "sqs", [128, D])
        xlnb = _p_sbuf_push(p, "xlnb", [128, D], BF16)

        @blk.vector
        def _(v):
            for k in range(2 * TOPK):
                p.op(v, "vt", lambda k=k: v.memset(yti[k][:], 0.0),
                     writes=(f"yti{k}",))

        for tile in range(NT):
            gtok = tile * 128
            par = tile % 2
            x1b, x1n = x1t[par], f"x1t{par}"
            xsem = "x0" if par == 0 else "x1s"

            wgt_t = wgt_t2[tile % 2]
            wgn = f"wgt_t{tile % 2}"

            @blk.sync
            def _(sync):
                p.dma(sync, "sync", xsem, x1b[:], x1_h[gtok:gtok + 128, :],
                      writes=(x1n,))
                p.dma(sync, "sync", xsem, wgt_t[:], wgs_h[gtok:gtok + 128, :],
                      writes=(wgn,))

            gidx = gidx2[tile % 2]
            gxn = f"gidx{tile % 2}"

            @blk.vector
            def _(v):
                p.op(v, "vt", lambda gidx=gidx, wgt_t=wgt_t: v.tensor_copy(
                    out=gidx[:], in_=wgt_t[:, TOPK:2 * TOPK]),
                    reads=(wgn,), writes=(gxn,))

            @blk.gpsimd
            def _(g):
                for k in range(TOPK):
                    p.idma(g, "gp", "id",
                           out=yti[k][:], out_offset=None,
                           in_=y_h,
                           in_offset=IndirectOffsetOnAxis(gidx[:, k:k + 1], 0),
                           bounds_check=bcreg, oob_is_err=False,
                           reads=("gidx", f"yti{k}"), writes=(f"yti{k}",))

            @blk.vector
            def _(v):
                p.op(v, "vt", lambda: v.tensor_scalar_mul(
                    out=acc[:], in0=yti[0][:], scalar1=wgt_t[:, 0:1]),
                    reads=("yti0", "wgt_t"), writes=("acc",))
                for k in range(1, TOPK):
                    p.op(v, "vt", lambda k=k: v.scalar_tensor_tensor(
                        out=acc[:], in0=yti[k][:], scalar=wgt_t[:, k:k + 1],
                        in1=acc[:], op0=ALU.mult, op1=ALU.add),
                        reads=(f"yti{k}", "wgt_t", "acc"), writes=("acc",))
                p.op(v, "vt", lambda: v.tensor_add(
                    out=x2t[:], in0=x1b[:], in1=acc[:]),
                    reads=(x1n, "acc"), writes=("x2t",))

            _emit_ln(p, blk, x2t[:], "x2t", xlnb[:], "xlnb", rstd[:, 0:1], "rstd",
                     xcb[:], "xcb", nmean, sqs, c_eps, c_invD, c_mhalf)

            @blk.sync
            def _(sync):
                p.dma(sync, "sync", "st", x2_h[gtok:gtok + 128, :], x2t[:],
                      reads=("x2t",))
                p.dma(sync, "sync", "st", xl2_h[gtok:gtok + 128, :], xlnb[:],
                      reads=("xlnb",))

        _p_sbuf_pop(p, "yti0")

        Kt = _p_sbuf_push(p, "Kt", [128, 8 * S], BF16)
        Vt = _p_sbuf_push(p, "Vt", [128, (S // 128) * D], BF16)
        nk2h = _p_sbuf_push(p, "nk2h", [1, S], BF16)
        q2row = _p_sbuf_push(p, "q2row", [1, S], BF16)
        onesc2 = _p_sbuf_push(p, "onesc2", [1, 1], BF16)
        wa = [_p_sbuf_push(p, f"wa{i}", [128, 8 * D], BF16) for i in range(2)]
        xlg = [_p_sbuf_push(p, "xlg0", [128, D], BF16)] * 2
        xlTg = _p_sbuf_push(p, "xlTg", [128, 8 * 512], BF16)
        ksq = _p_sbuf_push(p, "ksq", [128, 512], BF16)
        q2c = _p_sbuf_push(p, "q2c", [128, 1])
        Qg = _p_sbuf_push(p, "Qg", [128, 1024], BF16)
        Psb = [_p_sbuf_push(p, f"PsbB{i}", [128, 512], BF16) for i in range(2)]
        PTs = [_p_sbuf_push(p, f"PTs{i}", [128, 512], BF16) for i in range(2)]
        dcol = _p_sbuf_push(p, "dcol", [128, NKB])
        rcpc = _p_sbuf_push(p, "rcpc", [128, 1])
        attn = _p_sbuf_push(p, "attn4", [128, D], BF16)
        attnT = _p_sbuf_push(p, "attnT4", [128, 8 * 128], BF16)
        x3t = _p_sbuf_push(p, "x3t", [128, D])

        @blk.vector
        def _(v):
            p.op(v, "vt", lambda: v.memset(onesc2[:], 1.0), writes=("onesc2",))

        for sq in range(2):
            toff = sq * (TPC // 2)

            @blk.gpsimd
            def _(g):
                p.dma(g, "gp", "gd", wa[0][:].rearrange("p (c d) -> p c d", c=8),
                      Wk_h.rearrange("(c p) d -> p c d", p=128), writes=("wa0",))
                p.dma(g, "gp", "gd", wa[1][:].rearrange("p (c d) -> p c d", c=8),
                      Wv_h.rearrange("(c p) d -> p c d", p=128), writes=("wa1",))

            for grp in range(NG):
                for ti in range(4):
                    gtok = toff + (grp * 4 + ti) * 128
                    par = ti % 2
                    xb, xn_ = xlg[0], "xlg0"
                    xsem = "x0" if par == 0 else "x1s"

                    @blk.sync
                    def _(sync):
                        p.dma(sync, "sync", xsem, xb[:],
                              xl2_h[gtok:gtok + 128, :], writes=(xn_,))

                    ptn = "ptA" if ti % 2 == 0 else "ptB"
                    pt = ptA if ti % 2 == 0 else ptB

                    @blk.tensor
                    def _(t):
                        p.mm(t, [
                            (lambda pt=pt, c=c, xb=xb: t.transpose(
                                pt[:, c * 128:(c + 1) * 128],
                                xb[:, c * 128:(c + 1) * 128],
                                C["ident_bf"][:])) for c in range(8)],
                            reads=(xn_, "s_ident_bf"), writes=(ptn,))

                    @blk.vector
                    def _(v):
                        p.op(v, "vt", lambda pt=pt, ti=ti: v.tensor_copy(
                            out=xlTg[:].rearrange("p (c x) -> p c x", c=8)[
                                :, :, ti * 128:(ti + 1) * 128],
                            in_=pt[:].rearrange("p (c x) -> p c x", c=8)),
                            reads=(ptn,), writes=("xlTg",))

                for wave in range(4):
                    @blk.tensor
                    def _(t):
                        for j in range(2):
                            oc = wave * 2 + j
                            bank = psE if j == 0 else psF
                            bnm = "psE" if j == 0 else "psF"
                            p.mm(t, [
                                (lambda bank=bank, oc=oc, cdi=cdi: t.matmul(
                                    bank[:],
                                    wa[0][:, cdi * D + oc * 128:
                                          cdi * D + (oc + 1) * 128],
                                    xlTg[:, cdi * 512:(cdi + 1) * 512],
                                    start=(cdi == 0), stop=(cdi == 7)))
                                for cdi in range(8)],
                                reads=("wa0", "xlTg"), writes=(bnm,))

                    @blk.vector
                    def _(v):
                        for j in range(2):
                            oc = wave * 2 + j
                            bank = psE if j == 0 else psF
                            bnm = "psE" if j == 0 else "psF"
                            p.op(v, "vt", lambda bank=bank, oc=oc, grp=grp:
                                 v.tensor_scalar_add(
                                     out=Kt[:, oc * S + grp * 512:
                                            oc * S + (grp + 1) * 512],
                                     in0=bank[:], scalar1=bqc[:, 8 + oc: 9 + oc]),
                                 reads=(bnm, "bqc"), writes=("Kt",))

                for ti in range(4):
                    tile = grp * 4 + ti
                    @blk.tensor
                    def _(t):
                        for nb in range(2):
                            bank = psC if nb == 0 else psD
                            bnm = "psC" if nb == 0 else "psD"
                            emits = [(lambda bank=bank, nb=nb: t.matmul(
                                bank[:], ones_bf[:],
                                brow[0:1, nb * 512:(nb + 1) * 512],
                                start=True, stop=False))]
                            emits += [
                                (lambda bank=bank, nb=nb, cdi=cdi, ti=ti: t.matmul(
                                    bank[:],
                                    xlTg[:, cdi * 512 + ti * 128:
                                         cdi * 512 + (ti + 1) * 128],
                                    wa[1][:, cdi * D + nb * 512:
                                          cdi * D + (nb + 1) * 512],
                                    start=False, stop=(cdi == 7)))
                                for cdi in range(8)]
                            p.mm(t, emits,
                                 reads=("ones_bf", "brow", "xlTg", "wa1"),
                                 writes=(bnm,))

                    @blk.vector
                    def _(v):
                        for nb, bnm in enumerate(["psC", "psD"]):
                            bank = psC if nb == 0 else psD
                            p.op(v, "vt", lambda bank=bank, nb=nb, tile=tile:
                                 v.tensor_copy(
                                     out=Vt[:, tile * D + nb * 512:
                                            tile * D + (nb + 1) * 512],
                                     in_=bank[:]), reads=(bnm,), writes=("Vt",))

            @blk.gpsimd
            def _(g):
                p.dma(g, "gp", "gd", wa[0][:].rearrange("p (c d) -> p c d", c=8),
                      Wq_h.rearrange("(c p) d -> p c d", p=128), writes=("wa0",))

            for grp in range(NG):
                for ti in range(4):
                    gtok = toff + (grp * 4 + ti) * 128
                    par = ti % 2
                    xb, xn_ = xlg[0], "xlg0"
                    xsem = "x0" if par == 0 else "x1s"

                    @blk.sync
                    def _(sync):
                        p.dma(sync, "sync", xsem, xb[:],
                              xl2_h[gtok:gtok + 128, :], writes=(xn_,))

                    ptn = "ptA" if ti % 2 == 0 else "ptB"
                    pt = ptA if ti % 2 == 0 else ptB

                    @blk.tensor
                    def _(t):
                        p.mm(t, [
                            (lambda pt=pt, c=c, xb=xb: t.transpose(
                                pt[:, c * 128:(c + 1) * 128],
                                xb[:, c * 128:(c + 1) * 128],
                                C["ident_bf"][:])) for c in range(8)],
                            reads=(xn_, "s_ident_bf"), writes=(ptn,))

                    @blk.vector
                    def _(v):
                        p.op(v, "vt", lambda pt=pt, ti=ti: v.tensor_copy(
                            out=xlTg[:].rearrange("p (c x) -> p c x", c=8)[
                                :, :, ti * 128:(ti + 1) * 128],
                            in_=pt[:].rearrange("p (c x) -> p c x", c=8)),
                            reads=(ptn,), writes=("xlTg",))

                for wave in range(4):
                    @blk.tensor
                    def _(t):
                        for j in range(2):
                            oc = wave * 2 + j
                            bank = psE if j == 0 else psF
                            bnm = "psE" if j == 0 else "psF"
                            p.mm(t, [
                                (lambda bank=bank, oc=oc, cdi=cdi: t.matmul(
                                    bank[:],
                                    wa[0][:, cdi * D + oc * 128:
                                          cdi * D + (oc + 1) * 128],
                                    xlTg[:, cdi * 512:(cdi + 1) * 512],
                                    start=(cdi == 0), stop=(cdi == 7)))
                                for cdi in range(8)],
                                reads=("wa0", "xlTg"), writes=(bnm,))

                    @blk.vector
                    def _(v):
                        for j in range(2):
                            oc = wave * 2 + j
                            bank = psE if j == 0 else psF
                            bnm = "psE" if j == 0 else "psF"
                            p.op(v, "vt", lambda bank=bank, oc=oc: v.tensor_scalar_add(
                                out=Qg[:, (oc % 2) * 512:(oc % 2 + 1) * 512],
                                in0=bank[:], scalar1=bqc[:, oc:oc + 1]),
                                reads=(bnm, "bqc"), writes=("Qg",))

                    @blk.scalar
                    def _(s):
                        for j in range(2):
                            oc = wave * 2 + j
                            p.op(s, "at", lambda oc=oc: s.activation(
                                out=ksq[:], in_=Qg[:, (oc % 2) * 512:
                                                  (oc % 2 + 1) * 512],
                                func=AF.Square), reads=("Qg",), writes=("ksq",))

                            @blk.tensor
                            def _(t):
                                p.mm(t, [lambda oc=oc: t.matmul(
                                    psB[0:1, 0:512], onesc_bf[:], ksq[:],
                                    start=(oc == 0), stop=(oc == 7))],
                                    reads=("onesc_bf", "ksq"), writes=("psB",))

                    @blk.sync
                    def _(sync):
                        for j in range(2):
                            oc = wave * 2 + j
                            p.dma(sync, "sync", "st",
                                  qt_h[oc * 128:(oc + 1) * 128,
                                       grp * 512:(grp + 1) * 512],
                                  Qg[:, (oc % 2) * 512:(oc % 2 + 1) * 512],
                                  reads=("Qg",))

                @blk.vector
                def _(v):
                    p.op(v, "vt", lambda grp=grp: v.tensor_copy(
                        out=q2row[0:1, grp * 512:(grp + 1) * 512],
                        in_=psB[0:1, 0:512]), reads=("psB",), writes=("q2row",))

            for kb in range(NKB):
                for c in range(8):
                    @blk.scalar
                    def _(s):
                        p.op(s, "at", lambda c=c, kb=kb: s.activation(
                            out=ksq[:], in_=Kt[:, c * S + kb * 512:
                                              c * S + (kb + 1) * 512],
                            func=AF.Square), reads=("Kt",), writes=("ksq",))

                    @blk.tensor
                    def _(t):
                        p.mm(t, [lambda c=c: t.matmul(
                            psB[0:1, 0:512], onesc_bf[:], ksq[:],
                            start=(c == 0), stop=(c == 7))],
                            reads=("onesc_bf", "ksq"), writes=("psB",))

                @blk.vector
                def _(v):
                    p.op(v, "vt", lambda kb=kb: v.tensor_scalar_mul(
                        out=nk2h[0:1, kb * 512:(kb + 1) * 512],
                        in_=psB[0:1, 0:512], scalar1=-0.5)
                        if False else v.tensor_scalar_mul(
                        out=nk2h[0:1, kb * 512:(kb + 1) * 512],
                        in0=psB[0:1, 0:512], scalar1=-0.5),
                        reads=("psB",), writes=("nk2h",))

            @blk.gpsimd
            def _(g):
                p.dma(g, "gp", "gd", wa[0][:].rearrange("p (c d) -> p c d", c=8),
                      Wo_h.rearrange("(c p) d -> p c d", p=128), writes=("wa0",))

            for qb in range(NQB):
                gtok = toff + qb * 128

                @blk.sync
                def _(sync):
                    p.dma(sync, "sync", "x0",
                          Qg[:].rearrange("p (c x) -> p c x", c=8),
                          qt_h.rearrange("(c p) x -> p c x", p=128)[
                              :, :, qb * 128:(qb + 1) * 128],
                          writes=("Qg",))

                @blk.tensor
                def _(t):
                    p.mm(t, [lambda qb=qb: t.matmul(
                        psB[:, 128:129], q2row[0:1, qb * 128:(qb + 1) * 128],
                        onesc2[:], start=True, stop=True)],
                        reads=("q2row", "onesc2"), writes=("psB",))

                @blk.vector
                def _(v):
                    p.op(v, "vt", lambda: v.tensor_scalar_mul(
                        out=q2c[:, 0:1], in0=psB[:, 128:129], scalar1=-1.0),
                        reads=("psB",), writes=("q2c",))
                    p.op(v, "vt", lambda: v.memset(dcol[:], 0.0), writes=("dcol",))

                def emit_qk(kb):
                    par_b = kb % 2
                    qkb = psA if par_b == 0 else psC
                    qkn = "psA" if par_b == 0 else "psC"
                    vb_t, vb_n = (psB, "psB") if par_b == 0 else (psD, "psD")
                    Pb, Pn = Psb[par_b], f"PsbB{par_b}"

                    @blk.tensor
                    def _(t):
                        emits = [
                            (lambda qkb=qkb, kb=kb, cdi=cdi: t.matmul(
                                qkb[:],
                                Qg[:, cdi * 128:(cdi + 1) * 128],
                                Kt[:, cdi * S + kb * 512:cdi * S + (kb + 1) * 512],
                                start=(cdi == 0), stop=False))
                            for cdi in range(8)]
                        emits.append(lambda qkb=qkb, kb=kb: t.matmul(
                            qkb[:], ones_bf[:],
                            nk2h[0:1, kb * 512:(kb + 1) * 512],
                            start=False, stop=True))
                        p.mm(t, emits, reads=("Qg", "Kt", "ones_bf", "nk2h"),
                             writes=(qkn,))

                    @blk.scalar
                    def _(s):
                        p.op(s, "at", lambda qkb=qkb, vb_t=vb_t: s.activation(
                            out=vb_t[:], in_=qkb[:], func=AF.Exp,
                            scale=c_two[:], bias=q2c[:, 0:1]),
                            reads=(qkn, "c_two", "q2c"), writes=(vb_n,))
                        p.op(s, "at", lambda kb=kb, vb_t=vb_t, Pb=Pb: s.activation(
                            out=Pb[:], in_=vb_t[:], func=AF.Exp,
                            bias=c_m1[:],
                            accum_out=dcol[:, kb:kb + 1]),
                            reads=(vb_n, "c_m1"), writes=(Pn, "dcol"))

                def emit_ptpv(kb):
                    par_b = kb % 2
                    Pb, Pn = Psb[par_b], f"PsbB{par_b}"
                    PTb, PTn = PTs[par_b], f"PTs{par_b}"
                    ptb_t, ptb_n = (ptA, "ptA") if par_b == 0 else (ptB, "ptB")

                    @blk.tensor
                    def _(t):
                        p.mm(t, [
                            (lambda j=j, Pb=Pb, ptb_t=ptb_t: t.transpose(
                                ptb_t[:, j * 128:(j + 1) * 128],
                                Pb[:, j * 128:(j + 1) * 128],
                                C["ident_bf"][:])) for j in range(4)],
                            reads=(Pn, "s_ident_bf"), writes=(ptb_n,))

                    @blk.vector
                    def _(v):
                        p.op(v, "vt", lambda PTb=PTb, ptb_t=ptb_t: v.tensor_copy(
                            out=PTb[:], in_=ptb_t[:, 0:512]),
                            reads=(ptb_n,), writes=(PTn,))

                    @blk.tensor
                    def _(t):
                        for nb in range(2):
                            bank = psE if nb == 0 else psF
                            bnm = "psE" if nb == 0 else "psF"
                            p.mm(t, [
                                (lambda bank=bank, nb=nb, kb=kb, j=j, PTb=PTb:
                                 t.matmul(
                                    bank[:],
                                    PTb[:, j * 128:(j + 1) * 128],
                                    Vt[:, (kb * 4 + j) * D + nb * 512:
                                       (kb * 4 + j) * D + (nb + 1) * 512],
                                    start=(kb == 0 and j == 0),
                                    stop=(kb == NKB - 1 and j == 3)))
                                for j in range(4)],
                                reads=(PTn, "Vt"), writes=(bnm,))

                for kb in range(NKB + 1):
                    if kb < NKB:
                        emit_qk(kb)
                    if kb > 0:
                        emit_ptpv(kb - 1)

                @blk.vector
                def _(v):
                    p.op(v, "vt", lambda: v.reduce_sum(
                        out=rcpc[:], in_=dcol[:], axis=AX.X),
                        reads=("dcol",), writes=("rcpc",))
                    p.op(v, "vt", lambda: v.reciprocal(out=rcpc[:], in_=rcpc[:]),
                         reads=("rcpc",), writes=("rcpc",))
                    for nb, bnm in enumerate(["psE", "psF"]):
                        bank = psE if nb == 0 else psF
                        p.op(v, "vt", lambda bank=bank, nb=nb: v.tensor_scalar_mul(
                            out=attn[:, nb * 512:(nb + 1) * 512], in0=bank[:],
                            scalar1=rcpc[:]), reads=(bnm, "rcpc"), writes=("attn4",))

                @blk.tensor
                def _(t):
                    p.mm(t, [
                        (lambda c=c: t.transpose(
                            ptB[:, c * 128:(c + 1) * 128],
                            attn[:, c * 128:(c + 1) * 128],
                            C["ident_bf"][:])) for c in range(8)],
                        reads=("attn4", "s_ident_bf"), writes=("ptB",))

                @blk.vector
                def _(v):
                    p.op(v, "vt", lambda: v.tensor_copy(
                        out=attnT[:], in_=ptB[:]), reads=("ptB",), writes=("attnT4",))

                @blk.sync
                def _(sync):
                    p.dma(sync, "sync", "x1s", x3t[:], x2_h[gtok:gtok + 128, :],
                          writes=("x3t",))

                @blk.tensor
                def _(t):
                    for nb in range(2):
                        bank = psA if nb == 0 else psC
                        bnm = "psA" if nb == 0 else "psC"
                        emits = [(lambda bank=bank, nb=nb: t.matmul(
                            bank[:], ones_bf[:],
                            brow[0:1, D + nb * 512: D + (nb + 1) * 512],
                            start=True, stop=False))]
                        emits += [
                            (lambda bank=bank, nb=nb, cdi=cdi: t.matmul(
                                bank[:],
                                attnT[:, cdi * 128:(cdi + 1) * 128],
                                wa[0][:, cdi * D + nb * 512:
                                      cdi * D + (nb + 1) * 512],
                                start=False, stop=(cdi == 7)))
                            for cdi in range(8)]
                        p.mm(t, emits, reads=("ones_bf", "brow", "attnT4", "wa0"),
                             writes=(bnm,))

                @blk.vector
                def _(v):
                    for nb, bnm in enumerate(["psA", "psC"]):
                        bank = psA if nb == 0 else psC
                        p.op(v, "vt", lambda bank=bank, nb=nb: v.tensor_add(
                            out=x3t[:, nb * 512:(nb + 1) * 512],
                            in0=x3t[:, nb * 512:(nb + 1) * 512], in1=bank[:]),
                            reads=("x3t", bnm), writes=("x3t",))

                @blk.sync
                def _(sync):
                    p.dma(sync, "sync", "st", x3_h[gtok:gtok + 128, :], x3t[:],
                          reads=("x3t",))

        @blk.gpsimd
        def _(g):
            for s in ["st", "sd", "gd", "id", "x0", "x1s"]:
                g.wait_ge(p.sem[s], p.n[s])
        _p_sbuf_pop(p, "Kt")
    return p


def launch4a_inputs(core, x1, wg, y_full, full):
    d = {"x1": x1, "wg": wg, "y_all": y_full,
         "Wq": full["L1_attn_Wq"], "Wk": full["L1_attn_Wk"],
         "Wv": full["L1_attn_Wv"], "Wo": full["L1_attn_Wo"],
         "bq": full["L1_attn_bq"].reshape(1, D),
         "bk": full["L1_attn_bk"].reshape(1, D),
         "bv": full["L1_attn_bv"].reshape(1, D),
         "bo": full["L1_attn_bo"].reshape(1, D)}
    d.update(const_inputs(L4A_CONSTS))
    return d



L4B_CONSTS = ["ident_bf"]
NIC1 = I1 // 128
FFN_NB = [(0, 512), (512, 512), (1024, 512), (1536, 512), (2048, 512), (2560, 256)]


QCW = 1024


def _emit_quant(p, blk, q, src, src_n, dst, dst_n, W):
    NB = W // 16
    n = q["_n"]
    xn, ax, g5, g7 = q["xn"], q["ax"], q["g5"], q["g7"]
    bm, hs, rcp = q["bm"], q["hs"], q["rcp"]
    i32 = q["i32"]
    i32_n = q["_i32n"]

    @blk.vector
    def _(v):
        p.op(v, "vt", lambda: v.tensor_reduce(
            out=bm[:, 0:NB], in_=src.rearrange("p (b x) -> p b x", x=16),
            axis=AX.X, op=ALU.max, apply_absolute_value=True),
            reads=(src_n,), writes=(n + "bm",))
        p.op(v, "vt", lambda: v.tensor_scalar(
            out=hs[:, 0:NB], in0=bm[:, 0:NB], scalar1=1.0 / 12.0,
            scalar2=5e-13, op0=ALU.mult, op1=ALU.max),
            reads=(n + "bm",), writes=(n + "hs",))
        p.op(v, "vt", lambda: v.reciprocal(
            out=rcp[:, 0:NB], in_=hs[:, 0:NB]),
            reads=(n + "hs",), writes=(n + "rcp",))
        p.op(v, "vt", lambda: v.tensor_tensor(
            out=xn[:, 0:W], in0=src,
            in1=rcp[:, 0:NB].unsqueeze(2).to_broadcast([128, NB, 16]),
            op=ALU.mult), reads=(src_n, n + "rcp"), writes=(n + "xn",))
        p.op(v, "vt", lambda: v.scalar_tensor_tensor(
            out=ax[:, 0:W], in0=xn[:, 0:W], scalar=-1.0, in1=xn[:, 0:W],
            op0=ALU.mult, op1=ALU.max), reads=(n + "xn",), writes=(n + "ax",))
        p.op(v, "vt", lambda: v.tensor_single_scalar(
            out=g5[:, 0:W], in_=ax[:, 0:W], scalar=5.0, op=ALU.is_gt),
            reads=(n + "ax",), writes=(n + "g5",))
        p.op(v, "vt", lambda: v.tensor_single_scalar(
            out=g7[:, 0:W], in_=ax[:, 0:W], scalar=7.0, op=ALU.is_gt),
            reads=(n + "ax",), writes=(n + "g7",))
        p.op(v, "vt", lambda: v.scalar_tensor_tensor(
            out=g5[:, 0:W], in0=g7[:, 0:W], scalar=1.0, in1=g5[:, 0:W],
            op0=ALU.mult, op1=ALU.add),
            reads=(n + "g5", n + "g7"), writes=(n + "g5",))
        p.op(v, "vt", lambda: v.tensor_single_scalar(
            out=g7[:, 0:W], in_=ax[:, 0:W], scalar=10.0, op=ALU.is_gt),
            reads=(n + "ax",), writes=(n + "g7",))
        p.op(v, "vt", lambda: v.scalar_tensor_tensor(
            out=g5[:, 0:W], in0=g7[:, 0:W], scalar=2.0, in1=g5[:, 0:W],
            op0=ALU.mult, op1=ALU.add),
            reads=(n + "g5", n + "g7"), writes=(n + "g5",))
        p.op(v, "vt", lambda: v.tensor_scalar(
            out=ax[:, 0:W], in0=ax[:, 0:W], scalar1=1.0, scalar2=4.49,
            op0=ALU.mult, op1=ALU.min), reads=(n + "ax",), writes=(n + "ax",))
        p.op(v, "vt", lambda: v.tensor_copy(
            out=i32[:, 0:W], in_=ax[:, 0:W]), reads=(n + "ax",), writes=(i32_n,))
        p.op(v, "vt", lambda: v.tensor_copy(
            out=ax[:, 0:W], in_=i32[:, 0:W]), reads=(i32_n,), writes=(n + "ax",))
        p.op(v, "vt", lambda: v.scalar_tensor_tensor(
            out=ax[:, 0:W], in0=g5[:, 0:W], scalar=2.0, in1=ax[:, 0:W],
            op0=ALU.mult, op1=ALU.add),
            reads=(n + "g5", n + "ax"), writes=(n + "ax",))
        p.op(v, "vt", lambda: v.tensor_single_scalar(
            out=g5[:, 0:W], in_=xn[:, 0:W], scalar=0.0, op=ALU.is_gt),
            reads=(n + "xn",), writes=(n + "g5",))
        p.op(v, "vt", lambda: v.tensor_scalar(
            out=g5[:, 0:W], in0=g5[:, 0:W], scalar1=2.0, scalar2=-1.0,
            op0=ALU.mult, op1=ALU.add), reads=(n + "g5",), writes=(n + "g5",))
        p.op(v, "vt", lambda: v.tensor_tensor(
            out=g5[:, 0:W], in0=g5[:, 0:W],
            in1=hs[:, 0:NB].unsqueeze(2).to_broadcast([128, NB, 16]),
            op=ALU.mult), reads=(n + "g5", n + "hs"), writes=(n + "g5",))
        p.op(v, "vt", lambda: v.scalar_tensor_tensor(
            out=dst, in0=ax[:, 0:W], scalar=1.0, in1=g5[:, 0:W],
            op0=ALU.mult, op1=ALU.mult),
            reads=(n + "ax", n + "g5"), writes=(dst_n,))


def _emit_quant_chunks(p, blk, qs, src_t, src_n, dst_t, dst_n, W):
    off = 0
    while off < W:
        cw = min(QCW, W - off)
        q = qs[qs[-1] % 2]
        qs[-1] += 1
        _emit_quant(p, blk, q, src_t[:, off:off + cw], src_n,
                    dst_t[:, off:off + cw], dst_n, cw)
        off += cw


def _push_quant_scratch(p):
    i32 = _p_sbuf_push(p, "qi32", [128, QCW], mybir.dt.int32)
    sets = []
    for pref in ("q0_", "q1_"):
        q = {"_n": pref, "i32": i32, "_i32n": "qi32"}
        for nm in ["xn", "ax", "g5", "g7"]:
            q[nm] = _p_sbuf_push(p, pref + nm, [128, QCW])
        for nm in ["bm", "hs", "rcp"]:
            q[nm] = _p_sbuf_push(p, pref + nm, [128, QCW // 16])
        sets.append(q)
    return sets + [0]


def build_launch4b():
    p = P("l4b")
    nc = p.nc
    x3_h = nc.dram_tensor("x3", [TPC, D], F32, kind="ExternalInput").ap()
    fg_h = nc.dram_tensor("f_gate", [I1, D], F32, kind="ExternalInput").ap()
    fu_h = nc.dram_tensor("f_up", [I1, D], F32, kind="ExternalInput").ap()
    fd_h = nc.dram_tensor("f_down", [D, I1], F32, kind="ExternalInput").ap()
    out_h = nc.dram_tensor("out", [TPC, D], F32, kind="ExternalOutput").ap()
    hq_h = nc.dram_tensor("hqsp", [TPC, I1], BF16).ap()

    hbs, C = _declare_consts(p, L4B_CONSTS)
    with p.ctx:
        for s in ["sd", "gd", "x0", "x1s", "pt", "vt", "at", "st", "gt"]:
            p.semaphore(s)
        p.DMA_SEMS = frozenset(["sd", "gd", "st", "x0", "x1s"])
        c_eps = p.sbuf("c_eps", [128, 1])
        c_invD = p.sbuf("c_invD", [128, 1])
        c_mhalf = p.sbuf("c_mhalf", [128, 1])
        nmean = p.sbuf("nmean", [128, 2])
        rstd = p.sbuf("rstd", [128, 2])

        ptA = p.psum("ptA", [128, 1024], BF16)
        ptB = p.psum("ptB", [128, 1024], BF16)
        psA = p.psum("psA", [128, 512])
        psB = p.psum("psB", [128, 512])
        psC = p.psum("psC", [128, 512])
        psD = p.psum("psD", [128, 512])

        blk = p.ctx.enter_context(nc.Block())
        _load_consts(p, blk, hbs, C)

        @blk.vector
        def _(v):
            p.op(v, "vt", lambda: v.memset(c_eps[:], LN_EPS), writes=("c_eps",))
            p.op(v, "vt", lambda: v.memset(c_invD[:], 1.0 / D), writes=("c_invD",))
            p.op(v, "vt", lambda: v.memset(c_mhalf[:], -0.5), writes=("c_mhalf",))

        qs = _push_quant_scratch(p)
        wgT = _p_sbuf_push(p, "wgT", [128, 8 * I1], BF16)
        wuT = _p_sbuf_push(p, "wuT", [128, 8 * I1], BF16)
        wrow = [_p_sbuf_push(p, f"wrow{i}", [128, I1]) for i in range(2)]
        wqb2p = [_p_sbuf_push(p, f"wqb{i}", [128, I1], BF16) for i in range(2)]

        def prep_gate_up(w_h, dstT, dst_n):
            for r in range(NIC1):
                par = r % 2
                wb, wn = wrow[par], f"wrow{par}"
                xsem = "x0" if par == 0 else "x1s"

                @blk.sync
                def _(sync):
                    p.dma(sync, "sync", xsem, wb[:, 0:D],
                          w_h[r * 128:(r + 1) * 128, :], writes=(wn,))

                wqb, wqn = wqb2p[par], f"wqb{par}"
                _emit_quant_chunks(p, blk, qs, wb[:, 0:D], wn, wqb[:, 0:D], wqn, D)

                @blk.tensor
                def _(t):
                    p.mm(t, [
                        (lambda c=c, wqb=wqb: t.transpose(
                            ptA[:, c * 128:(c + 1) * 128],
                            wqb[:, c * 128:(c + 1) * 128],
                            C["ident_bf"][:])) for c in range(8)],
                        reads=(wqn, "s_ident_bf"), writes=("ptA",))

                @blk.vector
                def _(v):
                    p.op(v, "vt", lambda r=r: v.tensor_copy(
                        out=dstT[:].rearrange("p (c x) -> p c x", c=8)[
                            :, :, r * 128:(r + 1) * 128],
                        in_=ptA[:].rearrange("p (c x) -> p c x", c=8)),
                        reads=("ptA",), writes=(dst_n,))

        prep_gate_up(fg_h, wgT, "wgT")
        prep_gate_up(fu_h, wuT, "wuT")

        _p_sbuf_pop(p, "wrow0")

        x3t = [_p_sbuf_push(p, f"x3t{i}", [128, D]) for i in range(2)]
        xln3p = [_p_sbuf_push(p, f"xln3{i}", [128, D]) for i in range(2)]
        stats = _p_sbuf_push(p, "stats", [128, 2 * NT])
        xcb = _p_sbuf_push(p, "xcb", [128, D])
        sqs = _p_sbuf_push(p, "sqs", [128, D])
        for tile in range(NT):
            gtok = tile * 128
            par = tile % 2
            xb, xn_ = x3t[par], f"x3t{par}"
            xsem = "x0" if par == 0 else "x1s"

            @blk.sync
            def _(sync):
                p.dma(sync, "sync", xsem, xb[:], x3_h[gtok:gtok + 128, :],
                      writes=(xn_,))

            @blk.vector
            def _(v):
                p.op(v, "vt", lambda xb=xb, tile=tile: v.reduce_sum(
                    out=stats[:, 2 * tile:2 * tile + 1], in_=xb[:], axis=AX.X,
                    negate=True), reads=(xn_,), writes=("stats",))
                p.op(v, "vt", lambda tile=tile: v.tensor_scalar_mul(
                    out=stats[:, 2 * tile:2 * tile + 1],
                    in0=stats[:, 2 * tile:2 * tile + 1], scalar1=1.0 / D),
                    reads=("stats",), writes=("stats",))
                p.op(v, "vt", lambda xb=xb, tile=tile: v.tensor_scalar_add(
                    out=xcb[:], in0=xb[:],
                    scalar1=stats[:, 2 * tile:2 * tile + 1]),
                    reads=(xn_, "stats"), writes=("xcb",))

            @blk.scalar
            def _(s):
                p.op(s, "at", lambda tile=tile: s.activation(
                    out=sqs[:], in_=xcb[:], func=AF.Square,
                    accum_out=stats[:, 2 * tile + 1:2 * tile + 2]),
                    reads=("xcb",), writes=("sqs", "stats"))
                p.op(s, "at", lambda tile=tile: s.activation(
                    out=stats[:, 2 * tile + 1:2 * tile + 2],
                    in_=stats[:, 2 * tile + 1:2 * tile + 2], func=AF.Ln,
                    scale=c_invD[:], bias=c_eps[:]),
                    reads=("stats", "c_invD", "c_eps"), writes=("stats",))
                p.op(s, "at", lambda tile=tile: s.activation(
                    out=stats[:, 2 * tile + 1:2 * tile + 2],
                    in_=stats[:, 2 * tile + 1:2 * tile + 2], func=AF.Exp,
                    scale=c_mhalf[:]),
                    reads=("stats", "c_mhalf"), writes=("stats",))

        _p_sbuf_pop(p, "xcb")

        xqp = [_p_sbuf_push(p, f"xq{i}", [128, D], BF16) for i in range(2)]
        xqTp = [_p_sbuf_push(p, f"xqT{i}", [128, 8 * 128], BF16) for i in range(2)]
        gsp = [_p_sbuf_push(p, f"gs{i}", [128, 512]) for i in range(2)]
        hbufp = [_p_sbuf_push(p, f"hbuf{i}", [128, I1]) for i in range(2)]
        hqp = [_p_sbuf_push(p, f"hq{i}", [128, I1], BF16) for i in range(2)]

        for tile in range(NT):
            gtok = tile * 128
            par = tile % 2
            xb, xn_ = x3t[par], f"x3t{par}"
            xsem = "x0" if par == 0 else "x1s"
            xln3 = xln3p[par]
            xq = xqp[par]
            xqT = xqTp[par]
            gs = gsp[par]
            hbuf = hbufp[par]
            hq = hqp[par]
            nm_xln3, nm_xq, nm_xqT = f"xln3{par}", f"xq{par}", f"xqT{par}"
            nm_gs, nm_hbuf, nm_hq = f"gs{par}", f"hbuf{par}", f"hq{par}"

            @blk.sync
            def _(sync):
                p.dma(sync, "sync", xsem, xb[:], x3_h[gtok:gtok + 128, :],
                      writes=(xn_,))

            @blk.vector
            def _(v):
                p.op(v, "vt", lambda xb=xb, tile=tile, xln3=xln3: v.tensor_scalar(
                    out=xln3[:], in0=xb[:],
                    scalar1=stats[:, 2 * tile:2 * tile + 1],
                    scalar2=stats[:, 2 * tile + 1:2 * tile + 2],
                    op0=ALU.add, op1=ALU.mult),
                    reads=(xn_, "stats"), writes=(nm_xln3,))

            _emit_quant_chunks(p, blk, qs, xln3[:], nm_xln3, xq[:], nm_xq, D)

            ptx = ptA if par == 0 else ptB
            ptxn = "ptA" if par == 0 else "ptB"

            @blk.tensor
            def _(t):
                p.mm(t, [
                    (lambda c=c, xq=xq, ptx=ptx: t.transpose(
                        ptx[:, c * 128:(c + 1) * 128],
                        xq[:, c * 128:(c + 1) * 128],
                        C["ident_bf"][:])) for c in range(8)],
                    reads=(nm_xq, "s_ident_bf"), writes=(ptxn,))

            @blk.vector
            def _(v):
                p.op(v, "vt", lambda xqT=xqT, ptx=ptx: v.tensor_copy(
                    out=xqT[:], in_=ptx[:]), reads=(ptxn,), writes=(nm_xqT,))

            for inb, (noff, nw) in enumerate(FFN_NB):
                bankG = psA if inb % 2 == 0 else psC
                bG = "psA" if inb % 2 == 0 else "psC"
                bankU = psB if inb % 2 == 0 else psD
                bU = "psB" if inb % 2 == 0 else "psD"

                @blk.tensor
                def _(t):
                    p.mm(t, [
                        (lambda bankG=bankG, noff=noff, nw=nw, c=c, xqT=xqT:
                         t.matmul(
                            bankG[:, 0:nw],
                            xqT[:, c * 128:(c + 1) * 128],
                            wgT[:, c * I1 + noff: c * I1 + noff + nw],
                            start=(c == 0), stop=(c == 7))) for c in range(8)],
                        reads=(nm_xqT, "wgT"), writes=(bG,))
                    p.mm(t, [
                        (lambda bankU=bankU, noff=noff, nw=nw, c=c, xqT=xqT:
                         t.matmul(
                            bankU[:, 0:nw],
                            xqT[:, c * 128:(c + 1) * 128],
                            wuT[:, c * I1 + noff: c * I1 + noff + nw],
                            start=(c == 0), stop=(c == 7))) for c in range(8)],
                        reads=(nm_xqT, "wuT"), writes=(bU,))

                @blk.scalar
                def _(s):
                    p.op(s, "at", lambda bankG=bankG, nw=nw, gs=gs: s.activation(
                        out=gs[:, 0:nw], in_=bankG[:, 0:nw], func=AF.Silu),
                        reads=(bG,), writes=(nm_gs,))

                @blk.vector
                def _(v):
                    p.op(v, "vt", lambda bankU=bankU, noff=noff, nw=nw, gs=gs,
                         hbuf=hbuf: v.tensor_tensor(
                             out=hbuf[:, noff:noff + nw], in0=gs[:, 0:nw],
                             in1=bankU[:, 0:nw], op=ALU.mult),
                         reads=(nm_gs, bU), writes=(nm_hbuf,))

            _emit_quant_chunks(p, blk, qs, hbuf[:], nm_hbuf, hq[:], nm_hq, I1)

            @blk.sync
            def _(sync):
                p.dma(sync, "sync", "st", hq_h[gtok:gtok + 128, :], hq[:],
                      reads=(nm_hq,))

        _p_sbuf_pop(p, "x3t0")
        _p_sbuf_pop(p, "wgT")

        wdT2 = _p_sbuf_push(p, "wdT2", [128, NIC1 * D], BF16)
        hq2 = [_p_sbuf_push(p, f"hq2{i}", [128, I1], BF16) for i in range(2)]
        hqT = _p_sbuf_push(p, "hqT", [128, NIC1 * 128], BF16)
        x3r = _p_sbuf_push(p, "x3r", [128, D])
        outt = _p_sbuf_push(p, "outt", [128, D])

        wrow2 = _p_sbuf_push(p, "wrow2", [128, I1])
        wqb2 = _p_sbuf_push(p, "wqb2", [128, I1], BF16)
        for r in range(8):
            @blk.sync
            def _(sync):
                p.dma(sync, "sync", "x0", wrow2[:],
                      fd_h[r * 128:(r + 1) * 128, :], writes=("wrow2",))

            _emit_quant_chunks(p, blk, qs, wrow2[:], "wrow2", wqb2[:], "wqb2", I1)

            for bt in range(3):
                nt_ = min(8, NIC1 - bt * 8)

                @blk.tensor
                def _(t):
                    p.mm(t, [
                        (lambda bt=bt, j=j: t.transpose(
                            ptA[:, j * 128:(j + 1) * 128],
                            wqb2[:, (bt * 8 + j) * 128:(bt * 8 + j + 1) * 128],
                            C["ident_bf"][:])) for j in range(nt_)],
                        reads=("wqb2", "s_ident_bf"), writes=("ptA",))

                @blk.vector
                def _(v):
                    p.op(v, "vt", lambda r=r, bt=bt, nt_=nt_: v.tensor_copy(
                        out=wdT2[:].rearrange("p (c x) -> p c x", c=NIC1)[
                            :, bt * 8: bt * 8 + nt_, r * 128:(r + 1) * 128],
                        in_=ptA[:, 0:nt_ * 128].rearrange(
                            "p (c x) -> p c x", x=128)),
                        reads=("ptA",), writes=("wdT2",))

        for tile in range(NT):
            gtok = tile * 128
            par = tile % 2
            hb, hn = hq2[par], f"hq2{par}"
            xsem = "x0" if par == 0 else "x1s"

            @blk.sync
            def _(sync):
                p.dma(sync, "sync", xsem, hb[:], hq_h[gtok:gtok + 128, :],
                      writes=(hn,))
                p.dma(sync, "sync", xsem, x3r[:], x3_h[gtok:gtok + 128, :],
                      writes=("x3r",))

            for bt in range(3):
                nt_ = min(8, NIC1 - bt * 8)
                pt = ptA if bt % 2 == 0 else ptB
                ptn = "ptA" if bt % 2 == 0 else "ptB"

                @blk.tensor
                def _(t):
                    p.mm(t, [
                        (lambda pt=pt, bt=bt, j=j, hb=hb: t.transpose(
                            pt[:, j * 128:(j + 1) * 128],
                            hb[:, (bt * 8 + j) * 128:(bt * 8 + j + 1) * 128],
                            C["ident_bf"][:])) for j in range(nt_)],
                        reads=(hn, "s_ident_bf"), writes=(ptn,))

                @blk.vector
                def _(v):
                    p.op(v, "vt", lambda pt=pt, bt=bt, nt_=nt_: v.tensor_copy(
                        out=hqT[:, bt * 8 * 128: (bt * 8 + nt_) * 128],
                        in_=pt[:, 0:nt_ * 128]), reads=(ptn,), writes=("hqT",))

            @blk.tensor
            def _(t):
                for nb in range(2):
                    bank = psA if nb == 0 else psB
                    bnm = "psA" if nb == 0 else "psB"
                    p.mm(t, [
                        (lambda bank=bank, nb=nb, ci=ci: t.matmul(
                            bank[:],
                            hqT[:, ci * 128:(ci + 1) * 128],
                            wdT2[:, ci * D + nb * 512: ci * D + (nb + 1) * 512],
                            start=(ci == 0), stop=(ci == NIC1 - 1)))
                        for ci in range(NIC1)],
                        reads=("hqT", "wdT2"), writes=(bnm,))

            @blk.vector
            def _(v):
                for nb, bnm in enumerate(["psA", "psB"]):
                    bank = psA if nb == 0 else psB
                    p.op(v, "vt", lambda bank=bank, nb=nb: v.tensor_add(
                        out=outt[:, nb * 512:(nb + 1) * 512],
                        in0=x3r[:, nb * 512:(nb + 1) * 512], in1=bank[:]),
                        reads=("x3r", bnm), writes=("outt",))

            @blk.sync
            def _(sync):
                p.dma(sync, "sync", "st", out_h[gtok:gtok + 128, :], outt[:],
                      reads=("outt",))

        @blk.gpsimd
        def _(g):
            for s in ["st", "sd", "gd", "x0", "x1s"]:
                g.wait_ge(p.sem[s], p.n[s])
        _p_sbuf_pop(p, "wdT2")
        _p_sbuf_pop(p, "qi32")
    return p


def launch4b_inputs(core, x3, full):
    d = {"x3": x3, "f_gate": full["L1_ffn_w_gate"], "f_up": full["L1_ffn_w_up"],
         "f_down": full["L1_ffn_w_down"]}
    d.update(const_inputs(L4B_CONSTS))
    return d
